# version 08
# baseline (speedup 1.0000x reference)
import contextlib
import numpy as np
import ml_dtypes
import concourse.bass as bass
import concourse.mybir as mybir
from concourse.bass_utils import run_bass_kernel_spmd

F32 = mybir.dt.float32
BF16 = mybir.dt.bfloat16
AF = mybir.ActivationFunctionType
ALU = mybir.AluOpType
bf16 = ml_dtypes.bfloat16

B = 2
T = 8192
DM = 2048
D = 128
NB = T // 512
NC = DM // 128
SCALE = 1.0 / float(np.sqrt(D))
RG = [[0, 1, 2, 3], [4, 5, 6, 7]]

ROWS = [(0, 0), (1, 0)] + [(qb, h) for qb in range(NB) for h in range(4)]
N_SACR = 2
ROW_GROUPS = []
_g = 0
for (qb, _h) in ROWS:
    ROW_GROUPS.append(list(range(_g, _g + 2 * (qb + 1))))
    _g += 2 * (qb + 1)
NGRP = _g
XP_OF = {s: s + 1 for s in range(NGRP)}
EV_EVAC_DONE = {}
EV_TS_DONE = {}
EV_DUMMY_DONE = {}
_ev = 0
for _r in range(len(ROWS)):
    if _r >= 1:
        _ev += 4
        EV_EVAC_DONE[_r - 1] = _ev
    _ev += 4
    EV_TS_DONE[_r] = _ev
    _ev += 1
    EV_DUMMY_DONE[_r] = _ev
_ev += 4
EV_EVAC_DONE[len(ROWS) - 1] = _ev
EV_TOTAL = _ev
LOC_OF = {}
ROW_OF = {}
for _ri in range(len(ROWS)):
    for _j, _s in enumerate(ROW_GROUPS[_ri]):
        LOC_OF[_s] = _j
        ROW_OF[_s] = _ri


def build_nc():
    nc = bass.Bass("TRN2", target_bir_lowering=False)

    xT = nc.dram_tensor("xT", [DM, T], BF16, kind="ExternalInput").ap()
    xq = nc.dram_tensor("xq", [512, T], BF16, kind="ExternalInput").ap()
    wq = nc.dram_tensor("wq", [DM, 512], BF16, kind="ExternalInput").ap()
    wk = nc.dram_tensor("wk", [DM, 128], BF16, kind="ExternalInput").ap()
    wv = nc.dram_tensor("wv", [DM, 128], BF16, kind="ExternalInput").ap()
    wo_q = nc.dram_tensor("wo_q", [DM, 512], BF16, kind="ExternalInput").ap()
    wf_q = nc.dram_tensor("wf_q", [DM, 512], F32, kind="ExternalInput").ap()
    bq4 = nc.dram_tensor("bq4", [128, 4], BF16, kind="ExternalInput").ap()
    bk1 = nc.dram_tensor("bk1", [128, 1], BF16, kind="ExternalInput").ap()
    bv1 = nc.dram_tensor("bv1", [128, 1], BF16, kind="ExternalInput").ap()
    fscI = nc.dram_tensor("fscI", [128, 4], F32, kind="ExternalInput").ap()
    fbiI = nc.dram_tensor("fbiI", [128, 4], F32, kind="ExternalInput").ap()
    cosI = nc.dram_tensor("cosI", [128, T], BF16, kind="ExternalInput").ap()
    sinI = nc.dram_tensor("sinI", [128, T], BF16, kind="ExternalInput").ap()
    maskI = nc.dram_tensor("maskI", [128, 1024], BF16, kind="ExternalInput").ap()
    idenI = nc.dram_tensor("idenI", [128, 128], BF16, kind="ExternalInput").ap()
    rotI = nc.dram_tensor("rotI", [128, 128], BF16, kind="ExternalInput").ap()
    onecI = nc.dram_tensor("onecI", [128, 2], BF16, kind="ExternalInput").ap()
    onerI = nc.dram_tensor("onerI", [1, 128], BF16, kind="ExternalInput").ap()
    onerFI = nc.dram_tensor("onerFI", [1, 128], F32, kind="ExternalInput").ap()

    out_T = nc.dram_tensor("out_T", [512, T], F32, kind="ExternalOutput").ap()

    O_scr = nc.dram_tensor("O_scr", [512, T], BF16, kind="Internal")
    OG = nc.dram_tensor("OG", [DM, T], BF16, kind="Internal")
    xn_scr = nc.dram_tensor("xn_scr", [DM, T], BF16, kind="Internal").ap()

    st = contextlib.ExitStack()
    E = st.enter_context

    iden = E(nc.sbuf_tensor("iden", [128, 128], BF16))
    rotM = E(nc.sbuf_tensor("rotM", [128, 128], BF16))
    maska = E(nc.sbuf_tensor("maska", [128, 1024], BF16))
    onec = E(nc.sbuf_tensor("onec", [128, 2], BF16))
    oner = E(nc.sbuf_tensor("oner", [1, 128], BF16))
    onerF = E(nc.sbuf_tensor("onerF", [1, 128], F32))
    bqb = E(nc.sbuf_tensor("bqb", [128, 4], BF16))
    bkb = E(nc.sbuf_tensor("bkb", [128, 1], BF16))
    bvb = E(nc.sbuf_tensor("bvb", [128, 1], BF16))
    bq_s = E(nc.sbuf_tensor("bq_s", [128, 4], F32))
    bk_s = E(nc.sbuf_tensor("bk_s", [128, 1], F32))
    bv_s = E(nc.sbuf_tensor("bv_s", [128, 1], F32))
    fsc = E(nc.sbuf_tensor("fsc", [128, 4], F32))
    fbi = E(nc.sbuf_tensor("fbi", [128, 4], F32))

    names = ["s_ld", "s_ldt", "s_ldc", "s_sqf", "s_sqb", "s_sqmm", "s_stsum",
             "s_rw1", "s_rw2", "s_row", "s_bcmm", "s_bc", "s_nrm", "s_prj", "s_evA",
             "s_rotmm", "s_rp", "s_vtb", "s_vte", "s_sp",
             "s_mm", "s_xp", "s_pv", "s_ev", "s_tr", "s_sto", "s_cc",
             "s_cw", "s_wsp", "s_cg", "s_cmm", "s_cev", "s_sil", "s_cdone", "s_out"]
    S = {n: E(nc.semaphore(n)) for n in names}

    with st:
        sAB = contextlib.ExitStack()
        q_s = sAB.enter_context(nc.sbuf_tensor("q_s", [128, 4 * T], BF16))
        k_s = sAB.enter_context(nc.sbuf_tensor("k_s", [128, T], BF16))
        v_s = sAB.enter_context(nc.sbuf_tensor("v_s", [128, (T // 128) * 129], BF16))
        with contextlib.ExitStack() as sA:
            EA = sA.enter_context
            wq_s = EA(nc.sbuf_tensor("wq_s", [128, NC * 512], BF16))
            wk_s = EA(nc.sbuf_tensor("wk_s", [128, NC * 128], BF16))
            wv_s = EA(nc.sbuf_tensor("wv_s", [128, NC * 128], BF16))
            xT_s = EA(nc.sbuf_tensor("xT_s", [128, 2 * NC * 512], BF16))
            xsq = EA(nc.sbuf_tensor("xsq", [128, 4 * 512], BF16))
            xn_s = EA(nc.sbuf_tensor("xn_s", [128, NC * 512], BF16))
            cs_s = EA(nc.sbuf_tensor("cs_s", [128, 2 * 512], BF16))
            sn_s = EA(nc.sbuf_tensor("sn_s", [128, 2 * 512], BF16))
            rowm = EA(nc.sbuf_tensor("rowm", [1, 512], F32))
            rowv = EA(nc.sbuf_tensor("rowv", [1, 512], F32))
            rowt = EA(nc.sbuf_tensor("rowt", [1, 512], F32))
            rowr = EA(nc.sbuf_tensor("rowr", [1, 512], F32))
            rowbf = EA(nc.sbuf_tensor("rowbf", [1, 1024], BF16))
            Rb = EA(nc.sbuf_tensor("Rb", [128, 512], F32))
            Mb = EA(nc.sbuf_tensor("Mb", [128, 512], BF16))
            tmpn = EA(nc.sbuf_tensor("tmpn", [128, 512], BF16))
            qraw = EA(nc.sbuf_tensor("qraw", [128, 2 * 512], BF16))
            t1b = EA(nc.sbuf_tensor("t1b", [128, 512], BF16))
            t2b = EA(nc.sbuf_tensor("t2b", [128, 512], BF16))
            vE = EA(nc.sbuf_tensor("vE", [128, 512], BF16))
            ps_sum = EA(nc.psum_tensor("ps_sum", [128, 512], F32))
            ps_sq = EA(nc.psum_tensor("ps_sq", [128, 512], F32))
            ps_bc = EA(nc.psum_tensor("ps_bc", [128, 1024], F32))
            ps_pA = EA(nc.psum_tensor("ps_pA", [128, 512], F32))
            ps_pB = EA(nc.psum_tensor("ps_pB", [128, 512], F32))
            ps_rot = EA(nc.psum_tensor("ps_rot", [128, 512], F32))
            ps_vt = EA(nc.psum_tensor("ps_vt", [128, 4 * 128], BF16))

            with nc.Block("A") as blk:

                @blk.sync
                def _(sync):
                    for dst, src in [
                        (iden[:, :], idenI), (rotM[:, :], rotI), (maska[:, :], maskI),
                        (onec[:, :], onecI), (oner[:, :], onerI), (onerF[:, :], onerFI),
                        (bqb[:, :], bq4), (bkb[:, :], bk1), (bvb[:, :], bv1),
                        (fsc[:, :], fscI), (fbi[:, :], fbiI),
                    ]:
                        sync.dma_start(out=dst, in_=src).then_inc(S["s_ldc"], 16)
                    sync.dma_start(out=wq_s[:, :].rearrange("p (c m) -> p c m", c=NC),
                                   in_=wq.rearrange("(c p) m -> p c m", p=128)).then_inc(S["s_ldc"], 16)
                    sync.dma_start(out=wk_s[:, :].rearrange("p (c m) -> p c m", c=NC),
                                   in_=wk.rearrange("(c p) m -> p c m", p=128)).then_inc(S["s_ldc"], 16)
                    sync.dma_start(out=wv_s[:, :].rearrange("p (c m) -> p c m", c=NC),
                                   in_=wv.rearrange("(c p) m -> p c m", p=128)).then_inc(S["s_ldc"], 16)
                    for tb in range(NB):
                        buf = tb % 2
                        if tb >= 2:
                            sync.wait_ge(S["s_sqb"], tb - 1)
                            sync.wait_ge(S["s_stsum"], tb - 1)
                            sync.wait_ge(S["s_nrm"], NC * (tb - 1))
                        src_b = xT[:, tb * 512:(tb + 1) * 512].rearrange("(c p) t -> p c t", p=128)
                        dst_b = xT_s[:, buf * 8192:(buf + 1) * 8192].rearrange("p (c t) -> p c t", c=NC)
                        for pt in range(4):
                            sync.dma_start(out=dst_b[:, pt * 4:(pt + 1) * 4, :],
                                           in_=src_b[:, pt * 4:(pt + 1) * 4, :]).then_inc(S["s_ld"], 16)
                        if tb >= 2:
                            sync.wait_ge(S["s_rp"], 5 * (tb - 1))
                        sync.dma_start(out=cs_s[:, buf * 512:(buf + 1) * 512],
                                       in_=cosI[:, tb * 512:(tb + 1) * 512]).then_inc(S["s_ldt"], 16)
                        sync.dma_start(out=sn_s[:, buf * 512:(buf + 1) * 512],
                                       in_=sinI[:, tb * 512:(tb + 1) * 512]).then_inc(S["s_ldt"], 16)

                @blk.gpsimd
                def _(gpsimd):
                    for tb in range(NB):
                        gpsimd.wait_ge(S["s_nrm"], NC * (tb + 1))
                        dst = xn_scr[:, tb * 512:(tb + 1) * 512].rearrange("(c p) t -> p c t", p=128)
                        src = xn_s[:, :].rearrange("p (c t) -> p c t", c=NC)
                        gpsimd.dma_start(out=dst[:, 0:8, :], in_=src[:, 0:8, :]).then_inc(S["s_sp"], 16)
                        gpsimd.dma_start(out=dst[:, 8:16, :], in_=src[:, 8:16, :]).then_inc(S["s_sp"], 16)

                @blk.scalar
                def _(scalar):
                    scalar.wait_ge(S["s_ldc"], 16 * 14)
                    for tb in range(NB):
                        buf = tb % 2
                        xb = xT_s[:, buf * 8192:(buf + 1) * 8192]
                        scalar.wait_ge(S["s_ld"], 64 * (tb + 1))
                        for c in range(NC):
                            gi = tb * NC + c
                            if gi >= 4:
                                scalar.wait_ge(S["s_sqmm"], gi - 3)
                            scalar.activation(xsq[:, (gi % 4) * 512:(gi % 4 + 1) * 512],
                                              xb[:, c * 512:(c + 1) * 512],
                                              AF.Square).then_inc(S["s_sqf"], 1)
                        scalar.drain().then_inc(S["s_sqb"], 1)
                        scalar.wait_ge(S["s_rw1"], tb + 1)
                        scalar.activation(rowt[0:1, :], rowv[0:1, :], AF.Ln)
                        scalar.activation(rowt[0:1, :], rowt[0:1, :], AF.Exp, scale=0.5)
                        scalar.drain().then_inc(S["s_rw2"], 1)
                        evs = [("K", qraw[:, 0:512], bk_s[0:128, 0:1]),
                               ("V", vE[:, :], bv_s[0:128, 0:1]),
                               ("Q0", qraw[:, 512:1024], bq_s[0:128, 0:1]),
                               ("Q1", qraw[:, 0:512], bq_s[0:128, 1:2]),
                               ("Q2", qraw[:, 512:1024], bq_s[0:128, 2:3]),
                               ("Q3", qraw[:, 0:512], bq_s[0:128, 3:4])]
                        for gi, (nm, dst, bias_ap) in enumerate(evs):
                            scalar.wait_ge(S["s_prj"], 6 * tb + gi + 1)
                            if nm == "K" and tb >= 1:
                                scalar.wait_ge(S["s_rp"], 5 * tb)
                                scalar.wait_ge(S["s_rotmm"], 5 * tb)
                            if nm == "V" and tb >= 1:
                                scalar.wait_ge(S["s_vtb"], tb)
                            if nm in ("Q1", "Q2", "Q3"):
                                k_ = {"Q1": 1, "Q2": 2, "Q3": 3}[nm]
                                scalar.wait_ge(S["s_rp"], 5 * tb + k_)
                                scalar.wait_ge(S["s_rotmm"], 5 * tb + k_)
                            psum = ps_pA if gi % 2 == 0 else ps_pB
                            scalar.activation(dst, psum[:, :], AF.Identity, bias=bias_ap)
                            scalar.drain().then_inc(S["s_evA"], 1)

                @blk.vector
                def _(vector):
                    vector.wait_ge(S["s_ldc"], 16 * 14)
                    vector.memset(v_s[:, :].rearrange("p (n f) -> p n f", f=129)[:, :, 128:129], 1.0)
                    vector.tensor_copy(bq_s[:, :], bqb[:, :])
                    vector.tensor_copy(bk_s[:, :], bkb[:, :])
                    vector.tensor_copy(bv_s[:, :], bvb[:, :])
                    for tb in range(NB):
                        buf = tb % 2
                        xb = xT_s[:, buf * 8192:(buf + 1) * 8192]
                        csb = cs_s[:, buf * 512:(buf + 1) * 512]
                        snb = sn_s[:, buf * 512:(buf + 1) * 512]
                        vector.wait_ge(S["s_stsum"], tb + 1)
                        vector.wait_ge(S["s_sqmm"], NC * (tb + 1))
                        vector.tensor_scalar(rowm[0:1, :], ps_sum[0:1, :], 1.0 / DM, None, ALU.mult)
                        vector.tensor_copy(rowbf[0:1, 0:512], rowm[0:1, :])
                        vector.tensor_scalar(rowv[0:1, :], ps_sq[0:1, :], 1.0 / DM, None, ALU.mult)
                        vector.tensor_tensor(rowt[0:1, :], rowm[0:1, :], rowm[0:1, :], ALU.mult)
                        vector.tensor_tensor(rowv[0:1, :], rowv[0:1, :], rowt[0:1, :], ALU.subtract)
                        vector.tensor_copy(rowbf[0:1, 512:1024], rowv[0:1, :])
                        vector.tensor_copy(rowv[0:1, :], rowbf[0:1, 512:1024])
                        vector.tensor_scalar(rowv[0:1, :], rowv[0:1, :], 1.0e-5, None, ALU.add)
                        vector.drain().then_inc(S["s_rw1"], 1)
                        vector.wait_ge(S["s_rw2"], tb + 1)
                        vector.tensor_copy(rowbf[0:1, 512:1024], rowt[0:1, :])
                        vector.reciprocal(rowr[0:1, :], rowbf[0:1, 512:1024])
                        vector.drain().then_inc(S["s_row"], 1)
                        vector.wait_ge(S["s_bcmm"], tb + 1)
                        vector.tensor_copy(Rb[:, :], ps_bc[:, 0:512])
                        vector.tensor_copy(Mb[:, :], ps_bc[:, 512:1024])
                        vector.drain().then_inc(S["s_bc"], 1)
                        if tb >= 1:
                            vector.wait_ge(S["s_prj"], 6 * tb)
                            vector.wait_ge(S["s_sp"], 32 * tb)
                        for c in range(NC):
                            vector.tensor_tensor(tmpn[:, :], xb[:, c * 512:(c + 1) * 512], Mb[:, :], ALU.subtract)
                            vector.tensor_tensor(xn_s[:, c * 512:(c + 1) * 512], tmpn[:, :], Rb[:, :],
                                                 ALU.mult).then_inc(S["s_nrm"], 1)
                        vector.wait_ge(S["s_ldt"], 32 * (tb + 1))
                        ropes = [(qraw[:, 0:512], k_s[:, tb * 512:(tb + 1) * 512])] + [
                            (qraw[:, (512 if h % 2 == 0 else 0):(1024 if h % 2 == 0 else 512)],
                             q_s[:, h * T + tb * 512: h * T + (tb + 1) * 512]) for h in range(4)]
                        for ri_, (srcq, dsta) in enumerate(ropes):
                            vector.wait_ge(S["s_rotmm"], 5 * tb + ri_ + 1)
                            vector.tensor_tensor(t1b[:, :], srcq, csb, ALU.mult)
                            vector.tensor_tensor(t2b[:, :], ps_rot[:, :], snb, ALU.mult)
                            vector.tensor_tensor(dsta, t1b[:, :], t2b[:, :], ALU.add)
                            vector.drain().then_inc(S["s_rp"], 1)
                        vector.wait_ge(S["s_vtb"], tb + 1)
                        for sub in range(4):
                            tile = tb * 4 + sub
                            vector.tensor_copy(v_s[:, tile * 129: tile * 129 + 128],
                                               ps_vt[:, sub * 128:(sub + 1) * 128])
                        vector.drain().then_inc(S["s_vte"], 1)

                @blk.tensor
                def _(tensor):
                    tensor.wait_ge(S["s_ldc"], 16 * 14)
                    for tb in range(NB):
                        buf = tb % 2
                        xb = xT_s[:, buf * 8192:(buf + 1) * 8192]
                        tensor.wait_ge(S["s_ld"], 64 * (tb + 1))
                        if tb >= 1:
                            tensor.wait_ge(S["s_rw1"], tb)
                        mm = None
                        for c in range(NC):
                            mm = tensor.matmul(ps_sum[0:1, :], onec[:, 0:1],
                                               xb[:, c * 512:(c + 1) * 512],
                                               start=(c == 0), stop=(c == NC - 1))
                        mm.then_inc(S["s_stsum"], 1)
                        for c in range(NC):
                            gi = tb * NC + c
                            tensor.wait_ge(S["s_sqf"], gi + 1)
                            mm = tensor.matmul(ps_sq[0:1, :], onec[:, 0:1],
                                               xsq[:, (gi % 4) * 512:(gi % 4 + 1) * 512],
                                               start=(c == 0), stop=(c == NC - 1))
                            mm.then_inc(S["s_sqmm"], 1)
                        tensor.wait_ge(S["s_row"], tb + 1)
                        if tb >= 1:
                            tensor.wait_ge(S["s_bc"], tb)
                        tensor.matmul(ps_bc[:, 0:512], onerF[0:1, :], rowr[0:1, :], start=True, stop=True)
                        mm = tensor.matmul(ps_bc[:, 512:1024], oner[0:1, :], rowbf[0:1, 0:512], start=True, stop=True)
                        mm.then_inc(S["s_bcmm"], 1)
                        projs = [("K", wk_s, 0), ("V", wv_s, 0), ("Q0", wq_s, 0),
                                 ("Q1", wq_s, 1), ("Q2", wq_s, 2), ("Q3", wq_s, 3)]
                        for gi, (nm, wsb, mi) in enumerate(projs):
                            psum = ps_pA if gi % 2 == 0 else ps_pB
                            need = 6 * tb + gi - 1
                            if need >= 1:
                                tensor.wait_ge(S["s_evA"], need)
                            for c in range(NC):
                                tensor.wait_ge(S["s_nrm"], NC * tb + c + 1)
                                if nm in ("K", "V"):
                                    lhs = wsb[:, c * 128:(c + 1) * 128]
                                else:
                                    lhs = wsb[:, c * 512 + mi * 128: c * 512 + (mi + 1) * 128]
                                mm = tensor.matmul(psum[:, :], lhs, xn_s[:, c * 512:(c + 1) * 512],
                                                   start=(c == 0), stop=(c == NC - 1))
                            mm.then_inc(S["s_prj"], 1)
                            if nm == "Q0":
                                tensor.wait_ge(S["s_evA"], 6 * tb + 3)
                                if tb >= 1:
                                    tensor.wait_ge(S["s_vte"], tb)
                                for sub in range(4):
                                    mm2 = tensor.transpose(ps_vt[:, sub * 128:(sub + 1) * 128],
                                                           vE[:, sub * 128:(sub + 1) * 128],
                                                           iden[:, :])
                                mm2.then_inc(S["s_vtb"], 1)
                            if nm != "V":
                                rot_i = {"K": 0, "Q0": 1, "Q1": 2, "Q2": 3, "Q3": 4}[nm]
                                tensor.wait_ge(S["s_evA"], 6 * tb + gi + 1)
                                if 5 * tb + rot_i >= 1:
                                    tensor.wait_ge(S["s_rp"], 5 * tb + rot_i)
                                slot = qraw[:, 0:512] if nm in ("K", "Q1", "Q3") else qraw[:, 512:1024]
                                mm3 = tensor.matmul(ps_rot[:, :], rotM[:, :], slot, start=True, stop=True)
                                mm3.then_inc(S["s_rotmm"], 1)

        with contextlib.ExitStack() as sB:
            EB = sB.enter_context
            pT = EB(nc.sbuf_tensor("pT", [128, 3 * 1024], BF16))
            on_s = EB(nc.sbuf_tensor("on_s", [128, 4 * 128], BF16))
            rden = EB(nc.sbuf_tensor("rden", [128, 8], F32))
            oTb = EB(nc.sbuf_tensor("oTb", [128, 2 * 512], BF16))
            xdum = EB(nc.sbuf_tensor("xdum", [1, 2], BF16))
            sS0 = EB(nc.psum_tensor("sS0", [128, 1024], F32))
            sS1 = EB(nc.psum_tensor("sS1", [128, 1024], F32))
            ob01 = EB(nc.psum_tensor("ob01", [128, 260], F32))
            ob23 = EB(nc.psum_tensor("ob23", [128, 260], F32))
            ptr = EB(nc.psum_tensor("ptr", [128, 512], BF16))

            def ob_seg(sub):
                tt = ob01 if sub < 2 else ob23
                return tt[:, (sub % 2) * 129:(sub % 2 + 1) * 129]

            with nc.Block("Bb") as blk:

                @blk.tensor
                def _(tensor):
                    tensor.wait_ge(S["s_rp"], 5 * NB)
                    tensor.wait_ge(S["s_vte"], NB)

                    def emit_dummies(ri):
                        if ri >= 1:
                            tensor.wait_ge(S["s_ev"], EV_TS_DONE[ri - 1])
                        tensor.matmul(ob01[0:128, 258:260], iden[:, :], onec[:, 0:2],
                                      start=True, stop=True, skip_group_check=True)
                        tensor.matmul(ob23[0:128, 258:260], iden[:, :], onec[:, 0:2],
                                      start=True, stop=True, skip_group_check=True)

                    def emit_qk(s_id):
                        ri = ROW_OF[s_id]
                        qb, h = ROWS[ri]
                        j = LOC_OF[s_id]
                        q_ap = q_s[:, h * T + qb * 512: h * T + (qb + 1) * 512]
                        ps = sS0 if s_id % 2 == 0 else sS1
                        if s_id >= 2:
                            tensor.wait_ge(S["s_xp"], XP_OF[s_id - 2])
                        mm = None
                        for ti in range(2):
                            kt = 2 * j + ti
                            is_diag = (kt >= 4 * qb)
                            mm = tensor.matmul(ps[:, ti * 512:(ti + 1) * 512],
                                               k_s[:, kt * 128:(kt + 1) * 128], q_ap,
                                               start=True, stop=not is_diag)
                            if is_diag:
                                off = 512 - 128 * (kt - 4 * qb)
                                mm = tensor.matmul(ps[:, ti * 512:(ti + 1) * 512],
                                                   iden[:, :], maska[:, off:off + 512],
                                                   start=False, stop=True)
                        mm.then_inc(S["s_mm"], 1)

                    def emit_pv(s_id):
                        ri = ROW_OF[s_id]
                        if LOC_OF[s_id] == 0:
                            emit_dummies(ri)
                        tensor.wait_ge(S["s_xp"], XP_OF[s_id] + 2)
                        slot = s_id % 3
                        j = LOC_OF[s_id]
                        last = (s_id == ROW_GROUPS[ri][-1])
                        mm = None
                        for ti in range(2):
                            kt = 2 * j + ti
                            for sub in range(4):
                                mm = tensor.matmul(
                                    ob_seg(sub),
                                    pT[:, slot * 1024 + ti * 512 + sub * 128:
                                       slot * 1024 + ti * 512 + (sub + 1) * 128],
                                    v_s[:, kt * 129: kt * 129 + 129],
                                    start=False, stop=(last and ti == 1),
                                    skip_group_check=True)
                        mm.then_inc(S["s_pv"], 1)
                        if last:
                            tensor.wait_ge(S["s_ev"], EV_DUMMY_DONE[ri])
                            for si in range(4):
                                mmt = tensor.transpose(ptr[:, si * 128:(si + 1) * 128],
                                                       on_s[:, si * 128:(si + 1) * 128], iden[:, :])
                            mmt.then_inc(S["s_tr"], 1)

                    for s_id in range(NGRP):
                        emit_qk(s_id)
                        if s_id >= 2:
                            emit_pv(s_id - 2)
                    emit_pv(NGRP - 2)
                    emit_pv(NGRP - 1)

                @blk.scalar
                def _(scalar):
                    for s_id in range(NGRP):
                        scalar.wait_ge(S["s_mm"], s_id + 1)
                        if s_id >= 3:
                            scalar.wait_ge(S["s_pv"], s_id - 2)
                        ps = sS0 if s_id % 2 == 0 else sS1
                        slot = s_id % 3
                        scalar.activation(pT[:, slot * 1024:(slot + 1) * 1024],
                                          ps[:, :], AF.Exp, scale=SCALE)
                        scalar.drain().then_inc(S["s_xp"], 1)
                    scalar.activation(xdum[0:1, :], xdum[0:1, :], AF.Exp)
                    scalar.drain().then_inc(S["s_xp"], 1)
                    scalar.activation(xdum[0:1, :], xdum[0:1, :], AF.Exp)
                    scalar.drain().then_inc(S["s_xp"], 1)

                @blk.vector
                def _(vector):
                    vector.memset(xdum[0:1, :], 0.0)

                    def evacs(r):
                        par = r % 2
                        vector.wait_ge(S["s_tr"], r + 1)
                        if r >= 2:
                            vector.wait_ge(S["s_sto"], 16 * (r - 1))
                        for si in range(4):
                            vector.tensor_copy(oTb[:, par * 512 + si * 128: par * 512 + (si + 1) * 128],
                                               ptr[:, si * 128:(si + 1) * 128]).then_inc(S["s_ev"], 1)

                    for ri in range(len(ROWS)):
                        vector.wait_ge(S["s_pv"], ROW_GROUPS[ri][-1] + 1)
                        if ri >= 1:
                            evacs(ri - 1)
                        for sub in range(4):
                            vector.reciprocal(rden[:, sub:sub + 1], ob_seg(sub)[:, 128:129])
                        for sub in range(4):
                            vector.tensor_scalar(on_s[:, sub * 128:(sub + 1) * 128], ob_seg(sub)[:, 0:128],
                                                 rden[:, sub:sub + 1], None,
                                                 ALU.mult).then_inc(S["s_ev"], 1)
                        vector.memset(rden[:, 4:5], 0.0)
                        vector.drain().then_inc(S["s_ev"], 1)
                    evacs(len(ROWS) - 1)

                @blk.gpsimd
                def _(gpsimd):
                    for ri, (qb, h) in enumerate(ROWS):
                        par = ri % 2
                        gpsimd.wait_ge(S["s_ev"], EV_EVAC_DONE[ri])
                        gpsimd.dma_start(out=O_scr.ap()[h * 128:(h + 1) * 128, qb * 512:(qb + 1) * 512],
                                         in_=oTb[:, par * 512:(par + 1) * 512]).then_inc(S["s_sto"], 16)
                    gpsimd.wait_ge(S["s_sto"], 16 * len(ROWS))
                    gpsimd.collective_compute(
                        "AllGather", ALU.bypass, replica_groups=RG,
                        ins=[O_scr.ap().opt()], outs=[OG.ap().opt()]).then_inc(S["s_cc"], 1)

        sAB.close()
        with contextlib.ExitStack() as sC:
            EC = sC.enter_context
            wos = EC(nc.sbuf_tensor("wos", [128, NC * 512], BF16))
            wraw = EC(nc.sbuf_tensor("wraw", [128, NC * 512], F32))
            whi = EC(nc.sbuf_tensor("whi", [128, NC * 512], BF16))
            wlo = EC(nc.sbuf_tensor("wlo", [128, NC * 512], BF16))
            ogs = EC(nc.sbuf_tensor("ogs", [128, 2 * NC * 512], BF16))
            xns = EC(nc.sbuf_tensor("xns", [128, 2 * NC * 512], BF16))
            xqs = EC(nc.sbuf_tensor("xqs", [128, 2 * 4 * 512], BF16))
            stile = EC(nc.sbuf_tensor("stile", [128, 512], BF16))
            gpre = EC(nc.sbuf_tensor("gpre", [128, 512], F32))
            gact = EC(nc.sbuf_tensor("gact", [128, 512], F32))
            sgt = EC(nc.sbuf_tensor("sgt", [128, 512], F32))
            outb = EC(nc.sbuf_tensor("outb", [128, 2 * 4 * 512], F32))
            psS0 = EC(nc.psum_tensor("psS0", [128, 512], F32))
            psS1 = EC(nc.psum_tensor("psS1", [128, 512], F32))
            psG0 = EC(nc.psum_tensor("psG0", [128, 512], F32))
            psG1 = EC(nc.psum_tensor("psG1", [128, 512], F32))

            with nc.Block("Cc") as blk:

                @blk.sync
                def _(sync):
                    sync.wait_ge(S["s_cc"], 1)
                    for half in range(2):
                        sync.dma_start(
                            out=wos[:, :].rearrange("p (c m) -> p c m", c=NC)[:, half * 8:(half + 1) * 8, :],
                            in_=wo_q.rearrange("(c p) m -> p c m", p=128)[:, half * 8:(half + 1) * 8, :],
                        ).then_inc(S["s_cw"], 16)
                        sync.dma_start(
                            out=wraw[:, :].rearrange("p (c m) -> p c m", c=NC)[:, half * 8:(half + 1) * 8, :],
                            in_=wf_q.rearrange("(c p) m -> p c m", p=128)[:, half * 8:(half + 1) * 8, :],
                        ).then_inc(S["s_cw"], 16)
                    for tb in range(NB):
                        par = tb % 2
                        if tb >= 2:
                            sync.wait_ge(S["s_cmm"], 8 * (tb - 1))
                            sync.wait_ge(S["s_cdone"], 4 * (tb - 1))
                        ogd = ogs[:, par * 8192:(par + 1) * 8192].rearrange("p (c t) -> p c t", c=NC)
                        ogsrc = OG.ap()[:, tb * 512:(tb + 1) * 512].rearrange("(c p) t -> p c t", p=128)
                        xnd = xns[:, par * 8192:(par + 1) * 8192].rearrange("p (c t) -> p c t", c=NC)
                        xnsrc = xn_scr[:, tb * 512:(tb + 1) * 512].rearrange("(c p) t -> p c t", p=128)
                        for half in range(2):
                            sync.dma_start(out=ogd[:, half * 8:(half + 1) * 8, :],
                                           in_=ogsrc[:, half * 8:(half + 1) * 8, :]).then_inc(S["s_cg"], 16)
                            sync.dma_start(out=xnd[:, half * 8:(half + 1) * 8, :],
                                           in_=xnsrc[:, half * 8:(half + 1) * 8, :]).then_inc(S["s_cg"], 16)
                        sync.dma_start(
                            out=xqs[:, par * 2048:(par + 1) * 2048].rearrange("p (c t) -> p c t", c=4),
                            in_=xq[:, tb * 512:(tb + 1) * 512].rearrange("(c p) t -> p c t", p=128),
                        ).then_inc(S["s_cg"], 16)

                @blk.vector
                def _(vector):
                    vector.wait_ge(S["s_cw"], 64)
                    for c in range(NC):
                        vector.tensor_copy(whi[:, c * 512:(c + 1) * 512], wraw[:, c * 512:(c + 1) * 512])
                        vector.tensor_tensor(wlo[:, c * 512:(c + 1) * 512],
                                             wraw[:, c * 512:(c + 1) * 512],
                                             whi[:, c * 512:(c + 1) * 512],
                                             ALU.subtract).then_inc(S["s_wsp"], 1)
                    for tb in range(NB):
                        par = tb % 2
                        if tb >= 2:
                            vector.wait_ge(S["s_out"], 16 * (tb - 1))
                        for m in range(4):
                            gm = 4 * tb + m
                            psS = psS0 if gm % 2 == 0 else psS1
                            psG = psG0 if gm % 2 == 0 else psG1
                            vector.wait_ge(S["s_cmm"], 8 * tb + 2 * m + 1)
                            vector.tensor_copy(stile[:, :], psS[:, :])
                            vector.wait_ge(S["s_cmm"], 8 * tb + 2 * m + 2)
                            if gm >= 1:
                                vector.wait_ge(S["s_sil"], gm)
                            vector.tensor_scalar(gpre[:, :], psG[:, :], fsc[:, m:m + 1],
                                                 fbi[:, m:m + 1], ALU.mult, ALU.add)
                            vector.drain().then_inc(S["s_cev"], 1)
                            vector.wait_ge(S["s_sil"], gm + 1)
                            vector.tensor_tensor(sgt[:, :], stile[:, :], gact[:, :], ALU.mult)
                            vector.tensor_tensor(outb[:, par * 2048 + m * 512: par * 2048 + (m + 1) * 512],
                                                 sgt[:, :],
                                                 xqs[:, par * 2048 + m * 512: par * 2048 + (m + 1) * 512],
                                                 ALU.add).then_inc(S["s_cdone"], 1)

                @blk.scalar
                def _(scalar):
                    for gm in range(4 * NB):
                        scalar.wait_ge(S["s_cev"], gm + 1)
                        if gm >= 1:
                            scalar.wait_ge(S["s_cdone"], gm)
                        scalar.activation(gact[:, :], gpre[:, :], AF.Silu)
                        scalar.drain().then_inc(S["s_sil"], 1)

                @blk.tensor
                def _(tensor):
                    tensor.wait_ge(S["s_wsp"], NC)
                    for tb in range(NB):
                        par = tb % 2
                        tensor.wait_ge(S["s_cg"], 80 * (tb + 1))
                        ogb = ogs[:, par * 8192:(par + 1) * 8192]
                        xnb = xns[:, par * 8192:(par + 1) * 8192]
                        for m in range(4):
                            gm = 4 * tb + m
                            psS = psS0 if gm % 2 == 0 else psS1
                            psG = psG0 if gm % 2 == 0 else psG1
                            if gm >= 2:
                                tensor.wait_ge(S["s_cev"], gm - 1)
                            mm = None
                            for c in range(NC):
                                mm = tensor.matmul(psS[:, :],
                                                   wos[:, c * 512 + m * 128: c * 512 + (m + 1) * 128],
                                                   ogb[:, c * 512:(c + 1) * 512],
                                                   start=(c == 0), stop=(c == NC - 1))
                            mm.then_inc(S["s_cmm"], 1)
                            for pi, wsb in enumerate([whi, wlo]):
                                for c in range(NC):
                                    mm = tensor.matmul(psG[:, :],
                                                       wsb[:, c * 512 + m * 128: c * 512 + (m + 1) * 128],
                                                       xnb[:, c * 512:(c + 1) * 512],
                                                       start=(pi == 0 and c == 0),
                                                       stop=(pi == 1 and c == NC - 1))
                            mm.then_inc(S["s_cmm"], 1)

                @blk.gpsimd
                def _(gpsimd):
                    for tb in range(NB):
                        par = tb % 2
                        gpsimd.wait_ge(S["s_cdone"], 4 * (tb + 1))
                        gpsimd.dma_start(
                            out=out_T[:, tb * 512:(tb + 1) * 512].rearrange("(c p) t -> p c t", p=128),
                            in_=outb[:, par * 2048:(par + 1) * 2048].rearrange("p (c t) -> p c t", c=4),
                        ).then_inc(S["s_out"], 16)
                    gpsimd.wait_ge(S["s_out"], 16 * NB)

    return nc


def kernel(**inputs):
    x = inputs["L0_attn_x"]
    Wq = inputs["L0_attn_Wq"]
    bq = inputs["L0_attn_bq"]
    Wk = inputs["L0_attn_Wk"]
    bk = inputs["L0_attn_bk"]
    Wv = inputs["L0_attn_Wv"]
    bv = inputs["L0_attn_bv"]
    Wo = inputs["L0_attn_Wo"]
    wf = inputs["L0_ffn_weight"]
    fs = inputs["L0_ffn_scale_weight"]
    fb = inputs["L0_ffn_bias"]

    tt = np.arange(64, dtype=np.float32) / np.float32(64)
    inv = np.float32(1.0) / (np.float32(1.0e6) ** tt)
    ang = np.outer(np.arange(T, dtype=np.float32), inv)
    cosT = np.cos(ang).astype(np.float32).astype(bf16)
    sinT = np.sin(ang).astype(np.float32).astype(bf16)
    cosF = np.ascontiguousarray(np.concatenate([cosT.T, cosT.T], axis=0))
    sinF = np.ascontiguousarray(np.concatenate([sinT.T, sinT.T], axis=0))
    jp = np.arange(128, dtype=np.int64)[:, None]
    uu = np.arange(1024, dtype=np.int64)[None, :]
    maska = np.where(jp <= uu - 512, np.float32(0.0), np.float32(-1.0e30)).astype(bf16)
    iden = np.eye(128, dtype=np.float32).astype(bf16)
    rotM = np.zeros((128, 128), dtype=np.float32)
    for d in range(64):
        rotM[d + 64, d] = -1.0
        rotM[d, d + 64] = 1.0
    rotM = rotM.astype(bf16)
    onec = np.ones((128, 2), dtype=np.float32).astype(bf16)
    oner = np.ones((1, 128), dtype=np.float32).astype(bf16)
    onerF = np.ones((1, 128), dtype=np.float32)

    wfT = np.ascontiguousarray(wf.T)
    in_maps = []
    for c in range(8):
        b, k = c // 4, c % 4
        xTb = np.ascontiguousarray(np.asarray(x[b]).T)
        m = {
            "xT": xTb,
            "xq": np.ascontiguousarray(xTb[512 * k:512 * (k + 1), :]),
            "wq": np.ascontiguousarray(Wq[:, 512 * k:512 * (k + 1)]),
            "wk": np.ascontiguousarray(Wk[:, 128 * k:128 * (k + 1)]),
            "wv": np.ascontiguousarray(Wv[:, 128 * k:128 * (k + 1)]),
            "wo_q": np.ascontiguousarray(Wo[:, 512 * k:512 * (k + 1)]),
            "wf_q": np.ascontiguousarray(wfT[:, 512 * k:512 * (k + 1)]),
            "bq4": np.ascontiguousarray(bq[512 * k:512 * (k + 1)].reshape(4, 128).T),
            "bk1": np.ascontiguousarray(bk[128 * k:128 * (k + 1)].reshape(1, 128).T),
            "bv1": np.ascontiguousarray(bv[128 * k:128 * (k + 1)].reshape(1, 128).T),
            "fscI": np.ascontiguousarray(fs[512 * k:512 * (k + 1)].reshape(4, 128).T.astype(np.float32)),
            "fbiI": np.ascontiguousarray(fb[512 * k:512 * (k + 1)].reshape(4, 128).T.astype(np.float32)),
            "cosI": cosF, "sinI": sinF, "maskI": maska, "idenI": iden,
            "rotI": rotM, "onecI": onec, "onerI": oner, "onerFI": onerF,
        }
        in_maps.append(m)

    nc = build_nc()
    res = run_bass_kernel_spmd(nc, in_maps, core_ids=list(range(8)))

    out = np.empty((B, T, DM), dtype=np.float32)
    for c in range(8):
        b, k = c // 4, c % 4
        out[b, :, 512 * k:512 * (k + 1)] = res.results[c]["out_T"].T
    return out




# version 10
# speedup vs baseline: 1.1577x; 1.1577x over previous
import contextlib
import numpy as np
import ml_dtypes
import concourse.bass as bass
import concourse.mybir as mybir
from concourse.bass_utils import run_bass_kernel_spmd

F32 = mybir.dt.float32
BF16 = mybir.dt.bfloat16
AF = mybir.ActivationFunctionType
ALU = mybir.AluOpType
bf16 = ml_dtypes.bfloat16

B = 2
T = 8192
DM = 2048
D = 128
NB = T // 512
NC = DM // 128
SCALE = 1.0 / float(np.sqrt(D))
RG = [[0, 1, 2, 3], [4, 5, 6, 7]]

ROWS = [(0, 0), (1, 0)] + [(qb, h) for qb in range(NB) for h in range(4)]
N_SACR = 2
ROW_GROUPS = []
_g = 0
for (qb, _h) in ROWS:
    ROW_GROUPS.append(list(range(_g, _g + 2 * (qb + 1))))
    _g += 2 * (qb + 1)
NGRP = _g
XP_OF = {s: s + 1 for s in range(NGRP)}
EV_EVAC_DONE = {}
EV_TS_DONE = {}
EV_DUMMY_DONE = {}
_ev = 0
for _r in range(len(ROWS)):
    if _r >= 1:
        _ev += 4
        EV_EVAC_DONE[_r - 1] = _ev
    _ev += 4
    EV_TS_DONE[_r] = _ev
    _ev += 1
    EV_DUMMY_DONE[_r] = _ev
_ev += 4
EV_EVAC_DONE[len(ROWS) - 1] = _ev
EV_TOTAL = _ev
LOC_OF = {}
ROW_OF = {}
for _ri in range(len(ROWS)):
    for _j, _s in enumerate(ROW_GROUPS[_ri]):
        LOC_OF[_s] = _j
        ROW_OF[_s] = _ri


def build_nc():
    nc = bass.Bass("TRN2", target_bir_lowering=False)

    xT = nc.dram_tensor("xT", [DM, T], BF16, kind="ExternalInput").ap()
    xq = nc.dram_tensor("xq", [512, T], BF16, kind="ExternalInput").ap()
    wq = nc.dram_tensor("wq", [DM, 512], BF16, kind="ExternalInput").ap()
    wk = nc.dram_tensor("wk", [DM, 128], BF16, kind="ExternalInput").ap()
    wv = nc.dram_tensor("wv", [DM, 128], BF16, kind="ExternalInput").ap()
    wo_q = nc.dram_tensor("wo_q", [DM, 512], BF16, kind="ExternalInput").ap()
    wf_q = nc.dram_tensor("wf_q", [DM, 512], F32, kind="ExternalInput").ap()
    bq4 = nc.dram_tensor("bq4", [128, 4], BF16, kind="ExternalInput").ap()
    bk1 = nc.dram_tensor("bk1", [128, 1], BF16, kind="ExternalInput").ap()
    bv1 = nc.dram_tensor("bv1", [128, 1], BF16, kind="ExternalInput").ap()
    fscI = nc.dram_tensor("fscI", [128, 4], F32, kind="ExternalInput").ap()
    fbiI = nc.dram_tensor("fbiI", [128, 4], F32, kind="ExternalInput").ap()
    cosI = nc.dram_tensor("cosI", [128, T], BF16, kind="ExternalInput").ap()
    sinI = nc.dram_tensor("sinI", [128, T], BF16, kind="ExternalInput").ap()
    maskI = nc.dram_tensor("maskI", [128, 1024], BF16, kind="ExternalInput").ap()
    idenI = nc.dram_tensor("idenI", [128, 128], BF16, kind="ExternalInput").ap()
    rotI = nc.dram_tensor("rotI", [128, 128], BF16, kind="ExternalInput").ap()
    onecI = nc.dram_tensor("onecI", [128, 2], BF16, kind="ExternalInput").ap()
    onerI = nc.dram_tensor("onerI", [1, 128], BF16, kind="ExternalInput").ap()
    onerFI = nc.dram_tensor("onerFI", [1, 128], F32, kind="ExternalInput").ap()

    out_T = nc.dram_tensor("out_T", [512, T], F32, kind="ExternalOutput").ap()

    O_scr = nc.dram_tensor("O_scr", [512, T], BF16, kind="Internal")
    OG = nc.dram_tensor("OG", [DM, T], BF16, kind="Internal")
    xn_scr = nc.dram_tensor("xn_scr", [DM, T], BF16, kind="Internal").ap()

    st = contextlib.ExitStack()
    E = st.enter_context

    iden = E(nc.sbuf_tensor("iden", [128, 128], BF16))
    rotM = E(nc.sbuf_tensor("rotM", [128, 128], BF16))
    maska = E(nc.sbuf_tensor("maska", [128, 1024], BF16))
    onec = E(nc.sbuf_tensor("onec", [128, 2], BF16))
    oner = E(nc.sbuf_tensor("oner", [1, 128], BF16))
    onerF = E(nc.sbuf_tensor("onerF", [1, 128], F32))
    bqb = E(nc.sbuf_tensor("bqb", [128, 4], BF16))
    bkb = E(nc.sbuf_tensor("bkb", [128, 1], BF16))
    bvb = E(nc.sbuf_tensor("bvb", [128, 1], BF16))
    bq_s = E(nc.sbuf_tensor("bq_s", [128, 4], F32))
    bk_s = E(nc.sbuf_tensor("bk_s", [128, 1], F32))
    bv_s = E(nc.sbuf_tensor("bv_s", [128, 1], F32))
    fsc = E(nc.sbuf_tensor("fsc", [128, 4], F32))
    fbi = E(nc.sbuf_tensor("fbi", [128, 4], F32))

    names = ["s_ld", "s_ldt", "s_ldc", "s_sqf", "s_sqb", "s_sqmm", "s_stsum",
             "s_rw1", "s_rw2", "s_row", "s_bcmm", "s_bc", "s_nrm", "s_prj", "s_evA",
             "s_rotmm", "s_rp", "s_vtb", "s_vte", "s_sp",
             "s_mm", "s_xp", "s_pv", "s_ev", "s_tr", "s_sto", "s_cc",
             "s_cw", "s_wsp", "s_cg", "s_cmm", "s_cev", "s_sil", "s_cdone", "s_out"]
    S = {n: E(nc.semaphore(n)) for n in names}

    with st:
        sAB = contextlib.ExitStack()
        q_s = sAB.enter_context(nc.sbuf_tensor("q_s", [128, 4 * T], BF16))
        k_s = sAB.enter_context(nc.sbuf_tensor("k_s", [128, T], BF16))
        v_s = sAB.enter_context(nc.sbuf_tensor("v_s", [128, (T // 128) * 129], BF16))
        with contextlib.ExitStack() as sA:
            EA = sA.enter_context
            wq_s = EA(nc.sbuf_tensor("wq_s", [128, NC * 512], BF16))
            wk_s = EA(nc.sbuf_tensor("wk_s", [128, NC * 128], BF16))
            wv_s = EA(nc.sbuf_tensor("wv_s", [128, NC * 128], BF16))
            xT_s = EA(nc.sbuf_tensor("xT_s", [128, 2 * NC * 512], BF16))
            xsq = EA(nc.sbuf_tensor("xsq", [128, 4 * 512], BF16))
            xn_s = EA(nc.sbuf_tensor("xn_s", [128, NC * 512], BF16))
            cs_s = EA(nc.sbuf_tensor("cs_s", [128, 2 * 512], BF16))
            sn_s = EA(nc.sbuf_tensor("sn_s", [128, 2 * 512], BF16))
            rowm = EA(nc.sbuf_tensor("rowm", [1, 512], F32))
            rowv = EA(nc.sbuf_tensor("rowv", [1, 512], F32))
            rowt = EA(nc.sbuf_tensor("rowt", [1, 512], F32))
            rowr = EA(nc.sbuf_tensor("rowr", [1, 512], F32))
            rowbf = EA(nc.sbuf_tensor("rowbf", [1, 1024], BF16))
            Rb = EA(nc.sbuf_tensor("Rb", [128, 512], F32))
            Mb = EA(nc.sbuf_tensor("Mb", [128, 512], BF16))
            tmpn = EA(nc.sbuf_tensor("tmpn", [128, 512], BF16))
            qraw = EA(nc.sbuf_tensor("qraw", [128, 2 * 512], BF16))
            t1b = EA(nc.sbuf_tensor("t1b", [128, 512], BF16))
            t2b = EA(nc.sbuf_tensor("t2b", [128, 512], BF16))
            vE = EA(nc.sbuf_tensor("vE", [128, 512], BF16))
            ps_sum = EA(nc.psum_tensor("ps_sum", [128, 512], F32))
            ps_sq = EA(nc.psum_tensor("ps_sq", [128, 512], F32))
            ps_bc = EA(nc.psum_tensor("ps_bc", [128, 1024], F32))
            ps_pA = EA(nc.psum_tensor("ps_pA", [128, 512], F32))
            ps_pB = EA(nc.psum_tensor("ps_pB", [128, 512], F32))
            ps_rot = EA(nc.psum_tensor("ps_rot", [128, 512], F32))
            ps_vt = EA(nc.psum_tensor("ps_vt", [128, 4 * 128], BF16))

            with nc.Block("A") as blk:

                @blk.sync
                def _(sync):
                    for dst, src in [
                        (iden[:, :], idenI), (rotM[:, :], rotI), (maska[:, :], maskI),
                        (onec[:, :], onecI), (oner[:, :], onerI), (onerF[:, :], onerFI),
                        (bqb[:, :], bq4), (bkb[:, :], bk1), (bvb[:, :], bv1),
                        (fsc[:, :], fscI), (fbi[:, :], fbiI),
                    ]:
                        sync.dma_start(out=dst, in_=src).then_inc(S["s_ldc"], 16)
                    sync.dma_start(out=wq_s[:, :].rearrange("p (c m) -> p c m", c=NC),
                                   in_=wq.rearrange("(c p) m -> p c m", p=128)).then_inc(S["s_ldc"], 16)
                    sync.dma_start(out=wk_s[:, :].rearrange("p (c m) -> p c m", c=NC),
                                   in_=wk.rearrange("(c p) m -> p c m", p=128)).then_inc(S["s_ldc"], 16)
                    sync.dma_start(out=wv_s[:, :].rearrange("p (c m) -> p c m", c=NC),
                                   in_=wv.rearrange("(c p) m -> p c m", p=128)).then_inc(S["s_ldc"], 16)
                    for tb in range(NB):
                        buf = tb % 2
                        if tb >= 2:
                            sync.wait_ge(S["s_sqb"], tb - 1)
                            sync.wait_ge(S["s_stsum"], tb - 1)
                            sync.wait_ge(S["s_nrm"], NC * (tb - 1))
                        src_b = xT[:, tb * 512:(tb + 1) * 512].rearrange("(c p) t -> p c t", p=128)
                        dst_b = xT_s[:, buf * 8192:(buf + 1) * 8192].rearrange("p (c t) -> p c t", c=NC)
                        for pt in range(4):
                            sync.dma_start(out=dst_b[:, pt * 4:(pt + 1) * 4, :],
                                           in_=src_b[:, pt * 4:(pt + 1) * 4, :]).then_inc(S["s_ld"], 16)
                        if tb >= 2:
                            sync.wait_ge(S["s_rp"], 5 * (tb - 1))
                        sync.dma_start(out=cs_s[:, buf * 512:(buf + 1) * 512],
                                       in_=cosI[:, tb * 512:(tb + 1) * 512]).then_inc(S["s_ldt"], 16)
                        sync.dma_start(out=sn_s[:, buf * 512:(buf + 1) * 512],
                                       in_=sinI[:, tb * 512:(tb + 1) * 512]).then_inc(S["s_ldt"], 16)

                @blk.gpsimd
                def _(gpsimd):
                    for tb in range(NB):
                        gpsimd.wait_ge(S["s_nrm"], NC * (tb + 1))
                        dst = xn_scr[:, tb * 512:(tb + 1) * 512].rearrange("(c p) t -> p c t", p=128)
                        src = xn_s[:, :].rearrange("p (c t) -> p c t", c=NC)
                        gpsimd.dma_start(out=dst[:, 0:8, :], in_=src[:, 0:8, :]).then_inc(S["s_sp"], 16)
                        gpsimd.dma_start(out=dst[:, 8:16, :], in_=src[:, 8:16, :]).then_inc(S["s_sp"], 16)

                @blk.scalar
                def _(scalar):
                    scalar.wait_ge(S["s_ldc"], 16 * 14)
                    for tb in range(NB):
                        buf = tb % 2
                        xb = xT_s[:, buf * 8192:(buf + 1) * 8192]
                        scalar.wait_ge(S["s_ld"], 64 * (tb + 1))
                        for c in range(NC):
                            gi = tb * NC + c
                            if gi >= 4:
                                scalar.wait_ge(S["s_sqmm"], gi - 3)
                            scalar.activation(xsq[:, (gi % 4) * 512:(gi % 4 + 1) * 512],
                                              xb[:, c * 512:(c + 1) * 512],
                                              AF.Square).then_inc(S["s_sqf"], 1)
                        scalar.drain().then_inc(S["s_sqb"], 1)
                        scalar.wait_ge(S["s_rw1"], tb + 1)
                        scalar.activation(rowt[0:1, :], rowv[0:1, :], AF.Ln)
                        scalar.activation(rowt[0:1, :], rowt[0:1, :], AF.Exp, scale=0.5)
                        scalar.drain().then_inc(S["s_rw2"], 1)
                        evs = [("K", qraw[:, 0:512], bk_s[0:128, 0:1]),
                               ("V", vE[:, :], bv_s[0:128, 0:1]),
                               ("Q0", qraw[:, 512:1024], bq_s[0:128, 0:1]),
                               ("Q1", qraw[:, 0:512], bq_s[0:128, 1:2]),
                               ("Q2", qraw[:, 512:1024], bq_s[0:128, 2:3]),
                               ("Q3", qraw[:, 0:512], bq_s[0:128, 3:4])]
                        for gi, (nm, dst, bias_ap) in enumerate(evs):
                            scalar.wait_ge(S["s_prj"], 6 * tb + gi + 1)
                            if nm == "K" and tb >= 1:
                                scalar.wait_ge(S["s_rp"], 5 * tb)
                                scalar.wait_ge(S["s_rotmm"], 5 * tb)
                            if nm == "V" and tb >= 1:
                                scalar.wait_ge(S["s_vtb"], tb)
                            if nm in ("Q1", "Q2", "Q3"):
                                k_ = {"Q1": 1, "Q2": 2, "Q3": 3}[nm]
                                scalar.wait_ge(S["s_rp"], 5 * tb + k_)
                                scalar.wait_ge(S["s_rotmm"], 5 * tb + k_)
                            psum = ps_pA if gi % 2 == 0 else ps_pB
                            scalar.activation(dst, psum[:, :], AF.Identity,
                                              bias=bias_ap).then_inc(S["s_evA"], 1)

                @blk.vector
                def _(vector):
                    vector.wait_ge(S["s_ldc"], 16 * 14)
                    vector.memset(v_s[:, :].rearrange("p (n f) -> p n f", f=129)[:, :, 128:129], 1.0)
                    vector.tensor_copy(bq_s[:, :], bqb[:, :])
                    vector.tensor_copy(bk_s[:, :], bkb[:, :])
                    vector.tensor_copy(bv_s[:, :], bvb[:, :])
                    for tb in range(NB):
                        buf = tb % 2
                        xb = xT_s[:, buf * 8192:(buf + 1) * 8192]
                        csb = cs_s[:, buf * 512:(buf + 1) * 512]
                        snb = sn_s[:, buf * 512:(buf + 1) * 512]
                        vector.wait_ge(S["s_stsum"], tb + 1)
                        vector.wait_ge(S["s_sqmm"], NC * (tb + 1))
                        vector.tensor_scalar(rowm[0:1, :], ps_sum[0:1, :], 1.0 / DM, None, ALU.mult)
                        vector.tensor_copy(rowbf[0:1, 0:512], rowm[0:1, :])
                        vector.tensor_scalar(rowv[0:1, :], ps_sq[0:1, :], 1.0 / DM, None, ALU.mult)
                        vector.tensor_tensor(rowt[0:1, :], rowm[0:1, :], rowm[0:1, :], ALU.mult)
                        vector.tensor_tensor(rowv[0:1, :], rowv[0:1, :], rowt[0:1, :], ALU.subtract)
                        vector.tensor_copy(rowbf[0:1, 512:1024], rowv[0:1, :])
                        vector.tensor_copy(rowv[0:1, :], rowbf[0:1, 512:1024])
                        vector.tensor_scalar(rowv[0:1, :], rowv[0:1, :], 1.0e-5, None, ALU.add)
                        vector.drain().then_inc(S["s_rw1"], 1)
                        vector.wait_ge(S["s_rw2"], tb + 1)
                        vector.tensor_copy(rowbf[0:1, 512:1024], rowt[0:1, :])
                        vector.reciprocal(rowr[0:1, :], rowbf[0:1, 512:1024])
                        vector.drain().then_inc(S["s_row"], 1)
                        vector.wait_ge(S["s_bcmm"], tb + 1)
                        vector.tensor_copy(Rb[:, :], ps_bc[:, 0:512])
                        vector.tensor_copy(Mb[:, :], ps_bc[:, 512:1024])
                        vector.drain().then_inc(S["s_bc"], 1)
                        if tb >= 1:
                            vector.wait_ge(S["s_prj"], 6 * tb)
                            vector.wait_ge(S["s_sp"], 32 * tb)
                        for c in range(NC):
                            vector.tensor_tensor(tmpn[:, :], xb[:, c * 512:(c + 1) * 512], Mb[:, :], ALU.subtract)
                            vector.tensor_tensor(xn_s[:, c * 512:(c + 1) * 512], tmpn[:, :], Rb[:, :],
                                                 ALU.mult).then_inc(S["s_nrm"], 1)
                        vector.wait_ge(S["s_ldt"], 32 * (tb + 1))
                        ropes = [(qraw[:, 0:512], k_s[:, tb * 512:(tb + 1) * 512])] + [
                            (qraw[:, (512 if h % 2 == 0 else 0):(1024 if h % 2 == 0 else 512)],
                             q_s[:, h * T + tb * 512: h * T + (tb + 1) * 512]) for h in range(4)]
                        for ri_, (srcq, dsta) in enumerate(ropes):
                            vector.wait_ge(S["s_rotmm"], 5 * tb + ri_ + 1)
                            vector.tensor_tensor(t1b[:, :], srcq, csb, ALU.mult)
                            vector.tensor_tensor(t2b[:, :], ps_rot[:, :], snb, ALU.mult)
                            vector.tensor_tensor(dsta, t1b[:, :], t2b[:, :], ALU.add)
                            vector.drain().then_inc(S["s_rp"], 1)
                        vector.wait_ge(S["s_vtb"], tb + 1)
                        for sub in range(4):
                            tile = tb * 4 + sub
                            vector.tensor_copy(v_s[:, tile * 129: tile * 129 + 128],
                                               ps_vt[:, sub * 128:(sub + 1) * 128])
                        vector.drain().then_inc(S["s_vte"], 1)

                @blk.tensor
                def _(tensor):
                    tensor.wait_ge(S["s_ldc"], 16 * 14)
                    for tb in range(NB):
                        buf = tb % 2
                        xb = xT_s[:, buf * 8192:(buf + 1) * 8192]
                        tensor.wait_ge(S["s_ld"], 64 * (tb + 1))
                        if tb >= 1:
                            tensor.wait_ge(S["s_rw1"], tb)
                        mm = None
                        for c in range(NC):
                            mm = tensor.matmul(ps_sum[0:1, :], onec[:, 0:1],
                                               xb[:, c * 512:(c + 1) * 512],
                                               start=(c == 0), stop=(c == NC - 1))
                        mm.then_inc(S["s_stsum"], 1)
                        for c in range(NC):
                            gi = tb * NC + c
                            tensor.wait_ge(S["s_sqf"], gi + 1)
                            mm = tensor.matmul(ps_sq[0:1, :], onec[:, 0:1],
                                               xsq[:, (gi % 4) * 512:(gi % 4 + 1) * 512],
                                               start=(c == 0), stop=(c == NC - 1))
                            mm.then_inc(S["s_sqmm"], 1)
                        tensor.wait_ge(S["s_row"], tb + 1)
                        if tb >= 1:
                            tensor.wait_ge(S["s_bc"], tb)
                        tensor.matmul(ps_bc[:, 0:512], onerF[0:1, :], rowr[0:1, :], start=True, stop=True)
                        mm = tensor.matmul(ps_bc[:, 512:1024], oner[0:1, :], rowbf[0:1, 0:512], start=True, stop=True)
                        mm.then_inc(S["s_bcmm"], 1)
                        projs = [("K", wk_s, 0), ("V", wv_s, 0), ("Q0", wq_s, 0),
                                 ("Q1", wq_s, 1), ("Q2", wq_s, 2), ("Q3", wq_s, 3)]
                        for gi, (nm, wsb, mi) in enumerate(projs):
                            psum = ps_pA if gi % 2 == 0 else ps_pB
                            need = 6 * tb + gi - 1
                            if need >= 1:
                                tensor.wait_ge(S["s_evA"], need)
                            for c in range(NC):
                                tensor.wait_ge(S["s_nrm"], NC * tb + c + 1)
                                if nm in ("K", "V"):
                                    lhs = wsb[:, c * 128:(c + 1) * 128]
                                else:
                                    lhs = wsb[:, c * 512 + mi * 128: c * 512 + (mi + 1) * 128]
                                mm = tensor.matmul(psum[:, :], lhs, xn_s[:, c * 512:(c + 1) * 512],
                                                   start=(c == 0), stop=(c == NC - 1))
                            mm.then_inc(S["s_prj"], 1)
                            if nm == "Q0":
                                tensor.wait_ge(S["s_evA"], 6 * tb + 3)
                                if tb >= 1:
                                    tensor.wait_ge(S["s_vte"], tb)
                                for sub in range(4):
                                    mm2 = tensor.transpose(ps_vt[:, sub * 128:(sub + 1) * 128],
                                                           vE[:, sub * 128:(sub + 1) * 128],
                                                           iden[:, :])
                                mm2.then_inc(S["s_vtb"], 1)
                            if nm != "V":
                                rot_i = {"K": 0, "Q0": 1, "Q1": 2, "Q2": 3, "Q3": 4}[nm]
                                tensor.wait_ge(S["s_evA"], 6 * tb + gi + 1)
                                if 5 * tb + rot_i >= 1:
                                    tensor.wait_ge(S["s_rp"], 5 * tb + rot_i)
                                slot = qraw[:, 0:512] if nm in ("K", "Q1", "Q3") else qraw[:, 512:1024]
                                mm3 = tensor.matmul(ps_rot[:, :], rotM[:, :], slot, start=True, stop=True)
                                mm3.then_inc(S["s_rotmm"], 1)

        wos = st.enter_context(nc.sbuf_tensor("wos", [128, NC * 512], BF16, side="right"))
        wraw = st.enter_context(nc.sbuf_tensor("wraw", [128, NC * 512], F32, side="right"))
        whi = st.enter_context(nc.sbuf_tensor("whi", [128, NC * 512], BF16, side="right"))
        wlo = st.enter_context(nc.sbuf_tensor("wlo", [128, NC * 512], BF16, side="right"))
        with contextlib.ExitStack() as sB:
            EB = sB.enter_context
            pT = EB(nc.sbuf_tensor("pT", [128, 3 * 1024], BF16))
            on_s = EB(nc.sbuf_tensor("on_s", [128, 4 * 128], BF16))
            rden = EB(nc.sbuf_tensor("rden", [128, 8], F32))
            oTb = EB(nc.sbuf_tensor("oTb", [128, 2 * 512], BF16))
            xdum = EB(nc.sbuf_tensor("xdum", [1, 2], BF16))
            sS0 = EB(nc.psum_tensor("sS0", [128, 1024], F32))
            sS1 = EB(nc.psum_tensor("sS1", [128, 1024], F32))
            ob01 = EB(nc.psum_tensor("ob01", [128, 260], F32))
            ob23 = EB(nc.psum_tensor("ob23", [128, 260], F32))
            ptr = EB(nc.psum_tensor("ptr", [128, 512], BF16))

            def ob_seg(sub):
                tt = ob01 if sub < 2 else ob23
                return tt[:, (sub % 2) * 129:(sub % 2 + 1) * 129]

            with nc.Block("Bb") as blk:

                @blk.tensor
                def _(tensor):
                    tensor.wait_ge(S["s_rp"], 5 * NB)
                    tensor.wait_ge(S["s_vte"], NB)

                    def emit_dummies(ri):
                        if ri >= 1:
                            tensor.wait_ge(S["s_ev"], EV_TS_DONE[ri - 1])
                        tensor.matmul(ob01[0:128, 258:260], iden[:, :], onec[:, 0:2],
                                      start=True, stop=True, skip_group_check=True)
                        tensor.matmul(ob23[0:128, 258:260], iden[:, :], onec[:, 0:2],
                                      start=True, stop=True, skip_group_check=True)

                    def emit_qk(s_id):
                        ri = ROW_OF[s_id]
                        qb, h = ROWS[ri]
                        j = LOC_OF[s_id]
                        q_ap = q_s[:, h * T + qb * 512: h * T + (qb + 1) * 512]
                        ps = sS0 if s_id % 2 == 0 else sS1
                        if s_id >= 2:
                            tensor.wait_ge(S["s_xp"], XP_OF[s_id - 2])
                        mm = None
                        for ti in range(2):
                            kt = 2 * j + ti
                            is_diag = (kt >= 4 * qb)
                            mm = tensor.matmul(ps[:, ti * 512:(ti + 1) * 512],
                                               k_s[:, kt * 128:(kt + 1) * 128], q_ap,
                                               start=True, stop=not is_diag)
                            if is_diag:
                                off = 512 - 128 * (kt - 4 * qb)
                                mm = tensor.matmul(ps[:, ti * 512:(ti + 1) * 512],
                                                   iden[:, :], maska[:, off:off + 512],
                                                   start=False, stop=True)
                        mm.then_inc(S["s_mm"], 1)

                    def emit_pv(s_id):
                        ri = ROW_OF[s_id]
                        if LOC_OF[s_id] == 0:
                            emit_dummies(ri)
                        tensor.wait_ge(S["s_xp"], XP_OF[s_id] + 2)
                        slot = s_id % 3
                        j = LOC_OF[s_id]
                        last = (s_id == ROW_GROUPS[ri][-1])
                        mm = None
                        for ti in range(2):
                            kt = 2 * j + ti
                            for sub in range(4):
                                mm = tensor.matmul(
                                    ob_seg(sub),
                                    pT[:, slot * 1024 + ti * 512 + sub * 128:
                                       slot * 1024 + ti * 512 + (sub + 1) * 128],
                                    v_s[:, kt * 129: kt * 129 + 129],
                                    start=False, stop=(last and ti == 1),
                                    skip_group_check=True)
                        mm.then_inc(S["s_pv"], 1)
                        if last:
                            tensor.wait_ge(S["s_ev"], EV_DUMMY_DONE[ri])
                            for si in range(4):
                                mmt = tensor.transpose(ptr[:, si * 128:(si + 1) * 128],
                                                       on_s[:, si * 128:(si + 1) * 128], iden[:, :])
                            mmt.then_inc(S["s_tr"], 1)

                    for s_id in range(NGRP):
                        emit_qk(s_id)
                        if s_id >= 2:
                            emit_pv(s_id - 2)
                    emit_pv(NGRP - 2)
                    emit_pv(NGRP - 1)

                @blk.scalar
                def _(scalar):
                    for s_id in range(NGRP):
                        scalar.wait_ge(S["s_mm"], s_id + 1)
                        if s_id >= 3:
                            scalar.wait_ge(S["s_pv"], s_id - 2)
                        ps = sS0 if s_id % 2 == 0 else sS1
                        slot = s_id % 3
                        scalar.activation(pT[:, slot * 1024:(slot + 1) * 1024],
                                          ps[:, :], AF.Exp, scale=SCALE).then_inc(S["s_xp"], 1)
                    scalar.activation(xdum[0:1, :], xdum[0:1, :], AF.Exp).then_inc(S["s_xp"], 1)
                    scalar.activation(xdum[0:1, :], xdum[0:1, :], AF.Exp).then_inc(S["s_xp"], 1)

                @blk.vector
                def _(vector):
                    vector.memset(xdum[0:1, :], 0.0)

                    def evacs(r):
                        par = r % 2
                        vector.wait_ge(S["s_tr"], r + 1)
                        if r >= 2:
                            vector.wait_ge(S["s_sto"], 16 * (r - 1))
                        for si in range(4):
                            vector.tensor_copy(oTb[:, par * 512 + si * 128: par * 512 + (si + 1) * 128],
                                               ptr[:, si * 128:(si + 1) * 128]).then_inc(S["s_ev"], 1)

                    for ri in range(len(ROWS)):
                        vector.wait_ge(S["s_pv"], ROW_GROUPS[ri][-1] + 1)
                        if ri >= 1:
                            evacs(ri - 1)
                        for sub in range(4):
                            vector.reciprocal(rden[:, sub:sub + 1], ob_seg(sub)[:, 128:129])
                        for sub in range(4):
                            vector.tensor_scalar(on_s[:, sub * 128:(sub + 1) * 128], ob_seg(sub)[:, 0:128],
                                                 rden[:, sub:sub + 1], None,
                                                 ALU.mult).then_inc(S["s_ev"], 1)
                        vector.memset(rden[:, 4:5], 0.0)
                        vector.drain().then_inc(S["s_ev"], 1)
                    evacs(len(ROWS) - 1)

                @blk.gpsimd
                def _(gpsimd):
                    for ri, (qb, h) in enumerate(ROWS):
                        par = ri % 2
                        gpsimd.wait_ge(S["s_ev"], EV_EVAC_DONE[ri])
                        gpsimd.dma_start(out=O_scr.ap()[h * 128:(h + 1) * 128, qb * 512:(qb + 1) * 512],
                                         in_=oTb[:, par * 512:(par + 1) * 512]).then_inc(S["s_sto"], 16)
                    gpsimd.wait_ge(S["s_sto"], 16 * len(ROWS))
                    gpsimd.collective_compute(
                        "AllGather", ALU.bypass, replica_groups=RG,
                        ins=[O_scr.ap().opt()], outs=[OG.ap().opt()]).then_inc(S["s_cc"], 1)

        sAB.close()
        with contextlib.ExitStack() as sC:
            EC = sC.enter_context
            ogs = EC(nc.sbuf_tensor("ogs", [128, 2 * NC * 512], BF16))
            xns = EC(nc.sbuf_tensor("xns", [128, 2 * NC * 512], BF16))
            xqs = EC(nc.sbuf_tensor("xqs", [128, 2 * 4 * 512], BF16))
            stile = EC(nc.sbuf_tensor("stile", [128, 512], BF16))
            gpre = EC(nc.sbuf_tensor("gpre", [128, 512], F32))
            gact = EC(nc.sbuf_tensor("gact", [128, 512], F32))
            sgt = EC(nc.sbuf_tensor("sgt", [128, 512], F32))
            outb = EC(nc.sbuf_tensor("outb", [128, 2 * 4 * 512], F32))
            psS0 = EC(nc.psum_tensor("psS0", [128, 512], F32))
            psS1 = EC(nc.psum_tensor("psS1", [128, 512], F32))
            psG0 = EC(nc.psum_tensor("psG0", [128, 512], F32))
            psG1 = EC(nc.psum_tensor("psG1", [128, 512], F32))

            with nc.Block("Cc") as blk:

                @blk.sync
                def _(sync):
                    for half in range(2):
                        sync.dma_start(
                            out=wos[:, :].rearrange("p (c m) -> p c m", c=NC)[:, half * 8:(half + 1) * 8, :],
                            in_=wo_q.rearrange("(c p) m -> p c m", p=128)[:, half * 8:(half + 1) * 8, :],
                        ).then_inc(S["s_cw"], 16)
                        sync.dma_start(
                            out=wraw[:, :].rearrange("p (c m) -> p c m", c=NC)[:, half * 8:(half + 1) * 8, :],
                            in_=wf_q.rearrange("(c p) m -> p c m", p=128)[:, half * 8:(half + 1) * 8, :],
                        ).then_inc(S["s_cw"], 16)
                    sync.wait_ge(S["s_cc"], 1)
                    for tb in range(NB):
                        par = tb % 2
                        if tb >= 2:
                            sync.wait_ge(S["s_cmm"], 8 * (tb - 1))
                            sync.wait_ge(S["s_cdone"], 4 * (tb - 1))
                        ogd = ogs[:, par * 8192:(par + 1) * 8192].rearrange("p (c t) -> p c t", c=NC)
                        ogsrc = OG.ap()[:, tb * 512:(tb + 1) * 512].rearrange("(c p) t -> p c t", p=128)
                        xnd = xns[:, par * 8192:(par + 1) * 8192].rearrange("p (c t) -> p c t", c=NC)
                        xnsrc = xn_scr[:, tb * 512:(tb + 1) * 512].rearrange("(c p) t -> p c t", p=128)
                        for half in range(2):
                            sync.dma_start(out=ogd[:, half * 8:(half + 1) * 8, :],
                                           in_=ogsrc[:, half * 8:(half + 1) * 8, :]).then_inc(S["s_cg"], 16)
                            sync.dma_start(out=xnd[:, half * 8:(half + 1) * 8, :],
                                           in_=xnsrc[:, half * 8:(half + 1) * 8, :]).then_inc(S["s_cg"], 16)
                        sync.dma_start(
                            out=xqs[:, par * 2048:(par + 1) * 2048].rearrange("p (c t) -> p c t", c=4),
                            in_=xq[:, tb * 512:(tb + 1) * 512].rearrange("(c p) t -> p c t", p=128),
                        ).then_inc(S["s_cg"], 16)

                @blk.vector
                def _(vector):
                    vector.wait_ge(S["s_cw"], 64)
                    for c in range(NC):
                        vector.tensor_copy(whi[:, c * 512:(c + 1) * 512], wraw[:, c * 512:(c + 1) * 512])
                        vector.tensor_tensor(wlo[:, c * 512:(c + 1) * 512],
                                             wraw[:, c * 512:(c + 1) * 512],
                                             whi[:, c * 512:(c + 1) * 512],
                                             ALU.subtract).then_inc(S["s_wsp"], 1)
                    for tb in range(NB):
                        par = tb % 2
                        if tb >= 2:
                            vector.wait_ge(S["s_out"], 16 * (tb - 1))
                        for m in range(4):
                            gm = 4 * tb + m
                            psS = psS0 if gm % 2 == 0 else psS1
                            psG = psG0 if gm % 2 == 0 else psG1
                            vector.wait_ge(S["s_cmm"], 8 * tb + 2 * m + 1)
                            vector.tensor_copy(stile[:, :], psS[:, :])
                            vector.wait_ge(S["s_cmm"], 8 * tb + 2 * m + 2)
                            if gm >= 1:
                                vector.wait_ge(S["s_sil"], gm)
                            vector.tensor_scalar(gpre[:, :], psG[:, :], fsc[:, m:m + 1],
                                                 fbi[:, m:m + 1], ALU.mult,
                                                 ALU.add).then_inc(S["s_cev"], 1)
                            vector.wait_ge(S["s_sil"], gm + 1)
                            vector.tensor_tensor(sgt[:, :], stile[:, :], gact[:, :], ALU.mult)
                            vector.tensor_tensor(outb[:, par * 2048 + m * 512: par * 2048 + (m + 1) * 512],
                                                 sgt[:, :],
                                                 xqs[:, par * 2048 + m * 512: par * 2048 + (m + 1) * 512],
                                                 ALU.add).then_inc(S["s_cdone"], 1)

                @blk.scalar
                def _(scalar):
                    for gm in range(4 * NB):
                        scalar.wait_ge(S["s_cev"], gm + 1)
                        if gm >= 1:
                            scalar.wait_ge(S["s_cdone"], gm)
                        scalar.activation(gact[:, :], gpre[:, :], AF.Silu).then_inc(S["s_sil"], 1)

                @blk.tensor
                def _(tensor):
                    tensor.wait_ge(S["s_wsp"], NC)
                    for tb in range(NB):
                        par = tb % 2
                        tensor.wait_ge(S["s_cg"], 80 * (tb + 1))
                        ogb = ogs[:, par * 8192:(par + 1) * 8192]
                        xnb = xns[:, par * 8192:(par + 1) * 8192]
                        for m in range(4):
                            gm = 4 * tb + m
                            psS = psS0 if gm % 2 == 0 else psS1
                            psG = psG0 if gm % 2 == 0 else psG1
                            if gm >= 2:
                                tensor.wait_ge(S["s_cev"], gm - 1)
                            mm = None
                            for c in range(NC):
                                mm = tensor.matmul(psS[:, :],
                                                   wos[:, c * 512 + m * 128: c * 512 + (m + 1) * 128],
                                                   ogb[:, c * 512:(c + 1) * 512],
                                                   start=(c == 0), stop=(c == NC - 1))
                            mm.then_inc(S["s_cmm"], 1)
                            for pi, wsb in enumerate([whi, wlo]):
                                for c in range(NC):
                                    mm = tensor.matmul(psG[:, :],
                                                       wsb[:, c * 512 + m * 128: c * 512 + (m + 1) * 128],
                                                       xnb[:, c * 512:(c + 1) * 512],
                                                       start=(pi == 0 and c == 0),
                                                       stop=(pi == 1 and c == NC - 1))
                            mm.then_inc(S["s_cmm"], 1)

                @blk.gpsimd
                def _(gpsimd):
                    for tb in range(NB):
                        par = tb % 2
                        gpsimd.wait_ge(S["s_cdone"], 4 * (tb + 1))
                        gpsimd.dma_start(
                            out=out_T[:, tb * 512:(tb + 1) * 512].rearrange("(c p) t -> p c t", p=128),
                            in_=outb[:, par * 2048:(par + 1) * 2048].rearrange("p (c t) -> p c t", c=4),
                        ).then_inc(S["s_out"], 16)
                    gpsimd.wait_ge(S["s_out"], 16 * NB)

    return nc


def kernel(**inputs):
    x = inputs["L0_attn_x"]
    Wq = inputs["L0_attn_Wq"]
    bq = inputs["L0_attn_bq"]
    Wk = inputs["L0_attn_Wk"]
    bk = inputs["L0_attn_bk"]
    Wv = inputs["L0_attn_Wv"]
    bv = inputs["L0_attn_bv"]
    Wo = inputs["L0_attn_Wo"]
    wf = inputs["L0_ffn_weight"]
    fs = inputs["L0_ffn_scale_weight"]
    fb = inputs["L0_ffn_bias"]

    tt = np.arange(64, dtype=np.float32) / np.float32(64)
    inv = np.float32(1.0) / (np.float32(1.0e6) ** tt)
    ang = np.outer(np.arange(T, dtype=np.float32), inv)
    cosT = np.cos(ang).astype(np.float32).astype(bf16)
    sinT = np.sin(ang).astype(np.float32).astype(bf16)
    cosF = np.ascontiguousarray(np.concatenate([cosT.T, cosT.T], axis=0))
    sinF = np.ascontiguousarray(np.concatenate([sinT.T, sinT.T], axis=0))
    jp = np.arange(128, dtype=np.int64)[:, None]
    uu = np.arange(1024, dtype=np.int64)[None, :]
    maska = np.where(jp <= uu - 512, np.float32(0.0), np.float32(-1.0e30)).astype(bf16)
    iden = np.eye(128, dtype=np.float32).astype(bf16)
    rotM = np.zeros((128, 128), dtype=np.float32)
    for d in range(64):
        rotM[d + 64, d] = -1.0
        rotM[d, d + 64] = 1.0
    rotM = rotM.astype(bf16)
    onec = np.ones((128, 2), dtype=np.float32).astype(bf16)
    oner = np.ones((1, 128), dtype=np.float32).astype(bf16)
    onerF = np.ones((1, 128), dtype=np.float32)

    wfT = np.ascontiguousarray(wf.T)
    in_maps = []
    for c in range(8):
        b, k = c // 4, c % 4
        xTb = np.ascontiguousarray(np.asarray(x[b]).T)
        m = {
            "xT": xTb,
            "xq": np.ascontiguousarray(xTb[512 * k:512 * (k + 1), :]),
            "wq": np.ascontiguousarray(Wq[:, 512 * k:512 * (k + 1)]),
            "wk": np.ascontiguousarray(Wk[:, 128 * k:128 * (k + 1)]),
            "wv": np.ascontiguousarray(Wv[:, 128 * k:128 * (k + 1)]),
            "wo_q": np.ascontiguousarray(Wo[:, 512 * k:512 * (k + 1)]),
            "wf_q": np.ascontiguousarray(wfT[:, 512 * k:512 * (k + 1)]),
            "bq4": np.ascontiguousarray(bq[512 * k:512 * (k + 1)].reshape(4, 128).T),
            "bk1": np.ascontiguousarray(bk[128 * k:128 * (k + 1)].reshape(1, 128).T),
            "bv1": np.ascontiguousarray(bv[128 * k:128 * (k + 1)].reshape(1, 128).T),
            "fscI": np.ascontiguousarray(fs[512 * k:512 * (k + 1)].reshape(4, 128).T.astype(np.float32)),
            "fbiI": np.ascontiguousarray(fb[512 * k:512 * (k + 1)].reshape(4, 128).T.astype(np.float32)),
            "cosI": cosF, "sinI": sinF, "maskI": maska, "idenI": iden,
            "rotI": rotM, "onecI": onec, "onerI": oner, "onerFI": onerF,
        }
        in_maps.append(m)

    nc = build_nc()
    res = run_bass_kernel_spmd(nc, in_maps, core_ids=list(range(8)))

    out = np.empty((B, T, DM), dtype=np.float32)
    for c in range(8):
        b, k = c // 4, c % 4
        out[b, :, 512 * k:512 * (k + 1)] = res.results[c]["out_T"].T
    return out




# version 11
# speedup vs baseline: 1.7820x; 1.5393x over previous
import contextlib
import numpy as np
import ml_dtypes
import concourse.bass as bass
import concourse.mybir as mybir
from concourse.bass_utils import run_bass_kernel_spmd

F32 = mybir.dt.float32
BF16 = mybir.dt.bfloat16
AF = mybir.ActivationFunctionType
ALU = mybir.AluOpType
bf16 = ml_dtypes.bfloat16

B = 2
T = 8192
DM = 2048
D = 128
NB = T // 512
NC = DM // 128
SCALE = 1.0 / float(np.sqrt(D))
RG = [[0, 1, 2, 3], [4, 5, 6, 7]]

ROWS = [(0, 0), (1, 0)] + [(qb, h) for qb in range(NB) for h in range(4)]
N_SACR = 2
ROW_GROUPS = []
_g = 0
for (qb, _h) in ROWS:
    ROW_GROUPS.append(list(range(_g, _g + 2 * (qb + 1))))
    _g += 2 * (qb + 1)
NGRP = _g
XP_OF = {s: s + 1 for s in range(NGRP)}
EV_EVAC_DONE = {}
EV_TS_DONE = {}
EV_DUMMY_DONE = {}
_ev = 0
for _r in range(len(ROWS)):
    if _r >= 1:
        _ev += 4
        EV_EVAC_DONE[_r - 1] = _ev
    _ev += 4
    EV_TS_DONE[_r] = _ev
    _ev += 1
    EV_DUMMY_DONE[_r] = _ev
_ev += 4
EV_EVAC_DONE[len(ROWS) - 1] = _ev
EV_TOTAL = _ev
LOC_OF = {}
ROW_OF = {}
for _ri in range(len(ROWS)):
    for _j, _s in enumerate(ROW_GROUPS[_ri]):
        LOC_OF[_s] = _j
        ROW_OF[_s] = _ri


def build_nc():
    nc = bass.Bass("TRN2", target_bir_lowering=False)

    xT = nc.dram_tensor("xT", [DM, T], BF16, kind="ExternalInput").ap()
    xq = nc.dram_tensor("xq", [512, T], BF16, kind="ExternalInput").ap()
    wq = nc.dram_tensor("wq", [DM, 512], BF16, kind="ExternalInput").ap()
    wk = nc.dram_tensor("wk", [DM, 128], BF16, kind="ExternalInput").ap()
    wv = nc.dram_tensor("wv", [DM, 128], BF16, kind="ExternalInput").ap()
    wo_q = nc.dram_tensor("wo_q", [DM, 512], BF16, kind="ExternalInput").ap()
    wf_q = nc.dram_tensor("wf_q", [DM, 512], F32, kind="ExternalInput").ap()
    bq4 = nc.dram_tensor("bq4", [128, 4], BF16, kind="ExternalInput").ap()
    bk1 = nc.dram_tensor("bk1", [128, 1], BF16, kind="ExternalInput").ap()
    bv1 = nc.dram_tensor("bv1", [128, 1], BF16, kind="ExternalInput").ap()
    fscI = nc.dram_tensor("fscI", [128, 4], F32, kind="ExternalInput").ap()
    fbiI = nc.dram_tensor("fbiI", [128, 4], F32, kind="ExternalInput").ap()
    cosI = nc.dram_tensor("cosI", [128, T], BF16, kind="ExternalInput").ap()
    sinI = nc.dram_tensor("sinI", [128, T], BF16, kind="ExternalInput").ap()
    maskI = nc.dram_tensor("maskI", [128, 1024], BF16, kind="ExternalInput").ap()
    idenI = nc.dram_tensor("idenI", [128, 128], BF16, kind="ExternalInput").ap()
    rotI = nc.dram_tensor("rotI", [128, 128], BF16, kind="ExternalInput").ap()
    onecI = nc.dram_tensor("onecI", [128, 2], BF16, kind="ExternalInput").ap()
    onerI = nc.dram_tensor("onerI", [1, 128], BF16, kind="ExternalInput").ap()
    onerFI = nc.dram_tensor("onerFI", [1, 128], F32, kind="ExternalInput").ap()

    out_T = nc.dram_tensor("out_T", [512, T], F32, kind="ExternalOutput").ap()

    O_scr = nc.dram_tensor("O_scr", [512, T], BF16, kind="Internal")
    OG = nc.dram_tensor("OG", [DM, T], BF16, kind="Internal")
    xn_scr = nc.dram_tensor("xn_scr", [DM, T], BF16, kind="Internal").ap()

    st = contextlib.ExitStack()
    E = st.enter_context

    iden = E(nc.sbuf_tensor("iden", [128, 128], BF16))
    rotM = E(nc.sbuf_tensor("rotM", [128, 128], BF16))
    maska = E(nc.sbuf_tensor("maska", [128, 1024], BF16))
    onec = E(nc.sbuf_tensor("onec", [128, 2], BF16))
    oner = E(nc.sbuf_tensor("oner", [1, 128], BF16))
    onerF = E(nc.sbuf_tensor("onerF", [1, 128], F32))
    bqb = E(nc.sbuf_tensor("bqb", [128, 4], BF16))
    bkb = E(nc.sbuf_tensor("bkb", [128, 1], BF16))
    bvb = E(nc.sbuf_tensor("bvb", [128, 1], BF16))
    bq_s = E(nc.sbuf_tensor("bq_s", [128, 4], F32))
    bk_s = E(nc.sbuf_tensor("bk_s", [128, 1], F32))
    bv_s = E(nc.sbuf_tensor("bv_s", [128, 1], F32))
    fsc = E(nc.sbuf_tensor("fsc", [128, 4], F32))
    fbi = E(nc.sbuf_tensor("fbi", [128, 4], F32))

    names = ["s_ld", "s_ldt", "s_ldc", "s_sqf", "s_sqb", "s_sqmm", "s_stsum",
             "s_rw1", "s_rw2", "s_row", "s_bcmm", "s_bc", "s_nrm", "s_prj", "s_evA",
             "s_rotmm", "s_rp", "s_vtb", "s_vte", "s_sp",
             "s_mm", "s_xp", "s_pv", "s_ev", "s_tr", "s_sto", "s_cc",
             "s_cw", "s_wsp", "s_cg", "s_cmm", "s_cev", "s_sil", "s_cdone", "s_out"]
    S = {n: E(nc.semaphore(n)) for n in names}

    with st:
        sAB = contextlib.ExitStack()
        q_s = sAB.enter_context(nc.sbuf_tensor("q_s", [128, 4 * T], BF16))
        k_s = sAB.enter_context(nc.sbuf_tensor("k_s", [128, T], BF16))
        v_s = sAB.enter_context(nc.sbuf_tensor("v_s", [128, (T // 128) * 129], BF16))
        with contextlib.ExitStack() as sA:
            EA = sA.enter_context
            wq_s = EA(nc.sbuf_tensor("wq_s", [128, NC * 512], BF16))
            wk_s = EA(nc.sbuf_tensor("wk_s", [128, NC * 128], BF16))
            wv_s = EA(nc.sbuf_tensor("wv_s", [128, NC * 128], BF16))
            xT_s = EA(nc.sbuf_tensor("xT_s", [128, 2 * NC * 512], BF16))
            xsq = EA(nc.sbuf_tensor("xsq", [128, 4 * 512], BF16))
            xn_s = EA(nc.sbuf_tensor("xn_s", [128, NC * 512], BF16))
            cs_s = EA(nc.sbuf_tensor("cs_s", [128, 2 * 512], BF16))
            sn_s = EA(nc.sbuf_tensor("sn_s", [128, 2 * 512], BF16))
            rowm = EA(nc.sbuf_tensor("rowm", [1, 512], F32))
            rowv = EA(nc.sbuf_tensor("rowv", [1, 512], F32))
            rowt = EA(nc.sbuf_tensor("rowt", [1, 512], F32))
            rowr = EA(nc.sbuf_tensor("rowr", [1, 512], F32))
            rowbf = EA(nc.sbuf_tensor("rowbf", [1, 1024], BF16))
            Rb = EA(nc.sbuf_tensor("Rb", [128, 512], F32))
            Mb = EA(nc.sbuf_tensor("Mb", [128, 512], BF16))
            tmpn = EA(nc.sbuf_tensor("tmpn", [128, 512], BF16))
            qraw = EA(nc.sbuf_tensor("qraw", [128, 2 * 512], BF16))
            t1b = EA(nc.sbuf_tensor("t1b", [128, 512], BF16))
            t2b = EA(nc.sbuf_tensor("t2b", [128, 512], BF16))
            vE = EA(nc.sbuf_tensor("vE", [128, 512], BF16))
            ps_sum = EA(nc.psum_tensor("ps_sum", [128, 512], F32))
            ps_sq = EA(nc.psum_tensor("ps_sq", [128, 512], F32))
            ps_bc = EA(nc.psum_tensor("ps_bc", [128, 1024], F32))
            ps_pA = EA(nc.psum_tensor("ps_pA", [128, 512], F32))
            ps_pB = EA(nc.psum_tensor("ps_pB", [128, 512], F32))
            ps_rot = EA(nc.psum_tensor("ps_rot", [128, 512], F32))
            ps_vt = EA(nc.psum_tensor("ps_vt", [128, 4 * 128], BF16))

            with nc.Block("A") as blk:

                @blk.sync
                def _(sync):
                    for dst, src in [
                        (iden[:, :], idenI), (rotM[:, :], rotI), (maska[:, :], maskI),
                        (onec[:, :], onecI), (oner[:, :], onerI), (onerF[:, :], onerFI),
                        (bqb[:, :], bq4), (bkb[:, :], bk1), (bvb[:, :], bv1),
                        (fsc[:, :], fscI), (fbi[:, :], fbiI),
                    ]:
                        sync.dma_start(out=dst, in_=src).then_inc(S["s_ldc"], 16)
                    sync.dma_start(out=wq_s[:, :].rearrange("p (c m) -> p c m", c=NC),
                                   in_=wq.rearrange("(c p) m -> p c m", p=128)).then_inc(S["s_ldc"], 16)
                    sync.dma_start(out=wk_s[:, :].rearrange("p (c m) -> p c m", c=NC),
                                   in_=wk.rearrange("(c p) m -> p c m", p=128)).then_inc(S["s_ldc"], 16)
                    sync.dma_start(out=wv_s[:, :].rearrange("p (c m) -> p c m", c=NC),
                                   in_=wv.rearrange("(c p) m -> p c m", p=128)).then_inc(S["s_ldc"], 16)
                    for tb in range(NB):
                        buf = tb % 2
                        if tb >= 2:
                            sync.wait_ge(S["s_sqb"], tb - 1)
                            sync.wait_ge(S["s_stsum"], tb - 1)
                            sync.wait_ge(S["s_nrm"], NC * (tb - 1))
                        src_b = xT[:, tb * 512:(tb + 1) * 512].rearrange("(c p) t -> p c t", p=128)
                        dst_b = xT_s[:, buf * 8192:(buf + 1) * 8192].rearrange("p (c t) -> p c t", c=NC)
                        for pt in range(4):
                            sync.dma_start(out=dst_b[:, pt * 4:(pt + 1) * 4, :],
                                           in_=src_b[:, pt * 4:(pt + 1) * 4, :]).then_inc(S["s_ld"], 16)
                        if tb >= 2:
                            sync.wait_ge(S["s_rp"], 5 * (tb - 1))
                        sync.dma_start(out=cs_s[:, buf * 512:(buf + 1) * 512],
                                       in_=cosI[:, tb * 512:(tb + 1) * 512]).then_inc(S["s_ldt"], 16)
                        sync.dma_start(out=sn_s[:, buf * 512:(buf + 1) * 512],
                                       in_=sinI[:, tb * 512:(tb + 1) * 512]).then_inc(S["s_ldt"], 16)

                @blk.gpsimd
                def _(gpsimd):
                    for tb in range(NB):
                        gpsimd.wait_ge(S["s_nrm"], NC * (tb + 1))
                        dst = xn_scr[:, tb * 512:(tb + 1) * 512].rearrange("(c p) t -> p c t", p=128)
                        src = xn_s[:, :].rearrange("p (c t) -> p c t", c=NC)
                        gpsimd.dma_start(out=dst[:, 0:8, :], in_=src[:, 0:8, :]).then_inc(S["s_sp"], 16)
                        gpsimd.dma_start(out=dst[:, 8:16, :], in_=src[:, 8:16, :]).then_inc(S["s_sp"], 16)

                @blk.scalar
                def _(scalar):
                    scalar.wait_ge(S["s_ldc"], 16 * 14)
                    for tb in range(NB):
                        buf = tb % 2
                        xb = xT_s[:, buf * 8192:(buf + 1) * 8192]
                        scalar.wait_ge(S["s_ld"], 64 * (tb + 1))
                        for c in range(NC):
                            gi = tb * NC + c
                            if gi >= 4:
                                scalar.wait_ge(S["s_sqmm"], gi - 3)
                            scalar.activation(xsq[:, (gi % 4) * 512:(gi % 4 + 1) * 512],
                                              xb[:, c * 512:(c + 1) * 512],
                                              AF.Square).then_inc(S["s_sqf"], 1)
                        scalar.drain().then_inc(S["s_sqb"], 1)
                        scalar.wait_ge(S["s_rw1"], tb + 1)
                        scalar.activation(rowt[0:1, :], rowv[0:1, :], AF.Ln)
                        scalar.activation(rowt[0:1, :], rowt[0:1, :], AF.Exp, scale=0.5)
                        scalar.drain().then_inc(S["s_rw2"], 1)
                        evs = [("K", qraw[:, 0:512], bk_s[0:128, 0:1]),
                               ("V", vE[:, :], bv_s[0:128, 0:1]),
                               ("Q0", qraw[:, 512:1024], bq_s[0:128, 0:1]),
                               ("Q1", qraw[:, 0:512], bq_s[0:128, 1:2]),
                               ("Q2", qraw[:, 512:1024], bq_s[0:128, 2:3]),
                               ("Q3", qraw[:, 0:512], bq_s[0:128, 3:4])]
                        for gi, (nm, dst, bias_ap) in enumerate(evs):
                            scalar.wait_ge(S["s_prj"], 6 * tb + gi + 1)
                            if nm == "K" and tb >= 1:
                                scalar.wait_ge(S["s_rp"], 5 * tb)
                                scalar.wait_ge(S["s_rotmm"], 5 * tb)
                            if nm == "V" and tb >= 1:
                                scalar.wait_ge(S["s_vtb"], tb)
                            if nm in ("Q1", "Q2", "Q3"):
                                k_ = {"Q1": 1, "Q2": 2, "Q3": 3}[nm]
                                scalar.wait_ge(S["s_rp"], 5 * tb + k_)
                                scalar.wait_ge(S["s_rotmm"], 5 * tb + k_)
                            psum = ps_pA if gi % 2 == 0 else ps_pB
                            scalar.activation(dst, psum[:, :], AF.Identity,
                                              bias=bias_ap).then_inc(S["s_evA"], 1)

                @blk.vector
                def _(vector):
                    vector.wait_ge(S["s_ldc"], 16 * 14)
                    vector.memset(v_s[:, :].rearrange("p (n f) -> p n f", f=129)[:, :, 128:129], 1.0)
                    vector.tensor_copy(bq_s[:, :], bqb[:, :])
                    vector.tensor_copy(bk_s[:, :], bkb[:, :])
                    vector.tensor_copy(bv_s[:, :], bvb[:, :])
                    for tb in range(NB):
                        buf = tb % 2
                        xb = xT_s[:, buf * 8192:(buf + 1) * 8192]
                        csb = cs_s[:, buf * 512:(buf + 1) * 512]
                        snb = sn_s[:, buf * 512:(buf + 1) * 512]
                        vector.wait_ge(S["s_stsum"], tb + 1)
                        vector.wait_ge(S["s_sqmm"], NC * (tb + 1))
                        vector.tensor_scalar(rowm[0:1, :], ps_sum[0:1, :], 1.0 / DM, None, ALU.mult)
                        vector.tensor_copy(rowbf[0:1, 0:512], rowm[0:1, :])
                        vector.tensor_scalar(rowv[0:1, :], ps_sq[0:1, :], 1.0 / DM, None, ALU.mult)
                        vector.tensor_tensor(rowt[0:1, :], rowm[0:1, :], rowm[0:1, :], ALU.mult)
                        vector.tensor_tensor(rowv[0:1, :], rowv[0:1, :], rowt[0:1, :], ALU.subtract)
                        vector.tensor_copy(rowbf[0:1, 512:1024], rowv[0:1, :])
                        vector.tensor_copy(rowv[0:1, :], rowbf[0:1, 512:1024])
                        vector.tensor_scalar(rowv[0:1, :], rowv[0:1, :], 1.0e-5, None, ALU.add)
                        vector.drain().then_inc(S["s_rw1"], 1)
                        vector.wait_ge(S["s_rw2"], tb + 1)
                        vector.tensor_copy(rowbf[0:1, 512:1024], rowt[0:1, :])
                        vector.reciprocal(rowr[0:1, :], rowbf[0:1, 512:1024])
                        vector.drain().then_inc(S["s_row"], 1)
                        vector.wait_ge(S["s_bcmm"], tb + 1)
                        vector.tensor_copy(Rb[:, :], ps_bc[:, 0:512])
                        vector.tensor_copy(Mb[:, :], ps_bc[:, 512:1024])
                        vector.drain().then_inc(S["s_bc"], 1)
                        if tb >= 1:
                            vector.wait_ge(S["s_prj"], 6 * tb)
                            vector.wait_ge(S["s_sp"], 32 * tb)
                        for c in range(NC):
                            vector.tensor_tensor(tmpn[:, :], xb[:, c * 512:(c + 1) * 512], Mb[:, :], ALU.subtract)
                            vector.tensor_tensor(xn_s[:, c * 512:(c + 1) * 512], tmpn[:, :], Rb[:, :],
                                                 ALU.mult).then_inc(S["s_nrm"], 1)
                        vector.wait_ge(S["s_ldt"], 32 * (tb + 1))
                        ropes = [(qraw[:, 0:512], k_s[:, tb * 512:(tb + 1) * 512])] + [
                            (qraw[:, (512 if h % 2 == 0 else 0):(1024 if h % 2 == 0 else 512)],
                             q_s[:, h * T + tb * 512: h * T + (tb + 1) * 512]) for h in range(4)]
                        for ri_, (srcq, dsta) in enumerate(ropes):
                            vector.wait_ge(S["s_rotmm"], 5 * tb + ri_ + 1)
                            vector.tensor_tensor(t1b[:, :], srcq, csb, ALU.mult)
                            vector.tensor_tensor(t2b[:, :], ps_rot[:, :], snb, ALU.mult)
                            vector.tensor_tensor(dsta, t1b[:, :], t2b[:, :], ALU.add)
                            vector.drain().then_inc(S["s_rp"], 1)
                        vector.wait_ge(S["s_vtb"], tb + 1)
                        for sub in range(4):
                            tile = tb * 4 + sub
                            vector.tensor_copy(v_s[:, tile * 129: tile * 129 + 128],
                                               ps_vt[:, sub * 128:(sub + 1) * 128])
                        vector.drain().then_inc(S["s_vte"], 1)

                @blk.tensor
                def _(tensor):
                    tensor.wait_ge(S["s_ldc"], 16 * 14)
                    for tb in range(NB):
                        buf = tb % 2
                        xb = xT_s[:, buf * 8192:(buf + 1) * 8192]
                        tensor.wait_ge(S["s_ld"], 64 * (tb + 1))
                        if tb >= 1:
                            tensor.wait_ge(S["s_rw1"], tb)
                        mm = None
                        for c in range(NC):
                            mm = tensor.matmul(ps_sum[0:1, :], onec[:, 0:1],
                                               xb[:, c * 512:(c + 1) * 512],
                                               start=(c == 0), stop=(c == NC - 1))
                        mm.then_inc(S["s_stsum"], 1)
                        for c in range(NC):
                            gi = tb * NC + c
                            tensor.wait_ge(S["s_sqf"], gi + 1)
                            mm = tensor.matmul(ps_sq[0:1, :], onec[:, 0:1],
                                               xsq[:, (gi % 4) * 512:(gi % 4 + 1) * 512],
                                               start=(c == 0), stop=(c == NC - 1))
                            mm.then_inc(S["s_sqmm"], 1)
                        tensor.wait_ge(S["s_row"], tb + 1)
                        if tb >= 1:
                            tensor.wait_ge(S["s_bc"], tb)
                        tensor.matmul(ps_bc[:, 0:512], onerF[0:1, :], rowr[0:1, :], start=True, stop=True)
                        mm = tensor.matmul(ps_bc[:, 512:1024], oner[0:1, :], rowbf[0:1, 0:512], start=True, stop=True)
                        mm.then_inc(S["s_bcmm"], 1)
                        projs = [("K", wk_s, 0), ("V", wv_s, 0), ("Q0", wq_s, 0),
                                 ("Q1", wq_s, 1), ("Q2", wq_s, 2), ("Q3", wq_s, 3)]
                        for gi, (nm, wsb, mi) in enumerate(projs):
                            psum = ps_pA if gi % 2 == 0 else ps_pB
                            need = 6 * tb + gi - 1
                            if need >= 1:
                                tensor.wait_ge(S["s_evA"], need)
                            for c in range(NC):
                                tensor.wait_ge(S["s_nrm"], NC * tb + c + 1)
                                if nm in ("K", "V"):
                                    lhs = wsb[:, c * 128:(c + 1) * 128]
                                else:
                                    lhs = wsb[:, c * 512 + mi * 128: c * 512 + (mi + 1) * 128]
                                mm = tensor.matmul(psum[:, :], lhs, xn_s[:, c * 512:(c + 1) * 512],
                                                   start=(c == 0), stop=(c == NC - 1))
                            mm.then_inc(S["s_prj"], 1)
                            if nm == "Q0":
                                tensor.wait_ge(S["s_evA"], 6 * tb + 3)
                                if tb >= 1:
                                    tensor.wait_ge(S["s_vte"], tb)
                                for sub in range(4):
                                    mm2 = tensor.transpose(ps_vt[:, sub * 128:(sub + 1) * 128],
                                                           vE[:, sub * 128:(sub + 1) * 128],
                                                           iden[:, :])
                                mm2.then_inc(S["s_vtb"], 1)
                            if nm != "V":
                                rot_i = {"K": 0, "Q0": 1, "Q1": 2, "Q2": 3, "Q3": 4}[nm]
                                tensor.wait_ge(S["s_evA"], 6 * tb + gi + 1)
                                if 5 * tb + rot_i >= 1:
                                    tensor.wait_ge(S["s_rp"], 5 * tb + rot_i)
                                slot = qraw[:, 0:512] if nm in ("K", "Q1", "Q3") else qraw[:, 512:1024]
                                mm3 = tensor.matmul(ps_rot[:, :], rotM[:, :], slot, start=True, stop=True)
                                mm3.then_inc(S["s_rotmm"], 1)

        wos = st.enter_context(nc.sbuf_tensor("wos", [128, NC * 512], BF16, side="right"))
        wraw = st.enter_context(nc.sbuf_tensor("wraw", [128, NC * 512], F32, side="right"))
        whi = st.enter_context(nc.sbuf_tensor("whi", [128, NC * 512], BF16, side="right"))
        wlo = st.enter_context(nc.sbuf_tensor("wlo", [128, NC * 512], BF16, side="right"))
        with contextlib.ExitStack() as sB:
            EB = sB.enter_context
            pT = EB(nc.sbuf_tensor("pT", [128, 4 * 1024], BF16))
            on_s = EB(nc.sbuf_tensor("on_s", [128, 4 * 128], BF16))
            rden = EB(nc.sbuf_tensor("rden", [128, 8], F32))
            oTb = EB(nc.sbuf_tensor("oTb", [128, 2 * 512], BF16))
            xdum = EB(nc.sbuf_tensor("xdum", [1, 2], BF16))
            sS0 = EB(nc.psum_tensor("sS0", [128, 1024], F32))
            sS1 = EB(nc.psum_tensor("sS1", [128, 1024], F32))
            ob01 = EB(nc.psum_tensor("ob01", [128, 260], F32))
            ob23 = EB(nc.psum_tensor("ob23", [128, 260], F32))
            ptr = EB(nc.psum_tensor("ptr", [128, 512], BF16))

            def ob_seg(sub):
                tt = ob01 if sub < 2 else ob23
                return tt[:, (sub % 2) * 129:(sub % 2 + 1) * 129]

            with nc.Block("Bb") as blk:

                @blk.tensor
                def _(tensor):
                    tensor.wait_ge(S["s_rp"], 5 * NB)
                    tensor.wait_ge(S["s_vte"], NB)

                    def emit_dummies(ri):
                        if ri >= 1:
                            tensor.wait_ge(S["s_ev"], EV_TS_DONE[ri - 1])
                        tensor.matmul(ob01[0:128, 258:260], iden[:, :], onec[:, 0:2],
                                      start=True, stop=True, skip_group_check=True)
                        tensor.matmul(ob23[0:128, 258:260], iden[:, :], onec[:, 0:2],
                                      start=True, stop=True, skip_group_check=True)

                    def emit_qk(s_id):
                        ri = ROW_OF[s_id]
                        qb, h = ROWS[ri]
                        j = LOC_OF[s_id]
                        q_ap = q_s[:, h * T + qb * 512: h * T + (qb + 1) * 512]
                        ps = sS0 if s_id % 2 == 0 else sS1
                        if s_id >= 2:
                            tensor.wait_ge(S["s_xp"], XP_OF[s_id - 2])
                        mm = None
                        for ti in range(2):
                            kt = 2 * j + ti
                            is_diag = (kt >= 4 * qb)
                            mm = tensor.matmul(ps[:, ti * 512:(ti + 1) * 512],
                                               k_s[:, kt * 128:(kt + 1) * 128], q_ap,
                                               start=True, stop=not is_diag)
                            if is_diag:
                                off = 512 - 128 * (kt - 4 * qb)
                                mm = tensor.matmul(ps[:, ti * 512:(ti + 1) * 512],
                                                   iden[:, :], maska[:, off:off + 512],
                                                   start=False, stop=True)
                        mm.then_inc(S["s_mm"], 1)

                    def emit_pv(s_id):
                        ri = ROW_OF[s_id]
                        if LOC_OF[s_id] == 0:
                            emit_dummies(ri)
                        tensor.wait_ge(S["s_xp"], XP_OF[s_id] + 2)
                        slot = s_id % 4
                        j = LOC_OF[s_id]
                        last = (s_id == ROW_GROUPS[ri][-1])
                        mm = None
                        for ti in range(2):
                            kt = 2 * j + ti
                            for sub in range(4):
                                mm = tensor.matmul(
                                    ob_seg(sub),
                                    pT[:, slot * 1024 + ti * 512 + sub * 128:
                                       slot * 1024 + ti * 512 + (sub + 1) * 128],
                                    v_s[:, kt * 129: kt * 129 + 129],
                                    start=False, stop=(last and ti == 1),
                                    skip_group_check=True)
                        mm.then_inc(S["s_pv"], 1)
                        if last:
                            tensor.wait_ge(S["s_ev"], EV_DUMMY_DONE[ri])
                            for si in range(4):
                                mmt = tensor.transpose(ptr[:, si * 128:(si + 1) * 128],
                                                       on_s[:, si * 128:(si + 1) * 128], iden[:, :])
                            mmt.then_inc(S["s_tr"], 1)

                    for s_id in range(NGRP):
                        emit_qk(s_id)
                        if s_id >= 3:
                            emit_pv(s_id - 3)
                    emit_pv(NGRP - 3)
                    emit_pv(NGRP - 2)
                    emit_pv(NGRP - 1)

                @blk.scalar
                def _(scalar):
                    for s_id in range(NGRP):
                        scalar.wait_ge(S["s_mm"], s_id + 1)
                        if s_id >= 4:
                            scalar.wait_ge(S["s_pv"], s_id - 3)
                        ps = sS0 if s_id % 2 == 0 else sS1
                        slot = s_id % 4
                        scalar.activation(pT[:, slot * 1024:(slot + 1) * 1024],
                                          ps[:, :], AF.Exp, scale=SCALE).then_inc(S["s_xp"], 1)
                    scalar.activation(xdum[0:1, :], xdum[0:1, :], AF.Exp).then_inc(S["s_xp"], 1)
                    scalar.activation(xdum[0:1, :], xdum[0:1, :], AF.Exp).then_inc(S["s_xp"], 1)

                @blk.vector
                def _(vector):
                    vector.memset(xdum[0:1, :], 0.0)

                    def evacs(r):
                        par = r % 2
                        vector.wait_ge(S["s_tr"], r + 1)
                        if r >= 2:
                            vector.wait_ge(S["s_sto"], 16 * (r - 1))
                        for si in range(4):
                            vector.tensor_copy(oTb[:, par * 512 + si * 128: par * 512 + (si + 1) * 128],
                                               ptr[:, si * 128:(si + 1) * 128]).then_inc(S["s_ev"], 1)

                    for ri in range(len(ROWS)):
                        vector.wait_ge(S["s_pv"], ROW_GROUPS[ri][-1] + 1)
                        if ri >= 1:
                            evacs(ri - 1)
                        for sub in range(4):
                            vector.reciprocal(rden[:, sub:sub + 1], ob_seg(sub)[:, 128:129])
                        for sub in range(4):
                            vector.tensor_scalar(on_s[:, sub * 128:(sub + 1) * 128], ob_seg(sub)[:, 0:128],
                                                 rden[:, sub:sub + 1], None,
                                                 ALU.mult).then_inc(S["s_ev"], 1)
                        vector.memset(rden[:, 4:5], 0.0)
                        vector.drain().then_inc(S["s_ev"], 1)
                    evacs(len(ROWS) - 1)

                @blk.gpsimd
                def _(gpsimd):
                    for ri, (qb, h) in enumerate(ROWS):
                        par = ri % 2
                        gpsimd.wait_ge(S["s_ev"], EV_EVAC_DONE[ri])
                        gpsimd.dma_start(out=O_scr.ap()[h * 128:(h + 1) * 128, qb * 512:(qb + 1) * 512],
                                         in_=oTb[:, par * 512:(par + 1) * 512]).then_inc(S["s_sto"], 16)
                    gpsimd.wait_ge(S["s_sto"], 16 * len(ROWS))
                    gpsimd.collective_compute(
                        "AllGather", ALU.bypass, replica_groups=RG,
                        ins=[O_scr.ap().opt()], outs=[OG.ap().opt()]).then_inc(S["s_cc"], 1)

        sAB.close()
        with contextlib.ExitStack() as sC:
            EC = sC.enter_context
            ogs = EC(nc.sbuf_tensor("ogs", [128, 2 * NC * 512], BF16))
            xns = EC(nc.sbuf_tensor("xns", [128, 2 * NC * 512], BF16))
            xqs = EC(nc.sbuf_tensor("xqs", [128, 2 * 4 * 512], BF16))
            stile = EC(nc.sbuf_tensor("stile", [128, 512], BF16))
            gpre = EC(nc.sbuf_tensor("gpre", [128, 512], F32))
            gact = EC(nc.sbuf_tensor("gact", [128, 512], F32))
            sgt = EC(nc.sbuf_tensor("sgt", [128, 512], F32))
            outb = EC(nc.sbuf_tensor("outb", [128, 2 * 4 * 512], F32))
            psS0 = EC(nc.psum_tensor("psS0", [128, 512], F32))
            psS1 = EC(nc.psum_tensor("psS1", [128, 512], F32))
            psG0 = EC(nc.psum_tensor("psG0", [128, 512], F32))
            psG1 = EC(nc.psum_tensor("psG1", [128, 512], F32))

            with nc.Block("Cc") as blk:

                @blk.sync
                def _(sync):
                    for half in range(2):
                        sync.dma_start(
                            out=wos[:, :].rearrange("p (c m) -> p c m", c=NC)[:, half * 8:(half + 1) * 8, :],
                            in_=wo_q.rearrange("(c p) m -> p c m", p=128)[:, half * 8:(half + 1) * 8, :],
                        ).then_inc(S["s_cw"], 16)
                        sync.dma_start(
                            out=wraw[:, :].rearrange("p (c m) -> p c m", c=NC)[:, half * 8:(half + 1) * 8, :],
                            in_=wf_q.rearrange("(c p) m -> p c m", p=128)[:, half * 8:(half + 1) * 8, :],
                        ).then_inc(S["s_cw"], 16)
                    sync.wait_ge(S["s_cc"], 1)
                    for tb in range(NB):
                        par = tb % 2
                        if tb >= 2:
                            sync.wait_ge(S["s_cmm"], 8 * (tb - 1))
                            sync.wait_ge(S["s_cdone"], 4 * (tb - 1))
                        ogd = ogs[:, par * 8192:(par + 1) * 8192].rearrange("p (c t) -> p c t", c=NC)
                        ogsrc = OG.ap()[:, tb * 512:(tb + 1) * 512].rearrange("(c p) t -> p c t", p=128)
                        xnd = xns[:, par * 8192:(par + 1) * 8192].rearrange("p (c t) -> p c t", c=NC)
                        xnsrc = xn_scr[:, tb * 512:(tb + 1) * 512].rearrange("(c p) t -> p c t", p=128)
                        for half in range(2):
                            sync.dma_start(out=ogd[:, half * 8:(half + 1) * 8, :],
                                           in_=ogsrc[:, half * 8:(half + 1) * 8, :]).then_inc(S["s_cg"], 16)
                            sync.dma_start(out=xnd[:, half * 8:(half + 1) * 8, :],
                                           in_=xnsrc[:, half * 8:(half + 1) * 8, :]).then_inc(S["s_cg"], 16)
                        sync.dma_start(
                            out=xqs[:, par * 2048:(par + 1) * 2048].rearrange("p (c t) -> p c t", c=4),
                            in_=xq[:, tb * 512:(tb + 1) * 512].rearrange("(c p) t -> p c t", p=128),
                        ).then_inc(S["s_cg"], 16)

                @blk.vector
                def _(vector):
                    vector.wait_ge(S["s_cw"], 64)
                    for c in range(NC):
                        vector.tensor_copy(whi[:, c * 512:(c + 1) * 512], wraw[:, c * 512:(c + 1) * 512])
                        vector.tensor_tensor(wlo[:, c * 512:(c + 1) * 512],
                                             wraw[:, c * 512:(c + 1) * 512],
                                             whi[:, c * 512:(c + 1) * 512],
                                             ALU.subtract).then_inc(S["s_wsp"], 1)
                    for tb in range(NB):
                        par = tb % 2
                        if tb >= 2:
                            vector.wait_ge(S["s_out"], 16 * (tb - 1))
                        for m in range(4):
                            gm = 4 * tb + m
                            psS = psS0 if gm % 2 == 0 else psS1
                            psG = psG0 if gm % 2 == 0 else psG1
                            vector.wait_ge(S["s_cmm"], 8 * tb + 2 * m + 1)
                            vector.tensor_copy(stile[:, :], psS[:, :])
                            vector.wait_ge(S["s_cmm"], 8 * tb + 2 * m + 2)
                            if gm >= 1:
                                vector.wait_ge(S["s_sil"], gm)
                            vector.tensor_scalar(gpre[:, :], psG[:, :], fsc[:, m:m + 1],
                                                 fbi[:, m:m + 1], ALU.mult,
                                                 ALU.add).then_inc(S["s_cev"], 1)
                            vector.wait_ge(S["s_sil"], gm + 1)
                            vector.tensor_tensor(sgt[:, :], stile[:, :], gact[:, :], ALU.mult)
                            vector.tensor_tensor(outb[:, par * 2048 + m * 512: par * 2048 + (m + 1) * 512],
                                                 sgt[:, :],
                                                 xqs[:, par * 2048 + m * 512: par * 2048 + (m + 1) * 512],
                                                 ALU.add).then_inc(S["s_cdone"], 1)

                @blk.scalar
                def _(scalar):
                    for gm in range(4 * NB):
                        scalar.wait_ge(S["s_cev"], gm + 1)
                        if gm >= 1:
                            scalar.wait_ge(S["s_cdone"], gm)
                        scalar.activation(gact[:, :], gpre[:, :], AF.Silu).then_inc(S["s_sil"], 1)

                @blk.tensor
                def _(tensor):
                    tensor.wait_ge(S["s_wsp"], NC)
                    for tb in range(NB):
                        par = tb % 2
                        tensor.wait_ge(S["s_cg"], 80 * (tb + 1))
                        ogb = ogs[:, par * 8192:(par + 1) * 8192]
                        xnb = xns[:, par * 8192:(par + 1) * 8192]
                        for m in range(4):
                            gm = 4 * tb + m
                            psS = psS0 if gm % 2 == 0 else psS1
                            psG = psG0 if gm % 2 == 0 else psG1
                            if gm >= 2:
                                tensor.wait_ge(S["s_cev"], gm - 1)
                            mm = None
                            for c in range(NC):
                                mm = tensor.matmul(psS[:, :],
                                                   wos[:, c * 512 + m * 128: c * 512 + (m + 1) * 128],
                                                   ogb[:, c * 512:(c + 1) * 512],
                                                   start=(c == 0), stop=(c == NC - 1))
                            mm.then_inc(S["s_cmm"], 1)
                            for pi, wsb in enumerate([whi, wlo]):
                                for c in range(NC):
                                    mm = tensor.matmul(psG[:, :],
                                                       wsb[:, c * 512 + m * 128: c * 512 + (m + 1) * 128],
                                                       xnb[:, c * 512:(c + 1) * 512],
                                                       start=(pi == 0 and c == 0),
                                                       stop=(pi == 1 and c == NC - 1))
                            mm.then_inc(S["s_cmm"], 1)

                @blk.gpsimd
                def _(gpsimd):
                    for tb in range(NB):
                        par = tb % 2
                        gpsimd.wait_ge(S["s_cdone"], 4 * (tb + 1))
                        gpsimd.dma_start(
                            out=out_T[:, tb * 512:(tb + 1) * 512].rearrange("(c p) t -> p c t", p=128),
                            in_=outb[:, par * 2048:(par + 1) * 2048].rearrange("p (c t) -> p c t", c=4),
                        ).then_inc(S["s_out"], 16)
                    gpsimd.wait_ge(S["s_out"], 16 * NB)

    return nc


def kernel(**inputs):
    x = inputs["L0_attn_x"]
    Wq = inputs["L0_attn_Wq"]
    bq = inputs["L0_attn_bq"]
    Wk = inputs["L0_attn_Wk"]
    bk = inputs["L0_attn_bk"]
    Wv = inputs["L0_attn_Wv"]
    bv = inputs["L0_attn_bv"]
    Wo = inputs["L0_attn_Wo"]
    wf = inputs["L0_ffn_weight"]
    fs = inputs["L0_ffn_scale_weight"]
    fb = inputs["L0_ffn_bias"]

    tt = np.arange(64, dtype=np.float32) / np.float32(64)
    inv = np.float32(1.0) / (np.float32(1.0e6) ** tt)
    ang = np.outer(np.arange(T, dtype=np.float32), inv)
    cosT = np.cos(ang).astype(np.float32).astype(bf16)
    sinT = np.sin(ang).astype(np.float32).astype(bf16)
    cosF = np.ascontiguousarray(np.concatenate([cosT.T, cosT.T], axis=0))
    sinF = np.ascontiguousarray(np.concatenate([sinT.T, sinT.T], axis=0))
    jp = np.arange(128, dtype=np.int64)[:, None]
    uu = np.arange(1024, dtype=np.int64)[None, :]
    maska = np.where(jp <= uu - 512, np.float32(0.0), np.float32(-1.0e30)).astype(bf16)
    iden = np.eye(128, dtype=np.float32).astype(bf16)
    rotM = np.zeros((128, 128), dtype=np.float32)
    for d in range(64):
        rotM[d + 64, d] = -1.0
        rotM[d, d + 64] = 1.0
    rotM = rotM.astype(bf16)
    onec = np.ones((128, 2), dtype=np.float32).astype(bf16)
    oner = np.ones((1, 128), dtype=np.float32).astype(bf16)
    onerF = np.ones((1, 128), dtype=np.float32)

    wfT = np.ascontiguousarray(wf.T)
    in_maps = []
    for c in range(8):
        b, k = c // 4, c % 4
        xTb = np.ascontiguousarray(np.asarray(x[b]).T)
        m = {
            "xT": xTb,
            "xq": np.ascontiguousarray(xTb[512 * k:512 * (k + 1), :]),
            "wq": np.ascontiguousarray(Wq[:, 512 * k:512 * (k + 1)]),
            "wk": np.ascontiguousarray(Wk[:, 128 * k:128 * (k + 1)]),
            "wv": np.ascontiguousarray(Wv[:, 128 * k:128 * (k + 1)]),
            "wo_q": np.ascontiguousarray(Wo[:, 512 * k:512 * (k + 1)]),
            "wf_q": np.ascontiguousarray(wfT[:, 512 * k:512 * (k + 1)]),
            "bq4": np.ascontiguousarray(bq[512 * k:512 * (k + 1)].reshape(4, 128).T),
            "bk1": np.ascontiguousarray(bk[128 * k:128 * (k + 1)].reshape(1, 128).T),
            "bv1": np.ascontiguousarray(bv[128 * k:128 * (k + 1)].reshape(1, 128).T),
            "fscI": np.ascontiguousarray(fs[512 * k:512 * (k + 1)].reshape(4, 128).T.astype(np.float32)),
            "fbiI": np.ascontiguousarray(fb[512 * k:512 * (k + 1)].reshape(4, 128).T.astype(np.float32)),
            "cosI": cosF, "sinI": sinF, "maskI": maska, "idenI": iden,
            "rotI": rotM, "onecI": onec, "onerI": oner, "onerFI": onerF,
        }
        in_maps.append(m)

    nc = build_nc()
    res = run_bass_kernel_spmd(nc, in_maps, core_ids=list(range(8)))

    out = np.empty((B, T, DM), dtype=np.float32)
    for c in range(8):
        b, k = c // 4, c % 4
        out[b, :, 512 * k:512 * (k + 1)] = res.results[c]["out_T"].T
    return out




# version 17
# speedup vs baseline: 1.7949x; 1.0072x over previous
import contextlib
import numpy as np
import ml_dtypes
import concourse.bass as bass
import concourse.mybir as mybir
from concourse.bass_utils import run_bass_kernel_spmd

F32 = mybir.dt.float32
BF16 = mybir.dt.bfloat16
AF = mybir.ActivationFunctionType
ALU = mybir.AluOpType
bf16 = ml_dtypes.bfloat16

B = 2
T = 8192
DM = 2048
D = 128
NB = T // 512
NC = DM // 128
SCALE = 1.0 / float(np.sqrt(D))
RG = [[0, 1, 2, 3], [4, 5, 6, 7]]

ROWS = [(0, 0), (1, 0)] + [(qb, h) for qb in range(NB) for h in range(4)]
N_SACR = 2
ROW_GROUPS = []
_g = 0
for (qb, _h) in ROWS:
    ROW_GROUPS.append(list(range(_g, _g + 2 * (qb + 1))))
    _g += 2 * (qb + 1)
NGRP = _g
XP_OF = {s: s + 1 for s in range(NGRP)}
EV_EVAC_DONE = {}
EV_TS_DONE = {}
EV_DUMMY_DONE = {}
_ev = 0
for _r in range(len(ROWS)):
    if _r >= 1:
        _ev += 4
        EV_EVAC_DONE[_r - 1] = _ev
    _ev += 4
    EV_TS_DONE[_r] = _ev
    _ev += 1
    EV_DUMMY_DONE[_r] = _ev
_ev += 4
EV_EVAC_DONE[len(ROWS) - 1] = _ev
EV_TOTAL = _ev
LOC_OF = {}
ROW_OF = {}
for _ri in range(len(ROWS)):
    for _j, _s in enumerate(ROW_GROUPS[_ri]):
        LOC_OF[_s] = _j
        ROW_OF[_s] = _ri


def build_nc():
    nc = bass.Bass("TRN2", target_bir_lowering=False)

    xT = nc.dram_tensor("xT", [DM, T], BF16, kind="ExternalInput").ap()
    xq = nc.dram_tensor("xq", [512, T], BF16, kind="ExternalInput").ap()
    wq = nc.dram_tensor("wq", [DM, 512], BF16, kind="ExternalInput").ap()
    wk = nc.dram_tensor("wk", [DM, 128], BF16, kind="ExternalInput").ap()
    wv = nc.dram_tensor("wv", [DM, 128], BF16, kind="ExternalInput").ap()
    wo_q = nc.dram_tensor("wo_q", [DM, 512], BF16, kind="ExternalInput").ap()
    wf_q = nc.dram_tensor("wf_q", [DM, 512], F32, kind="ExternalInput").ap()
    bq4 = nc.dram_tensor("bq4", [128, 4], BF16, kind="ExternalInput").ap()
    bk1 = nc.dram_tensor("bk1", [128, 1], BF16, kind="ExternalInput").ap()
    bv1 = nc.dram_tensor("bv1", [128, 1], BF16, kind="ExternalInput").ap()
    fscI = nc.dram_tensor("fscI", [128, 4], F32, kind="ExternalInput").ap()
    fbiI = nc.dram_tensor("fbiI", [128, 4], F32, kind="ExternalInput").ap()
    cosI = nc.dram_tensor("cosI", [128, T], BF16, kind="ExternalInput").ap()
    sinI = nc.dram_tensor("sinI", [128, T], BF16, kind="ExternalInput").ap()
    maskI = nc.dram_tensor("maskI", [128, 1024], BF16, kind="ExternalInput").ap()
    idenI = nc.dram_tensor("idenI", [128, 128], BF16, kind="ExternalInput").ap()
    rotI = nc.dram_tensor("rotI", [128, 128], BF16, kind="ExternalInput").ap()
    onecI = nc.dram_tensor("onecI", [128, 2], BF16, kind="ExternalInput").ap()
    onerI = nc.dram_tensor("onerI", [1, 128], BF16, kind="ExternalInput").ap()
    onerFI = nc.dram_tensor("onerFI", [1, 128], F32, kind="ExternalInput").ap()

    out_T = nc.dram_tensor("out_T", [512, T], F32, kind="ExternalOutput").ap()

    O_scr = nc.dram_tensor("O_scr", [512, T], BF16, kind="Internal")
    OG = nc.dram_tensor("OG", [DM, T], BF16, kind="Internal")
    xn_scr = nc.dram_tensor("xn_scr", [DM, T], BF16, kind="Internal").ap()

    st = contextlib.ExitStack()
    E = st.enter_context

    iden = E(nc.sbuf_tensor("iden", [128, 128], BF16))
    rotM = E(nc.sbuf_tensor("rotM", [128, 128], BF16))
    maska = E(nc.sbuf_tensor("maska", [128, 1024], BF16))
    onec = E(nc.sbuf_tensor("onec", [128, 2], BF16))
    oner = E(nc.sbuf_tensor("oner", [1, 128], BF16))
    onerF = E(nc.sbuf_tensor("onerF", [1, 128], F32))
    bqb = E(nc.sbuf_tensor("bqb", [128, 4], BF16))
    bkb = E(nc.sbuf_tensor("bkb", [128, 1], BF16))
    bvb = E(nc.sbuf_tensor("bvb", [128, 1], BF16))
    bq_s = E(nc.sbuf_tensor("bq_s", [128, 4], F32))
    bk_s = E(nc.sbuf_tensor("bk_s", [128, 1], F32))
    bv_s = E(nc.sbuf_tensor("bv_s", [128, 1], F32))
    fsc = E(nc.sbuf_tensor("fsc", [128, 4], F32))
    fbi = E(nc.sbuf_tensor("fbi", [128, 4], F32))

    names = ["s_ld", "s_ldt", "s_ldc", "s_sqf", "s_sqb", "s_sqmm", "s_stsum",
             "s_rw1", "s_rw2", "s_row", "s_bcmm", "s_bc", "s_nrm", "s_prj", "s_evA",
             "s_rotmm", "s_rp", "s_vtb", "s_vte", "s_sp",
             "s_mm", "s_xp", "s_pv", "s_ev", "s_tr", "s_sto", "s_cc", "s_cc2",
             "s_cw", "s_wsp", "s_cg", "s_cmm", "s_cev", "s_sil", "s_cdone", "s_out"]
    S = {n: E(nc.semaphore(n)) for n in names}

    with st:
        sAB = contextlib.ExitStack()
        q_s = sAB.enter_context(nc.sbuf_tensor("q_s", [128, 4 * T], BF16))
        k_s = sAB.enter_context(nc.sbuf_tensor("k_s", [128, T], BF16))
        v_s = sAB.enter_context(nc.sbuf_tensor("v_s", [128, (T // 128) * 129], BF16))
        with contextlib.ExitStack() as sA:
            EA = sA.enter_context
            wq_s = EA(nc.sbuf_tensor("wq_s", [128, NC * 512], BF16))
            wk_s = EA(nc.sbuf_tensor("wk_s", [128, NC * 128], BF16))
            wv_s = EA(nc.sbuf_tensor("wv_s", [128, NC * 128], BF16))
            xT_s = EA(nc.sbuf_tensor("xT_s", [128, 2 * NC * 512], BF16))
            xsq = EA(nc.sbuf_tensor("xsq", [128, 4 * 512], BF16))
            xn_s = EA(nc.sbuf_tensor("xn_s", [128, NC * 512], BF16))
            cs_s = EA(nc.sbuf_tensor("cs_s", [128, 2 * 512], BF16))
            sn_s = EA(nc.sbuf_tensor("sn_s", [128, 2 * 512], BF16))
            rowm = EA(nc.sbuf_tensor("rowm", [1, 512], F32))
            rowv = EA(nc.sbuf_tensor("rowv", [1, 512], F32))
            rowt = EA(nc.sbuf_tensor("rowt", [1, 512], F32))
            rowr = EA(nc.sbuf_tensor("rowr", [1, 512], F32))
            rowbf = EA(nc.sbuf_tensor("rowbf", [1, 1024], BF16))
            Rb = EA(nc.sbuf_tensor("Rb", [128, 512], F32))
            Mb = EA(nc.sbuf_tensor("Mb", [128, 512], BF16))
            tmpn = EA(nc.sbuf_tensor("tmpn", [128, 512], BF16))
            qraw = EA(nc.sbuf_tensor("qraw", [128, 2 * 512], BF16))
            t1b = EA(nc.sbuf_tensor("t1b", [128, 512], BF16))
            t2b = EA(nc.sbuf_tensor("t2b", [128, 512], BF16))
            vE = EA(nc.sbuf_tensor("vE", [128, 512], BF16))
            ps_sum = EA(nc.psum_tensor("ps_sum", [128, 512], F32))
            ps_sq = EA(nc.psum_tensor("ps_sq", [128, 512], F32))
            ps_bc = EA(nc.psum_tensor("ps_bc", [128, 1024], F32))
            ps_pA = EA(nc.psum_tensor("ps_pA", [128, 512], F32))
            ps_pB = EA(nc.psum_tensor("ps_pB", [128, 512], F32))
            ps_rot = EA(nc.psum_tensor("ps_rot", [128, 512], F32))
            ps_vt = EA(nc.psum_tensor("ps_vt", [128, 4 * 128], BF16))

            with nc.Block("A") as blk:

                @blk.sync
                def _(sync):
                    for dst, src in [
                        (iden[:, :], idenI), (rotM[:, :], rotI), (maska[:, :], maskI),
                        (onec[:, :], onecI), (oner[:, :], onerI), (onerF[:, :], onerFI),
                        (bqb[:, :], bq4), (bkb[:, :], bk1), (bvb[:, :], bv1),
                        (fsc[:, :], fscI), (fbi[:, :], fbiI),
                    ]:
                        sync.dma_start(out=dst, in_=src).then_inc(S["s_ldc"], 16)
                    sync.dma_start(out=wq_s[:, :].rearrange("p (c m) -> p c m", c=NC),
                                   in_=wq.rearrange("(c p) m -> p c m", p=128)).then_inc(S["s_ldc"], 16)
                    sync.dma_start(out=wk_s[:, :].rearrange("p (c m) -> p c m", c=NC),
                                   in_=wk.rearrange("(c p) m -> p c m", p=128)).then_inc(S["s_ldc"], 16)
                    sync.dma_start(out=wv_s[:, :].rearrange("p (c m) -> p c m", c=NC),
                                   in_=wv.rearrange("(c p) m -> p c m", p=128)).then_inc(S["s_ldc"], 16)
                    for tb in range(NB):
                        buf = tb % 2
                        if tb >= 2:
                            sync.wait_ge(S["s_sqb"], tb - 1)
                            sync.wait_ge(S["s_stsum"], tb - 1)
                            sync.wait_ge(S["s_nrm"], NC * (tb - 1))
                        src_b = xT[:, tb * 512:(tb + 1) * 512].rearrange("(c p) t -> p c t", p=128)
                        dst_b = xT_s[:, buf * 8192:(buf + 1) * 8192].rearrange("p (c t) -> p c t", c=NC)
                        for pt in range(4):
                            sync.dma_start(out=dst_b[:, pt * 4:(pt + 1) * 4, :],
                                           in_=src_b[:, pt * 4:(pt + 1) * 4, :]).then_inc(S["s_ld"], 16)
                        if tb >= 2:
                            sync.wait_ge(S["s_rp"], 5 * (tb - 1))
                        sync.dma_start(out=cs_s[:, buf * 512:(buf + 1) * 512],
                                       in_=cosI[:, tb * 512:(tb + 1) * 512]).then_inc(S["s_ldt"], 16)
                        sync.dma_start(out=sn_s[:, buf * 512:(buf + 1) * 512],
                                       in_=sinI[:, tb * 512:(tb + 1) * 512]).then_inc(S["s_ldt"], 16)

                @blk.gpsimd
                def _(gpsimd):
                    for tb in range(NB):
                        gpsimd.wait_ge(S["s_nrm"], NC * (tb + 1))
                        dst = xn_scr[:, tb * 512:(tb + 1) * 512].rearrange("(c p) t -> p c t", p=128)
                        src = xn_s[:, :].rearrange("p (c t) -> p c t", c=NC)
                        gpsimd.dma_start(out=dst[:, 0:8, :], in_=src[:, 0:8, :]).then_inc(S["s_sp"], 16)
                        gpsimd.dma_start(out=dst[:, 8:16, :], in_=src[:, 8:16, :]).then_inc(S["s_sp"], 16)

                @blk.scalar
                def _(scalar):
                    scalar.wait_ge(S["s_ldc"], 16 * 14)
                    for tb in range(NB):
                        buf = tb % 2
                        xb = xT_s[:, buf * 8192:(buf + 1) * 8192]
                        scalar.wait_ge(S["s_ld"], 64 * (tb + 1))
                        for c in range(NC):
                            gi = tb * NC + c
                            if gi >= 4:
                                scalar.wait_ge(S["s_sqmm"], gi - 3)
                            scalar.activation(xsq[:, (gi % 4) * 512:(gi % 4 + 1) * 512],
                                              xb[:, c * 512:(c + 1) * 512],
                                              AF.Square).then_inc(S["s_sqf"], 1)
                        scalar.drain().then_inc(S["s_sqb"], 1)
                        scalar.wait_ge(S["s_rw1"], tb + 1)
                        scalar.activation(rowt[0:1, :], rowv[0:1, :], AF.Ln)
                        scalar.activation(rowt[0:1, :], rowt[0:1, :], AF.Exp, scale=0.5)
                        scalar.drain().then_inc(S["s_rw2"], 1)
                        evs = [("K", qraw[:, 0:512], bk_s[0:128, 0:1]),
                               ("V", vE[:, :], bv_s[0:128, 0:1]),
                               ("Q0", qraw[:, 512:1024], bq_s[0:128, 0:1]),
                               ("Q1", qraw[:, 0:512], bq_s[0:128, 1:2]),
                               ("Q2", qraw[:, 512:1024], bq_s[0:128, 2:3]),
                               ("Q3", qraw[:, 0:512], bq_s[0:128, 3:4])]
                        for gi, (nm, dst, bias_ap) in enumerate(evs):
                            scalar.wait_ge(S["s_prj"], 6 * tb + gi + 1)
                            if nm == "K" and tb >= 1:
                                scalar.wait_ge(S["s_rp"], 5 * tb)
                                scalar.wait_ge(S["s_rotmm"], 5 * tb)
                            if nm == "V" and tb >= 1:
                                scalar.wait_ge(S["s_vtb"], tb)
                            if nm in ("Q1", "Q2", "Q3"):
                                k_ = {"Q1": 1, "Q2": 2, "Q3": 3}[nm]
                                scalar.wait_ge(S["s_rp"], 5 * tb + k_)
                                scalar.wait_ge(S["s_rotmm"], 5 * tb + k_)
                            psum = ps_pA if gi % 2 == 0 else ps_pB
                            scalar.activation(dst, psum[:, :], AF.Identity,
                                              bias=bias_ap).then_inc(S["s_evA"], 1)

                @blk.vector
                def _(vector):
                    vector.wait_ge(S["s_ldc"], 16 * 14)
                    vector.memset(v_s[:, :].rearrange("p (n f) -> p n f", f=129)[:, :, 128:129], 1.0)
                    vector.tensor_copy(bq_s[:, :], bqb[:, :])
                    vector.tensor_copy(bk_s[:, :], bkb[:, :])
                    vector.tensor_copy(bv_s[:, :], bvb[:, :])
                    for tb in range(NB):
                        buf = tb % 2
                        xb = xT_s[:, buf * 8192:(buf + 1) * 8192]
                        csb = cs_s[:, buf * 512:(buf + 1) * 512]
                        snb = sn_s[:, buf * 512:(buf + 1) * 512]
                        vector.wait_ge(S["s_stsum"], tb + 1)
                        vector.wait_ge(S["s_sqmm"], NC * (tb + 1))
                        vector.tensor_scalar(rowm[0:1, :], ps_sum[0:1, :], 1.0 / DM, None, ALU.mult)
                        vector.tensor_copy(rowbf[0:1, 0:512], rowm[0:1, :])
                        vector.tensor_scalar(rowv[0:1, :], ps_sq[0:1, :], 1.0 / DM, None, ALU.mult)
                        vector.tensor_tensor(rowt[0:1, :], rowm[0:1, :], rowm[0:1, :], ALU.mult)
                        vector.tensor_tensor(rowv[0:1, :], rowv[0:1, :], rowt[0:1, :], ALU.subtract)
                        vector.tensor_copy(rowbf[0:1, 512:1024], rowv[0:1, :])
                        vector.tensor_copy(rowv[0:1, :], rowbf[0:1, 512:1024])
                        vector.tensor_scalar(rowv[0:1, :], rowv[0:1, :], 1.0e-5, None, ALU.add)
                        vector.drain().then_inc(S["s_rw1"], 1)
                        vector.wait_ge(S["s_rw2"], tb + 1)
                        vector.tensor_copy(rowbf[0:1, 512:1024], rowt[0:1, :])
                        vector.reciprocal(rowr[0:1, :], rowbf[0:1, 512:1024])
                        vector.drain().then_inc(S["s_row"], 1)
                        vector.wait_ge(S["s_bcmm"], tb + 1)
                        vector.tensor_copy(Rb[:, :], ps_bc[:, 0:512])
                        vector.tensor_copy(Mb[:, :], ps_bc[:, 512:1024])
                        vector.drain().then_inc(S["s_bc"], 1)
                        if tb >= 1:
                            vector.wait_ge(S["s_prj"], 6 * tb)
                            vector.wait_ge(S["s_sp"], 32 * tb)
                        for c in range(NC):
                            vector.tensor_tensor(tmpn[:, :], xb[:, c * 512:(c + 1) * 512], Mb[:, :], ALU.subtract)
                            vector.tensor_tensor(xn_s[:, c * 512:(c + 1) * 512], tmpn[:, :], Rb[:, :],
                                                 ALU.mult).then_inc(S["s_nrm"], 1)
                        vector.wait_ge(S["s_ldt"], 32 * (tb + 1))
                        ropes = [(qraw[:, 0:512], k_s[:, tb * 512:(tb + 1) * 512])] + [
                            (qraw[:, (512 if h % 2 == 0 else 0):(1024 if h % 2 == 0 else 512)],
                             q_s[:, h * T + tb * 512: h * T + (tb + 1) * 512]) for h in range(4)]
                        for ri_, (srcq, dsta) in enumerate(ropes):
                            vector.wait_ge(S["s_rotmm"], 5 * tb + ri_ + 1)
                            vector.tensor_tensor(t1b[:, :], srcq, csb, ALU.mult)
                            vector.tensor_tensor(t2b[:, :], ps_rot[:, :], snb, ALU.mult)
                            vector.tensor_tensor(dsta, t1b[:, :], t2b[:, :], ALU.add)
                            vector.drain().then_inc(S["s_rp"], 1)
                        vector.wait_ge(S["s_vtb"], tb + 1)
                        for sub in range(4):
                            tile = tb * 4 + sub
                            vector.tensor_copy(v_s[:, tile * 129: tile * 129 + 128],
                                               ps_vt[:, sub * 128:(sub + 1) * 128])
                        vector.drain().then_inc(S["s_vte"], 1)

                @blk.tensor
                def _(tensor):
                    tensor.wait_ge(S["s_ldc"], 16 * 14)
                    for tb in range(NB):
                        buf = tb % 2
                        xb = xT_s[:, buf * 8192:(buf + 1) * 8192]
                        tensor.wait_ge(S["s_ld"], 64 * (tb + 1))
                        if tb >= 1:
                            tensor.wait_ge(S["s_rw1"], tb)
                        mm = None
                        for c in range(NC):
                            mm = tensor.matmul(ps_sum[0:1, :], onec[:, 0:1],
                                               xb[:, c * 512:(c + 1) * 512],
                                               start=(c == 0), stop=(c == NC - 1))
                        mm.then_inc(S["s_stsum"], 1)
                        for c in range(NC):
                            gi = tb * NC + c
                            tensor.wait_ge(S["s_sqf"], gi + 1)
                            mm = tensor.matmul(ps_sq[0:1, :], onec[:, 0:1],
                                               xsq[:, (gi % 4) * 512:(gi % 4 + 1) * 512],
                                               start=(c == 0), stop=(c == NC - 1))
                            mm.then_inc(S["s_sqmm"], 1)
                        tensor.wait_ge(S["s_row"], tb + 1)
                        if tb >= 1:
                            tensor.wait_ge(S["s_bc"], tb)
                        tensor.matmul(ps_bc[:, 0:512], onerF[0:1, :], rowr[0:1, :], start=True, stop=True)
                        mm = tensor.matmul(ps_bc[:, 512:1024], oner[0:1, :], rowbf[0:1, 0:512], start=True, stop=True)
                        mm.then_inc(S["s_bcmm"], 1)
                        projs = [("K", wk_s, 0), ("V", wv_s, 0), ("Q0", wq_s, 0),
                                 ("Q1", wq_s, 1), ("Q2", wq_s, 2), ("Q3", wq_s, 3)]
                        for gi, (nm, wsb, mi) in enumerate(projs):
                            psum = ps_pA if gi % 2 == 0 else ps_pB
                            need = 6 * tb + gi - 1
                            if need >= 1:
                                tensor.wait_ge(S["s_evA"], need)
                            for c in range(NC):
                                tensor.wait_ge(S["s_nrm"], NC * tb + c + 1)
                                if nm in ("K", "V"):
                                    lhs = wsb[:, c * 128:(c + 1) * 128]
                                else:
                                    lhs = wsb[:, c * 512 + mi * 128: c * 512 + (mi + 1) * 128]
                                mm = tensor.matmul(psum[:, :], lhs, xn_s[:, c * 512:(c + 1) * 512],
                                                   start=(c == 0), stop=(c == NC - 1))
                            mm.then_inc(S["s_prj"], 1)
                            if nm == "Q0":
                                tensor.wait_ge(S["s_evA"], 6 * tb + 3)
                                if tb >= 1:
                                    tensor.wait_ge(S["s_vte"], tb)
                                for sub in range(4):
                                    mm2 = tensor.transpose(ps_vt[:, sub * 128:(sub + 1) * 128],
                                                           vE[:, sub * 128:(sub + 1) * 128],
                                                           iden[:, :])
                                mm2.then_inc(S["s_vtb"], 1)
                            if nm != "V":
                                rot_i = {"K": 0, "Q0": 1, "Q1": 2, "Q2": 3, "Q3": 4}[nm]
                                tensor.wait_ge(S["s_evA"], 6 * tb + gi + 1)
                                if 5 * tb + rot_i >= 1:
                                    tensor.wait_ge(S["s_rp"], 5 * tb + rot_i)
                                slot = qraw[:, 0:512] if nm in ("K", "Q1", "Q3") else qraw[:, 512:1024]
                                mm3 = tensor.matmul(ps_rot[:, :], rotM[:, :], slot, start=True, stop=True)
                                mm3.then_inc(S["s_rotmm"], 1)

        wos = st.enter_context(nc.sbuf_tensor("wos", [128, NC * 512], BF16, side="right"))
        wraw = st.enter_context(nc.sbuf_tensor("wraw", [128, NC * 512], F32, side="right"))
        whi = st.enter_context(nc.sbuf_tensor("whi", [128, NC * 512], BF16, side="right"))
        wlo = st.enter_context(nc.sbuf_tensor("wlo", [128, NC * 512], BF16, side="right"))
        with contextlib.ExitStack() as sB:
            EB = sB.enter_context
            pT = EB(nc.sbuf_tensor("pT", [128, 4 * 1024], BF16))
            on_s = EB(nc.sbuf_tensor("on_s", [128, 4 * 128], BF16))
            rden = EB(nc.sbuf_tensor("rden", [128, 8], F32))
            oTb = EB(nc.sbuf_tensor("oTb", [128, 2 * 512], BF16))
            xdum = EB(nc.sbuf_tensor("xdum", [1, 2], BF16))
            sS0 = EB(nc.psum_tensor("sS0", [128, 1024], F32))
            sS1 = EB(nc.psum_tensor("sS1", [128, 1024], F32))
            ob01 = EB(nc.psum_tensor("ob01", [128, 260], F32))
            ob23 = EB(nc.psum_tensor("ob23", [128, 260], F32))
            ptr = EB(nc.psum_tensor("ptr", [128, 512], BF16))

            def ob_seg(sub):
                tt = ob01 if sub < 2 else ob23
                return tt[:, (sub % 2) * 129:(sub % 2 + 1) * 129]

            with nc.Block("Bb") as blk:

                @blk.tensor
                def _(tensor):
                    tensor.wait_ge(S["s_rp"], 5 * NB)
                    tensor.wait_ge(S["s_vte"], NB)

                    def emit_dummies(ri):
                        if ri >= 1:
                            tensor.wait_ge(S["s_ev"], EV_TS_DONE[ri - 1])
                        tensor.matmul(ob01[0:128, 258:260], iden[:, :], onec[:, 0:2],
                                      start=True, stop=True, skip_group_check=True)
                        tensor.matmul(ob23[0:128, 258:260], iden[:, :], onec[:, 0:2],
                                      start=True, stop=True, skip_group_check=True)

                    def emit_qk(s_id):
                        ri = ROW_OF[s_id]
                        qb, h = ROWS[ri]
                        j = LOC_OF[s_id]
                        q_ap = q_s[:, h * T + qb * 512: h * T + (qb + 1) * 512]
                        ps = sS0 if s_id % 2 == 0 else sS1
                        if s_id >= 2:
                            tensor.wait_ge(S["s_xp"], XP_OF[s_id - 2])
                        mm = None
                        for ti in range(2):
                            kt = 2 * j + ti
                            is_diag = (kt >= 4 * qb)
                            mm = tensor.matmul(ps[:, ti * 512:(ti + 1) * 512],
                                               k_s[:, kt * 128:(kt + 1) * 128], q_ap,
                                               start=True, stop=not is_diag)
                            if is_diag:
                                off = 512 - 128 * (kt - 4 * qb)
                                mm = tensor.matmul(ps[:, ti * 512:(ti + 1) * 512],
                                                   iden[:, :], maska[:, off:off + 512],
                                                   start=False, stop=True)
                        mm.then_inc(S["s_mm"], 1)

                    def emit_pv(s_id):
                        ri = ROW_OF[s_id]
                        if LOC_OF[s_id] == 0:
                            emit_dummies(ri)
                        tensor.wait_ge(S["s_xp"], XP_OF[s_id] + 2)
                        slot = s_id % 4
                        j = LOC_OF[s_id]
                        last = (s_id == ROW_GROUPS[ri][-1])
                        mm = None
                        for ti in range(2):
                            kt = 2 * j + ti
                            for sub in range(4):
                                mm = tensor.matmul(
                                    ob_seg(sub),
                                    pT[:, slot * 1024 + ti * 512 + sub * 128:
                                       slot * 1024 + ti * 512 + (sub + 1) * 128],
                                    v_s[:, kt * 129: kt * 129 + 129],
                                    start=False, stop=(last and ti == 1),
                                    skip_group_check=True)
                        mm.then_inc(S["s_pv"], 1)
                        if last:
                            tensor.wait_ge(S["s_ev"], EV_DUMMY_DONE[ri])
                            for si in range(4):
                                mmt = tensor.transpose(ptr[:, si * 128:(si + 1) * 128],
                                                       on_s[:, si * 128:(si + 1) * 128], iden[:, :])
                            mmt.then_inc(S["s_tr"], 1)

                    for s_id in range(NGRP):
                        emit_qk(s_id)
                        if s_id >= 3:
                            emit_pv(s_id - 3)
                    emit_pv(NGRP - 3)
                    emit_pv(NGRP - 2)
                    emit_pv(NGRP - 1)

                @blk.scalar
                def _(scalar):
                    for s_id in range(NGRP):
                        scalar.wait_ge(S["s_mm"], s_id + 1)
                        if s_id >= 4:
                            scalar.wait_ge(S["s_pv"], s_id - 3)
                        ps = sS0 if s_id % 2 == 0 else sS1
                        slot = s_id % 4
                        scalar.activation(pT[:, slot * 1024:(slot + 1) * 1024],
                                          ps[:, :], AF.Exp, scale=SCALE).then_inc(S["s_xp"], 1)
                    scalar.activation(xdum[0:1, :], xdum[0:1, :], AF.Exp).then_inc(S["s_xp"], 1)
                    scalar.activation(xdum[0:1, :], xdum[0:1, :], AF.Exp).then_inc(S["s_xp"], 1)

                @blk.vector
                def _(vector):
                    vector.memset(xdum[0:1, :], 0.0)

                    def evacs(r):
                        par = r % 2
                        vector.wait_ge(S["s_tr"], r + 1)
                        if r >= 2:
                            vector.wait_ge(S["s_sto"], 16 * (r - 1))
                        for si in range(4):
                            vector.tensor_copy(oTb[:, par * 512 + si * 128: par * 512 + (si + 1) * 128],
                                               ptr[:, si * 128:(si + 1) * 128]).then_inc(S["s_ev"], 1)

                    for ri in range(len(ROWS)):
                        vector.wait_ge(S["s_pv"], ROW_GROUPS[ri][-1] + 1)
                        if ri >= 1:
                            evacs(ri - 1)
                        for sub in range(4):
                            vector.reciprocal(rden[:, sub:sub + 1], ob_seg(sub)[:, 128:129])
                        for sub in range(4):
                            vector.tensor_scalar(on_s[:, sub * 128:(sub + 1) * 128], ob_seg(sub)[:, 0:128],
                                                 rden[:, sub:sub + 1], None,
                                                 ALU.mult).then_inc(S["s_ev"], 1)
                        vector.memset(rden[:, 4:5], 0.0)
                        vector.drain().then_inc(S["s_ev"], 1)
                    evacs(len(ROWS) - 1)

                @blk.gpsimd
                def _(gpsimd):
                    for ri, (qb, h) in enumerate(ROWS):
                        par = ri % 2
                        gpsimd.wait_ge(S["s_ev"], EV_EVAC_DONE[ri])
                        gpsimd.dma_start(out=O_scr.ap()[h * 128:(h + 1) * 128,
                                                        qb * 512:(qb + 1) * 512],
                                         in_=oTb[:, par * 512:(par + 1) * 512]).then_inc(S["s_sto"], 16)
                    gpsimd.wait_ge(S["s_sto"], 16 * len(ROWS))
                    gpsimd.collective_compute(
                        "AllGather", ALU.bypass, replica_groups=RG,
                        ins=[O_scr.ap().opt()], outs=[OG.ap().opt()]).then_inc(S["s_cc"], 1)

        sAB.close()
        with contextlib.ExitStack() as sC:
            EC = sC.enter_context
            ogs = EC(nc.sbuf_tensor("ogs", [128, 2 * NC * 512], BF16))
            xns = EC(nc.sbuf_tensor("xns", [128, 2 * NC * 512], BF16))
            xqs = EC(nc.sbuf_tensor("xqs", [128, 2 * 4 * 512], BF16))
            stile = EC(nc.sbuf_tensor("stile", [128, 512], BF16))
            gpre = EC(nc.sbuf_tensor("gpre", [128, 512], F32))
            gact = EC(nc.sbuf_tensor("gact", [128, 512], F32))
            sgt = EC(nc.sbuf_tensor("sgt", [128, 512], F32))
            outb = EC(nc.sbuf_tensor("outb", [128, 2 * 4 * 512], F32))
            psS0 = EC(nc.psum_tensor("psS0", [128, 512], F32))
            psS1 = EC(nc.psum_tensor("psS1", [128, 512], F32))
            psG0 = EC(nc.psum_tensor("psG0", [128, 512], F32))
            psG1 = EC(nc.psum_tensor("psG1", [128, 512], F32))

            with nc.Block("Cc") as blk:

                @blk.sync
                def _(sync):
                    for half in range(2):
                        sync.dma_start(
                            out=wos[:, :].rearrange("p (c m) -> p c m", c=NC)[:, half * 8:(half + 1) * 8, :],
                            in_=wo_q.rearrange("(c p) m -> p c m", p=128)[:, half * 8:(half + 1) * 8, :],
                        ).then_inc(S["s_cw"], 16)
                        sync.dma_start(
                            out=wraw[:, :].rearrange("p (c m) -> p c m", c=NC)[:, half * 8:(half + 1) * 8, :],
                            in_=wf_q.rearrange("(c p) m -> p c m", p=128)[:, half * 8:(half + 1) * 8, :],
                        ).then_inc(S["s_cw"], 16)
                    sync.wait_ge(S["s_cc"], 1)
                    for tb in range(NB):
                        par = tb % 2
                        if tb >= 2:
                            sync.wait_ge(S["s_cmm"], 8 * (tb - 1))
                            sync.wait_ge(S["s_cdone"], 4 * (tb - 1))
                        ogd = ogs[:, par * 8192:(par + 1) * 8192].rearrange("p (c t) -> p c t", c=NC)
                        ogsrc = OG.ap()[:, tb * 512:(tb + 1) * 512].rearrange("(c p) t -> p c t", p=128)
                        xnd = xns[:, par * 8192:(par + 1) * 8192].rearrange("p (c t) -> p c t", c=NC)
                        xnsrc = xn_scr[:, tb * 512:(tb + 1) * 512].rearrange("(c p) t -> p c t", p=128)
                        for half in range(2):
                            sync.dma_start(out=ogd[:, half * 8:(half + 1) * 8, :],
                                           in_=ogsrc[:, half * 8:(half + 1) * 8, :]).then_inc(S["s_cg"], 16)
                            sync.dma_start(out=xnd[:, half * 8:(half + 1) * 8, :],
                                           in_=xnsrc[:, half * 8:(half + 1) * 8, :]).then_inc(S["s_cg"], 16)
                        sync.dma_start(
                            out=xqs[:, par * 2048:(par + 1) * 2048].rearrange("p (c t) -> p c t", c=4),
                            in_=xq[:, tb * 512:(tb + 1) * 512].rearrange("(c p) t -> p c t", p=128),
                        ).then_inc(S["s_cg"], 16)

                @blk.vector
                def _(vector):
                    vector.wait_ge(S["s_cw"], 64)
                    for c in range(NC):
                        vector.tensor_copy(whi[:, c * 512:(c + 1) * 512], wraw[:, c * 512:(c + 1) * 512])
                        vector.tensor_tensor(wlo[:, c * 512:(c + 1) * 512],
                                             wraw[:, c * 512:(c + 1) * 512],
                                             whi[:, c * 512:(c + 1) * 512],
                                             ALU.subtract).then_inc(S["s_wsp"], 1)
                    for tb in range(NB):
                        par = tb % 2
                        if tb >= 2:
                            vector.wait_ge(S["s_out"], 16 * (tb - 1))
                        for m in range(4):
                            gm = 4 * tb + m
                            psS = psS0 if gm % 2 == 0 else psS1
                            psG = psG0 if gm % 2 == 0 else psG1
                            vector.wait_ge(S["s_cmm"], 8 * tb + 2 * m + 1)
                            vector.tensor_copy(stile[:, :], psS[:, :])
                            vector.wait_ge(S["s_cmm"], 8 * tb + 2 * m + 2)
                            if gm >= 1:
                                vector.wait_ge(S["s_sil"], gm)
                            vector.tensor_scalar(gpre[:, :], psG[:, :], fsc[:, m:m + 1],
                                                 fbi[:, m:m + 1], ALU.mult,
                                                 ALU.add).then_inc(S["s_cev"], 1)
                            vector.wait_ge(S["s_sil"], gm + 1)
                            vector.tensor_tensor(sgt[:, :], stile[:, :], gact[:, :], ALU.mult)
                            vector.tensor_tensor(outb[:, par * 2048 + m * 512: par * 2048 + (m + 1) * 512],
                                                 sgt[:, :],
                                                 xqs[:, par * 2048 + m * 512: par * 2048 + (m + 1) * 512],
                                                 ALU.add).then_inc(S["s_cdone"], 1)

                @blk.scalar
                def _(scalar):
                    for gm in range(4 * NB):
                        scalar.wait_ge(S["s_cev"], gm + 1)
                        if gm >= 1:
                            scalar.wait_ge(S["s_cdone"], gm)
                        scalar.activation(gact[:, :], gpre[:, :], AF.Silu).then_inc(S["s_sil"], 1)

                @blk.tensor
                def _(tensor):
                    tensor.wait_ge(S["s_wsp"], NC)
                    for tb in range(NB):
                        par = tb % 2
                        tensor.wait_ge(S["s_cg"], 80 * (tb + 1))
                        ogb = ogs[:, par * 8192:(par + 1) * 8192]
                        xnb = xns[:, par * 8192:(par + 1) * 8192]
                        for m in range(4):
                            gm = 4 * tb + m
                            psS = psS0 if gm % 2 == 0 else psS1
                            psG = psG0 if gm % 2 == 0 else psG1
                            if gm >= 2:
                                tensor.wait_ge(S["s_cev"], gm - 1)
                            mm = None
                            for c in range(NC):
                                mm = tensor.matmul(psS[:, :],
                                                   wos[:, c * 512 + m * 128: c * 512 + (m + 1) * 128],
                                                   ogb[:, c * 512:(c + 1) * 512],
                                                   start=(c == 0), stop=(c == NC - 1))
                            mm.then_inc(S["s_cmm"], 1)
                            for pi, wsb in enumerate([whi, wlo]):
                                for c in range(NC):
                                    mm = tensor.matmul(psG[:, :],
                                                       wsb[:, c * 512 + m * 128: c * 512 + (m + 1) * 128],
                                                       xnb[:, c * 512:(c + 1) * 512],
                                                       start=(pi == 0 and c == 0),
                                                       stop=(pi == 1 and c == NC - 1))
                            mm.then_inc(S["s_cmm"], 1)

                @blk.gpsimd
                def _(gpsimd):
                    for tb in range(NB):
                        par = tb % 2
                        gpsimd.wait_ge(S["s_cdone"], 4 * (tb + 1))
                        gpsimd.dma_start(
                            out=out_T[:, tb * 512:(tb + 1) * 512].rearrange("(c p) t -> p c t", p=128),
                            in_=outb[:, par * 2048:(par + 1) * 2048].rearrange("p (c t) -> p c t", c=4),
                        ).then_inc(S["s_out"], 16)
                    gpsimd.wait_ge(S["s_out"], 16 * NB)

    return nc


def kernel(**inputs):
    x = inputs["L0_attn_x"]
    Wq = inputs["L0_attn_Wq"]
    bq = inputs["L0_attn_bq"]
    Wk = inputs["L0_attn_Wk"]
    bk = inputs["L0_attn_bk"]
    Wv = inputs["L0_attn_Wv"]
    bv = inputs["L0_attn_bv"]
    Wo = inputs["L0_attn_Wo"]
    wf = inputs["L0_ffn_weight"]
    fs = inputs["L0_ffn_scale_weight"]
    fb = inputs["L0_ffn_bias"]

    tt = np.arange(64, dtype=np.float32) / np.float32(64)
    inv = np.float32(1.0) / (np.float32(1.0e6) ** tt)
    ang = np.outer(np.arange(T, dtype=np.float32), inv)
    cosT = np.cos(ang).astype(np.float32).astype(bf16)
    sinT = np.sin(ang).astype(np.float32).astype(bf16)
    cosF = np.ascontiguousarray(np.concatenate([cosT.T, cosT.T], axis=0))
    sinF = np.ascontiguousarray(np.concatenate([sinT.T, sinT.T], axis=0))
    jp = np.arange(128, dtype=np.int64)[:, None]
    uu = np.arange(1024, dtype=np.int64)[None, :]
    maska = np.where(jp <= uu - 512, np.float32(0.0), np.float32(-1.0e30)).astype(bf16)
    iden = np.eye(128, dtype=np.float32).astype(bf16)
    rotM = np.zeros((128, 128), dtype=np.float32)
    for d in range(64):
        rotM[d + 64, d] = -1.0
        rotM[d, d + 64] = 1.0
    rotM = rotM.astype(bf16)
    onec = np.ones((128, 2), dtype=np.float32).astype(bf16)
    oner = np.ones((1, 128), dtype=np.float32).astype(bf16)
    onerF = np.ones((1, 128), dtype=np.float32)

    wfT = np.ascontiguousarray(wf.T)
    in_maps = []
    for c in range(8):
        b, k = c // 4, c % 4
        xTb = np.ascontiguousarray(np.asarray(x[b]).T)
        m = {
            "xT": xTb,
            "xq": np.ascontiguousarray(xTb[512 * k:512 * (k + 1), :]),
            "wq": np.ascontiguousarray(Wq[:, 512 * k:512 * (k + 1)]),
            "wk": np.ascontiguousarray(Wk[:, 128 * k:128 * (k + 1)]),
            "wv": np.ascontiguousarray(Wv[:, 128 * k:128 * (k + 1)]),
            "wo_q": np.ascontiguousarray(Wo[:, 512 * k:512 * (k + 1)]),
            "wf_q": np.ascontiguousarray(wfT[:, 512 * k:512 * (k + 1)]),
            "bq4": np.ascontiguousarray(bq[512 * k:512 * (k + 1)].reshape(4, 128).T),
            "bk1": np.ascontiguousarray(bk[128 * k:128 * (k + 1)].reshape(1, 128).T),
            "bv1": np.ascontiguousarray(bv[128 * k:128 * (k + 1)].reshape(1, 128).T),
            "fscI": np.ascontiguousarray(fs[512 * k:512 * (k + 1)].reshape(4, 128).T.astype(np.float32)),
            "fbiI": np.ascontiguousarray(fb[512 * k:512 * (k + 1)].reshape(4, 128).T.astype(np.float32)),
            "cosI": cosF, "sinI": sinF, "maskI": maska, "idenI": iden,
            "rotI": rotM, "onecI": onec, "onerI": oner, "onerFI": onerF,
        }
        in_maps.append(m)

    nc = build_nc()
    res = run_bass_kernel_spmd(nc, in_maps, core_ids=list(range(8)))

    out = np.empty((B, T, DM), dtype=np.float32)
    for c in range(8):
        b, k = c // 4, c % 4
        out[b, :, 512 * k:512 * (k + 1)] = res.results[c]["out_T"].T
    return out




# version 22
# speedup vs baseline: 1.8074x; 1.0069x over previous
import contextlib
import numpy as np
import ml_dtypes
import concourse.bass as bass
import concourse.mybir as mybir
from concourse.bass_utils import run_bass_kernel_spmd

F32 = mybir.dt.float32
BF16 = mybir.dt.bfloat16
AF = mybir.ActivationFunctionType
ALU = mybir.AluOpType
bf16 = ml_dtypes.bfloat16

B = 2
T = 8192
DM = 2048
D = 128
NB = T // 512
NC = DM // 128
SCALE = 1.0 / float(np.sqrt(D))
RG = [[0, 1, 2, 3], [4, 5, 6, 7]]

ROWS = [(0, 0), (1, 0)] + [(qb, h) for qb in range(NB) for h in range(4)]
N_SACR = 2
ROW_GROUPS = []
_g = 0
for (qb, _h) in ROWS:
    ROW_GROUPS.append(list(range(_g, _g + 2 * (qb + 1))))
    _g += 2 * (qb + 1)
NGRP = _g
XP_OF = {s: s + 1 for s in range(NGRP)}
EV_EVAC_DONE = {}
EV_TS_DONE = {}
EV_DUMMY_DONE = {}
_ev = 0
for _r in range(len(ROWS)):
    if _r >= 1:
        _ev += 4
        EV_EVAC_DONE[_r - 1] = _ev
    _ev += 4
    EV_TS_DONE[_r] = _ev
    _ev += 1
    EV_DUMMY_DONE[_r] = _ev
_ev += 4
EV_EVAC_DONE[len(ROWS) - 1] = _ev
EV_TOTAL = _ev
LOC_OF = {}
ROW_OF = {}
for _ri in range(len(ROWS)):
    for _j, _s in enumerate(ROW_GROUPS[_ri]):
        LOC_OF[_s] = _j
        ROW_OF[_s] = _ri


def build_nc():
    nc = bass.Bass("TRN2", target_bir_lowering=False)

    xT = nc.dram_tensor("xT", [DM, T], BF16, kind="ExternalInput").ap()
    xq = nc.dram_tensor("xq", [512, T], BF16, kind="ExternalInput").ap()
    wq = nc.dram_tensor("wq", [DM, 512], BF16, kind="ExternalInput").ap()
    wk = nc.dram_tensor("wk", [DM, 128], BF16, kind="ExternalInput").ap()
    wv = nc.dram_tensor("wv", [DM, 128], BF16, kind="ExternalInput").ap()
    wo_q = nc.dram_tensor("wo_q", [DM, 512], BF16, kind="ExternalInput").ap()
    wf_q = nc.dram_tensor("wf_q", [DM, 512], F32, kind="ExternalInput").ap()
    bq4 = nc.dram_tensor("bq4", [128, 4], BF16, kind="ExternalInput").ap()
    bk1 = nc.dram_tensor("bk1", [128, 1], BF16, kind="ExternalInput").ap()
    bv1 = nc.dram_tensor("bv1", [128, 1], BF16, kind="ExternalInput").ap()
    fscI = nc.dram_tensor("fscI", [128, 4], F32, kind="ExternalInput").ap()
    fbiI = nc.dram_tensor("fbiI", [128, 4], F32, kind="ExternalInput").ap()
    cosI = nc.dram_tensor("cosI", [128, T], BF16, kind="ExternalInput").ap()
    sinI = nc.dram_tensor("sinI", [128, T], BF16, kind="ExternalInput").ap()
    maskI = nc.dram_tensor("maskI", [128, 1024], BF16, kind="ExternalInput").ap()
    idenI = nc.dram_tensor("idenI", [128, 128], BF16, kind="ExternalInput").ap()
    rotI = nc.dram_tensor("rotI", [128, 128], BF16, kind="ExternalInput").ap()
    onecI = nc.dram_tensor("onecI", [128, 2], BF16, kind="ExternalInput").ap()
    onerI = nc.dram_tensor("onerI", [1, 128], BF16, kind="ExternalInput").ap()
    onerFI = nc.dram_tensor("onerFI", [1, 128], F32, kind="ExternalInput").ap()

    out_T = nc.dram_tensor("out_T", [512, T], F32, kind="ExternalOutput").ap()

    O_scr = nc.dram_tensor("O_scr", [512, T], BF16, kind="Internal")
    OG = nc.dram_tensor("OG", [DM, T], BF16, kind="Internal")
    xn_scr = nc.dram_tensor("xn_scr", [DM, T], BF16, kind="Internal").ap()

    st = contextlib.ExitStack()
    E = st.enter_context

    iden = E(nc.sbuf_tensor("iden", [128, 128], BF16))
    rotM = E(nc.sbuf_tensor("rotM", [128, 128], BF16))
    maska = E(nc.sbuf_tensor("maska", [128, 1024], BF16))
    onec = E(nc.sbuf_tensor("onec", [128, 2], BF16))
    oner = E(nc.sbuf_tensor("oner", [1, 128], BF16))
    onerF = E(nc.sbuf_tensor("onerF", [1, 128], F32))
    bqb = E(nc.sbuf_tensor("bqb", [128, 4], BF16))
    bkb = E(nc.sbuf_tensor("bkb", [128, 1], BF16))
    bvb = E(nc.sbuf_tensor("bvb", [128, 1], BF16))
    bq_s = E(nc.sbuf_tensor("bq_s", [128, 4], F32))
    bk_s = E(nc.sbuf_tensor("bk_s", [128, 1], F32))
    bv_s = E(nc.sbuf_tensor("bv_s", [128, 1], F32))
    fsc = E(nc.sbuf_tensor("fsc", [128, 4], F32))
    fbi = E(nc.sbuf_tensor("fbi", [128, 4], F32))

    names = ["s_ld", "s_ldt", "s_ldc", "s_sqf", "s_sqmm", "s_stsum",
             "s_rw1", "s_rw2", "s_row", "s_bcmm", "s_bc", "s_nrm", "s_prj", "s_evA",
             "s_rotmm", "s_rp", "s_vtb", "s_vte", "s_sp",
             "s_mm", "s_xp", "s_pv", "s_ev", "s_tr", "s_sto", "s_cc", "s_cc2",
             "s_cw", "s_wsp", "s_cg", "s_cmm", "s_cev", "s_sil", "s_cdone", "s_out"]
    S = {n: E(nc.semaphore(n)) for n in names}

    with st:
        sAB = contextlib.ExitStack()
        q_s = sAB.enter_context(nc.sbuf_tensor("q_s", [128, 4 * T], BF16))
        k_s = sAB.enter_context(nc.sbuf_tensor("k_s", [128, T], BF16))
        v_s = sAB.enter_context(nc.sbuf_tensor("v_s", [128, (T // 128) * 129], BF16))
        with contextlib.ExitStack() as sA:
            EA = sA.enter_context
            wq_s = EA(nc.sbuf_tensor("wq_s", [128, NC * 512], BF16))
            wk_s = EA(nc.sbuf_tensor("wk_s", [128, NC * 128], BF16))
            wv_s = EA(nc.sbuf_tensor("wv_s", [128, NC * 128], BF16))
            xT_s = EA(nc.sbuf_tensor("xT_s", [128, 2 * NC * 512], BF16))
            xsq = EA(nc.sbuf_tensor("xsq", [128, 4 * 512], BF16))
            xn_s = EA(nc.sbuf_tensor("xn_s", [128, NC * 512], BF16))
            cs_s = EA(nc.sbuf_tensor("cs_s", [128, 2 * 512], BF16))
            sn_s = EA(nc.sbuf_tensor("sn_s", [128, 2 * 512], BF16))
            rowm = EA(nc.sbuf_tensor("rowm", [1, 512], F32))
            rowv = EA(nc.sbuf_tensor("rowv", [1, 512], F32))
            rowt = EA(nc.sbuf_tensor("rowt", [1, 512], F32))
            rowr = EA(nc.sbuf_tensor("rowr", [1, 512], F32))
            rowbf = EA(nc.sbuf_tensor("rowbf", [1, 1024], BF16))
            Rb = EA(nc.sbuf_tensor("Rb", [128, 512], F32))
            Mb = EA(nc.sbuf_tensor("Mb", [128, 512], BF16))
            tmpn = EA(nc.sbuf_tensor("tmpn", [128, 512], BF16))
            qraw = EA(nc.sbuf_tensor("qraw", [128, 2 * 512], BF16))
            t1b = EA(nc.sbuf_tensor("t1b", [128, 512], BF16))
            t2b = EA(nc.sbuf_tensor("t2b", [128, 512], BF16))
            vE = EA(nc.sbuf_tensor("vE", [128, 512], BF16))
            ps_sum = EA(nc.psum_tensor("ps_sum", [128, 512], F32))
            ps_sq = EA(nc.psum_tensor("ps_sq", [128, 512], F32))
            ps_bc = EA(nc.psum_tensor("ps_bc", [128, 1024], F32))
            ps_pA = EA(nc.psum_tensor("ps_pA", [128, 512], F32))
            ps_pB = EA(nc.psum_tensor("ps_pB", [128, 512], F32))
            ps_rot = EA(nc.psum_tensor("ps_rot", [128, 512], F32))
            ps_vt = EA(nc.psum_tensor("ps_vt", [128, 4 * 128], BF16))

            with nc.Block("A") as blk:

                @blk.sync
                def _(sync):
                    for dst, src in [
                        (iden[:, :], idenI), (rotM[:, :], rotI), (maska[:, :], maskI),
                        (onec[:, :], onecI), (oner[:, :], onerI), (onerF[:, :], onerFI),
                        (bqb[:, :], bq4), (bkb[:, :], bk1), (bvb[:, :], bv1),
                        (fsc[:, :], fscI), (fbi[:, :], fbiI),
                    ]:
                        sync.dma_start(out=dst, in_=src).then_inc(S["s_ldc"], 16)
                    sync.dma_start(out=wq_s[:, :].rearrange("p (c m) -> p c m", c=NC),
                                   in_=wq.rearrange("(c p) m -> p c m", p=128)).then_inc(S["s_ldc"], 16)
                    sync.dma_start(out=wk_s[:, :].rearrange("p (c m) -> p c m", c=NC),
                                   in_=wk.rearrange("(c p) m -> p c m", p=128)).then_inc(S["s_ldc"], 16)
                    sync.dma_start(out=wv_s[:, :].rearrange("p (c m) -> p c m", c=NC),
                                   in_=wv.rearrange("(c p) m -> p c m", p=128)).then_inc(S["s_ldc"], 16)
                    for tb in range(NB):
                        buf = tb % 2
                        if tb >= 2:
                            sync.wait_ge(S["s_sqf"], 16 * (tb - 1))
                            sync.wait_ge(S["s_stsum"], tb - 1)
                            sync.wait_ge(S["s_nrm"], NC * (tb - 1))
                        src_b = xT[:, tb * 512:(tb + 1) * 512].rearrange("(c p) t -> p c t", p=128)
                        dst_b = xT_s[:, buf * 8192:(buf + 1) * 8192].rearrange("p (c t) -> p c t", c=NC)
                        for pt in range(4):
                            sync.dma_start(out=dst_b[:, pt * 4:(pt + 1) * 4, :],
                                           in_=src_b[:, pt * 4:(pt + 1) * 4, :]).then_inc(S["s_ld"], 16)
                        if tb >= 2:
                            sync.wait_ge(S["s_rp"], 5 * (tb - 1))
                        sync.dma_start(out=cs_s[:, buf * 512:(buf + 1) * 512],
                                       in_=cosI[:, tb * 512:(tb + 1) * 512]).then_inc(S["s_ldt"], 16)
                        sync.dma_start(out=sn_s[:, buf * 512:(buf + 1) * 512],
                                       in_=sinI[:, tb * 512:(tb + 1) * 512]).then_inc(S["s_ldt"], 16)

                @blk.gpsimd
                def _(gpsimd):
                    for tb in range(NB):
                        gpsimd.wait_ge(S["s_nrm"], NC * (tb + 1))
                        dst = xn_scr[:, tb * 512:(tb + 1) * 512].rearrange("(c p) t -> p c t", p=128)
                        src = xn_s[:, :].rearrange("p (c t) -> p c t", c=NC)
                        gpsimd.dma_start(out=dst[:, 0:8, :], in_=src[:, 0:8, :]).then_inc(S["s_sp"], 16)
                        gpsimd.dma_start(out=dst[:, 8:16, :], in_=src[:, 8:16, :]).then_inc(S["s_sp"], 16)

                @blk.scalar
                def _(scalar):
                    scalar.wait_ge(S["s_ldc"], 16 * 14)
                    for tb in range(NB):
                        buf = tb % 2
                        scalar.wait_ge(S["s_rw1"], tb + 1)
                        scalar.activation(rowt[0:1, :], rowv[0:1, :], AF.Ln)
                        scalar.activation(rowt[0:1, :], rowt[0:1, :], AF.Exp, scale=0.5)
                        scalar.drain().then_inc(S["s_rw2"], 1)
                        evs = [("K", qraw[:, 0:512], bk_s[0:128, 0:1]),
                               ("V", vE[:, :], bv_s[0:128, 0:1]),
                               ("Q0", qraw[:, 512:1024], bq_s[0:128, 0:1]),
                               ("Q1", qraw[:, 0:512], bq_s[0:128, 1:2]),
                               ("Q2", qraw[:, 512:1024], bq_s[0:128, 2:3]),
                               ("Q3", qraw[:, 0:512], bq_s[0:128, 3:4])]
                        for gi, (nm, dst, bias_ap) in enumerate(evs):
                            scalar.wait_ge(S["s_prj"], 6 * tb + gi + 1)
                            if nm == "K" and tb >= 1:
                                scalar.wait_ge(S["s_rp"], 5 * tb)
                                scalar.wait_ge(S["s_rotmm"], 5 * tb)
                            if nm == "V" and tb >= 1:
                                scalar.wait_ge(S["s_vtb"], tb)
                            if nm in ("Q1", "Q2", "Q3"):
                                k_ = {"Q1": 1, "Q2": 2, "Q3": 3}[nm]
                                scalar.wait_ge(S["s_rp"], 5 * tb + k_)
                                scalar.wait_ge(S["s_rotmm"], 5 * tb + k_)
                            psum = ps_pA if gi % 2 == 0 else ps_pB
                            scalar.activation(dst, psum[:, :], AF.Identity,
                                              bias=bias_ap).then_inc(S["s_evA"], 1)

                @blk.vector
                def _(vector):
                    vector.wait_ge(S["s_ldc"], 16 * 14)
                    vector.memset(v_s[:, :].rearrange("p (n f) -> p n f", f=129)[:, :, 128:129], 1.0)
                    vector.tensor_copy(bq_s[:, :], bqb[:, :])
                    vector.tensor_copy(bk_s[:, :], bkb[:, :])
                    vector.tensor_copy(bv_s[:, :], bvb[:, :])
                    vector.wait_ge(S["s_ld"], 64)
                    for c in range(NC):
                        if c >= 4:
                            vector.wait_ge(S["s_sqmm"], c - 3)
                        vector.tensor_tensor(xsq[:, (c % 4) * 512:(c % 4 + 1) * 512],
                                             xT_s[:, c * 512:(c + 1) * 512],
                                             xT_s[:, c * 512:(c + 1) * 512],
                                             ALU.mult).then_inc(S["s_sqf"], 1)
                    for tb in range(NB):
                        buf = tb % 2
                        xb = xT_s[:, buf * 8192:(buf + 1) * 8192]
                        csb = cs_s[:, buf * 512:(buf + 1) * 512]
                        snb = sn_s[:, buf * 512:(buf + 1) * 512]
                        vector.wait_ge(S["s_stsum"], tb + 1)
                        vector.wait_ge(S["s_sqmm"], NC * (tb + 1))
                        vector.tensor_scalar(rowm[0:1, :], ps_sum[0:1, :], 1.0 / DM, None, ALU.mult)
                        vector.tensor_copy(rowbf[0:1, 0:512], rowm[0:1, :])
                        vector.tensor_scalar(rowv[0:1, :], ps_sq[0:1, :], 1.0 / DM, None, ALU.mult)
                        vector.tensor_tensor(rowt[0:1, :], rowm[0:1, :], rowm[0:1, :], ALU.mult)
                        vector.tensor_tensor(rowv[0:1, :], rowv[0:1, :], rowt[0:1, :], ALU.subtract)
                        vector.tensor_copy(rowbf[0:1, 512:1024], rowv[0:1, :])
                        vector.tensor_copy(rowv[0:1, :], rowbf[0:1, 512:1024])
                        vector.tensor_scalar(rowv[0:1, :], rowv[0:1, :], 1.0e-5, None, ALU.add)
                        vector.drain().then_inc(S["s_rw1"], 1)
                        vector.wait_ge(S["s_rw2"], tb + 1)
                        vector.tensor_copy(rowbf[0:1, 512:1024], rowt[0:1, :])
                        vector.reciprocal(rowr[0:1, :], rowbf[0:1, 512:1024])
                        vector.drain().then_inc(S["s_row"], 1)
                        vector.wait_ge(S["s_bcmm"], tb + 1)
                        vector.tensor_copy(Rb[:, :], ps_bc[:, 0:512])
                        vector.tensor_copy(Mb[:, :], ps_bc[:, 512:1024])
                        vector.drain().then_inc(S["s_bc"], 1)
                        if tb >= 1:
                            vector.wait_ge(S["s_prj"], 6 * tb)
                            vector.wait_ge(S["s_sp"], 32 * tb)
                        for c in range(NC):
                            vector.tensor_tensor(tmpn[:, :], xb[:, c * 512:(c + 1) * 512], Mb[:, :], ALU.subtract)
                            vector.tensor_tensor(xn_s[:, c * 512:(c + 1) * 512], tmpn[:, :], Rb[:, :],
                                                 ALU.mult).then_inc(S["s_nrm"], 1)
                        if tb + 1 < NB:
                            nbuf = (tb + 1) % 2
                            xb2 = xT_s[:, nbuf * 8192:(nbuf + 1) * 8192]
                            vector.wait_ge(S["s_ld"], 64 * (tb + 2))
                            for c in range(NC):
                                gi = (tb + 1) * NC + c
                                if gi >= 4:
                                    vector.wait_ge(S["s_sqmm"], gi - 3)
                                vector.tensor_tensor(xsq[:, (gi % 4) * 512:(gi % 4 + 1) * 512],
                                                     xb2[:, c * 512:(c + 1) * 512],
                                                     xb2[:, c * 512:(c + 1) * 512],
                                                     ALU.mult).then_inc(S["s_sqf"], 1)
                        vector.wait_ge(S["s_ldt"], 32 * (tb + 1))
                        ropes = [(qraw[:, 0:512], k_s[:, tb * 512:(tb + 1) * 512])] + [
                            (qraw[:, (512 if h % 2 == 0 else 0):(1024 if h % 2 == 0 else 512)],
                             q_s[:, h * T + tb * 512: h * T + (tb + 1) * 512]) for h in range(4)]
                        for ri_, (srcq, dsta) in enumerate(ropes):
                            vector.wait_ge(S["s_rotmm"], 5 * tb + ri_ + 1)
                            vector.tensor_tensor(t1b[:, :], srcq, csb, ALU.mult)
                            vector.tensor_tensor(t2b[:, :], ps_rot[:, :], snb, ALU.mult)
                            vector.tensor_tensor(dsta, t1b[:, :], t2b[:, :], ALU.add)
                            vector.drain().then_inc(S["s_rp"], 1)
                        vector.wait_ge(S["s_vtb"], tb + 1)
                        for sub in range(4):
                            tile = tb * 4 + sub
                            vector.tensor_copy(v_s[:, tile * 129: tile * 129 + 128],
                                               ps_vt[:, sub * 128:(sub + 1) * 128])
                        vector.drain().then_inc(S["s_vte"], 1)

                @blk.tensor
                def _(tensor):
                    tensor.wait_ge(S["s_ldc"], 16 * 14)

                    def emit_stats(tb2):
                        buf2 = tb2 % 2
                        xb2 = xT_s[:, buf2 * 8192:(buf2 + 1) * 8192]
                        tensor.wait_ge(S["s_ld"], 64 * (tb2 + 1))
                        if tb2 >= 1:
                            tensor.wait_ge(S["s_rw1"], tb2)
                        mm2 = None
                        for c in range(NC):
                            mm2 = tensor.matmul(ps_sum[0:1, :], onec[:, 0:1],
                                                xb2[:, c * 512:(c + 1) * 512],
                                                start=(c == 0), stop=(c == NC - 1))
                        mm2.then_inc(S["s_stsum"], 1)
                        for c in range(NC):
                            gi = tb2 * NC + c
                            tensor.wait_ge(S["s_sqf"], gi + 1)
                            mm2 = tensor.matmul(ps_sq[0:1, :], onec[:, 0:1],
                                                xsq[:, (gi % 4) * 512:(gi % 4 + 1) * 512],
                                                start=(c == 0), stop=(c == NC - 1))
                            mm2.then_inc(S["s_sqmm"], 1)

                    emit_stats(0)
                    for tb in range(NB):
                        buf = tb % 2
                        tensor.wait_ge(S["s_row"], tb + 1)
                        if tb >= 1:
                            tensor.wait_ge(S["s_bc"], tb)
                        tensor.matmul(ps_bc[:, 0:512], onerF[0:1, :], rowr[0:1, :], start=True, stop=True)
                        mm = tensor.matmul(ps_bc[:, 512:1024], oner[0:1, :], rowbf[0:1, 0:512], start=True, stop=True)
                        mm.then_inc(S["s_bcmm"], 1)
                        projs = [("K", wk_s, 0), ("V", wv_s, 0), ("Q0", wq_s, 0),
                                 ("Q1", wq_s, 1), ("Q2", wq_s, 2), ("Q3", wq_s, 3)]
                        for gi, (nm, wsb, mi) in enumerate(projs):
                            psum = ps_pA if gi % 2 == 0 else ps_pB
                            need = 6 * tb + gi - 1
                            if need >= 1:
                                tensor.wait_ge(S["s_evA"], need)
                            for c in range(NC):
                                tensor.wait_ge(S["s_nrm"], NC * tb + c + 1)
                                if nm in ("K", "V"):
                                    lhs = wsb[:, c * 128:(c + 1) * 128]
                                else:
                                    lhs = wsb[:, c * 512 + mi * 128: c * 512 + (mi + 1) * 128]
                                mm = tensor.matmul(psum[:, :], lhs, xn_s[:, c * 512:(c + 1) * 512],
                                                   start=(c == 0), stop=(c == NC - 1))
                            mm.then_inc(S["s_prj"], 1)
                            if nm == "V" and tb + 1 < NB:
                                emit_stats(tb + 1)
                            if nm == "Q0":
                                tensor.wait_ge(S["s_evA"], 6 * tb + 3)
                                if tb >= 1:
                                    tensor.wait_ge(S["s_vte"], tb)
                                for sub in range(4):
                                    mm2 = tensor.transpose(ps_vt[:, sub * 128:(sub + 1) * 128],
                                                           vE[:, sub * 128:(sub + 1) * 128],
                                                           iden[:, :])
                                mm2.then_inc(S["s_vtb"], 1)
                            if nm != "V":
                                rot_i = {"K": 0, "Q0": 1, "Q1": 2, "Q2": 3, "Q3": 4}[nm]
                                tensor.wait_ge(S["s_evA"], 6 * tb + gi + 1)
                                if 5 * tb + rot_i >= 1:
                                    tensor.wait_ge(S["s_rp"], 5 * tb + rot_i)
                                slot = qraw[:, 0:512] if nm in ("K", "Q1", "Q3") else qraw[:, 512:1024]
                                mm3 = tensor.matmul(ps_rot[:, :], rotM[:, :], slot, start=True, stop=True)
                                mm3.then_inc(S["s_rotmm"], 1)

        wos = st.enter_context(nc.sbuf_tensor("wos", [128, NC * 512], BF16, side="right"))
        wraw = st.enter_context(nc.sbuf_tensor("wraw", [128, NC * 512], F32, side="right"))
        whi = st.enter_context(nc.sbuf_tensor("whi", [128, NC * 512], BF16, side="right"))
        wlo = st.enter_context(nc.sbuf_tensor("wlo", [128, NC * 512], BF16, side="right"))
        with contextlib.ExitStack() as sB:
            EB = sB.enter_context
            pT = EB(nc.sbuf_tensor("pT", [128, 4 * 1024], BF16))
            on_s = EB(nc.sbuf_tensor("on_s", [128, 4 * 128], BF16))
            rden = EB(nc.sbuf_tensor("rden", [128, 8], F32))
            oTb = EB(nc.sbuf_tensor("oTb", [128, 2 * 512], BF16))
            xdum = EB(nc.sbuf_tensor("xdum", [1, 2], BF16))
            sS0 = EB(nc.psum_tensor("sS0", [128, 1024], F32))
            sS1 = EB(nc.psum_tensor("sS1", [128, 1024], F32))
            ob01 = EB(nc.psum_tensor("ob01", [128, 260], F32))
            ob23 = EB(nc.psum_tensor("ob23", [128, 260], F32))
            ptr = EB(nc.psum_tensor("ptr", [128, 512], BF16))

            def ob_seg(sub):
                tt = ob01 if sub < 2 else ob23
                return tt[:, (sub % 2) * 129:(sub % 2 + 1) * 129]

            with nc.Block("Bb") as blk:

                @blk.tensor
                def _(tensor):
                    tensor.wait_ge(S["s_rp"], 5 * NB)
                    tensor.wait_ge(S["s_vte"], NB)

                    def emit_dummies(ri):
                        if ri >= 1:
                            tensor.wait_ge(S["s_ev"], EV_TS_DONE[ri - 1])
                        tensor.matmul(ob01[0:128, 258:260], iden[:, :], onec[:, 0:2],
                                      start=True, stop=True, skip_group_check=True)
                        tensor.matmul(ob23[0:128, 258:260], iden[:, :], onec[:, 0:2],
                                      start=True, stop=True, skip_group_check=True)

                    def emit_qk(s_id):
                        ri = ROW_OF[s_id]
                        qb, h = ROWS[ri]
                        j = LOC_OF[s_id]
                        q_ap = q_s[:, h * T + qb * 512: h * T + (qb + 1) * 512]
                        ps = sS0 if s_id % 2 == 0 else sS1
                        if s_id >= 2:
                            tensor.wait_ge(S["s_xp"], XP_OF[s_id - 2])
                        mm = None
                        for ti in range(2):
                            kt = 2 * j + ti
                            is_diag = (kt >= 4 * qb)
                            mm = tensor.matmul(ps[:, ti * 512:(ti + 1) * 512],
                                               k_s[:, kt * 128:(kt + 1) * 128], q_ap,
                                               start=True, stop=not is_diag)
                            if is_diag:
                                off = 512 - 128 * (kt - 4 * qb)
                                mm = tensor.matmul(ps[:, ti * 512:(ti + 1) * 512],
                                                   iden[:, :], maska[:, off:off + 512],
                                                   start=False, stop=True)
                        mm.then_inc(S["s_mm"], 1)

                    def emit_pv(s_id):
                        ri = ROW_OF[s_id]
                        if LOC_OF[s_id] == 0:
                            emit_dummies(ri)
                        tensor.wait_ge(S["s_xp"], XP_OF[s_id] + 2)
                        slot = s_id % 4
                        j = LOC_OF[s_id]
                        last = (s_id == ROW_GROUPS[ri][-1])
                        mm = None
                        for ti in range(2):
                            kt = 2 * j + ti
                            for sub in range(4):
                                mm = tensor.matmul(
                                    ob_seg(sub),
                                    pT[:, slot * 1024 + ti * 512 + sub * 128:
                                       slot * 1024 + ti * 512 + (sub + 1) * 128],
                                    v_s[:, kt * 129: kt * 129 + 129],
                                    start=False, stop=(last and ti == 1),
                                    skip_group_check=True)
                        mm.then_inc(S["s_pv"], 1)
                        if last:
                            tensor.wait_ge(S["s_ev"], EV_DUMMY_DONE[ri])
                            for si in range(4):
                                mmt = tensor.transpose(ptr[:, si * 128:(si + 1) * 128],
                                                       on_s[:, si * 128:(si + 1) * 128], iden[:, :])
                            mmt.then_inc(S["s_tr"], 1)

                    for s_id in range(NGRP):
                        emit_qk(s_id)
                        if s_id >= 3:
                            emit_pv(s_id - 3)
                    emit_pv(NGRP - 3)
                    emit_pv(NGRP - 2)
                    emit_pv(NGRP - 1)

                @blk.scalar
                def _(scalar):
                    for s_id in range(NGRP):
                        scalar.wait_ge(S["s_mm"], s_id + 1)
                        if s_id >= 4:
                            scalar.wait_ge(S["s_pv"], s_id - 3)
                        ps = sS0 if s_id % 2 == 0 else sS1
                        slot = s_id % 4
                        scalar.activation(pT[:, slot * 1024:(slot + 1) * 1024],
                                          ps[:, :], AF.Exp, scale=SCALE).then_inc(S["s_xp"], 1)
                    scalar.activation(xdum[0:1, :], xdum[0:1, :], AF.Exp).then_inc(S["s_xp"], 1)
                    scalar.activation(xdum[0:1, :], xdum[0:1, :], AF.Exp).then_inc(S["s_xp"], 1)

                @blk.vector
                def _(vector):
                    vector.memset(xdum[0:1, :], 0.0)

                    def evacs(r):
                        par = r % 2
                        vector.wait_ge(S["s_tr"], r + 1)
                        if r >= 2:
                            vector.wait_ge(S["s_sto"], 16 * (r - 1))
                        for si in range(4):
                            vector.tensor_copy(oTb[:, par * 512 + si * 128: par * 512 + (si + 1) * 128],
                                               ptr[:, si * 128:(si + 1) * 128]).then_inc(S["s_ev"], 1)

                    for ri in range(len(ROWS)):
                        vector.wait_ge(S["s_pv"], ROW_GROUPS[ri][-1] + 1)
                        if ri >= 1:
                            evacs(ri - 1)
                        for sub in range(4):
                            vector.reciprocal(rden[:, sub:sub + 1], ob_seg(sub)[:, 128:129])
                        for sub in range(4):
                            vector.tensor_scalar(on_s[:, sub * 128:(sub + 1) * 128], ob_seg(sub)[:, 0:128],
                                                 rden[:, sub:sub + 1], None,
                                                 ALU.mult).then_inc(S["s_ev"], 1)
                        vector.memset(rden[:, 4:5], 0.0)
                        vector.drain().then_inc(S["s_ev"], 1)
                    evacs(len(ROWS) - 1)

                @blk.gpsimd
                def _(gpsimd):
                    for ri, (qb, h) in enumerate(ROWS):
                        par = ri % 2
                        gpsimd.wait_ge(S["s_ev"], EV_EVAC_DONE[ri])
                        gpsimd.dma_start(out=O_scr.ap()[h * 128:(h + 1) * 128,
                                                        qb * 512:(qb + 1) * 512],
                                         in_=oTb[:, par * 512:(par + 1) * 512]).then_inc(S["s_sto"], 16)
                    gpsimd.wait_ge(S["s_sto"], 16 * len(ROWS))
                    gpsimd.collective_compute(
                        "AllGather", ALU.bypass, replica_groups=RG,
                        ins=[O_scr.ap().opt()], outs=[OG.ap().opt()]).then_inc(S["s_cc"], 1)

        sAB.close()
        with contextlib.ExitStack() as sC:
            EC = sC.enter_context
            ogs = EC(nc.sbuf_tensor("ogs", [128, 2 * NC * 512], BF16))
            xns = EC(nc.sbuf_tensor("xns", [128, 2 * NC * 512], BF16))
            xqs = EC(nc.sbuf_tensor("xqs", [128, 2 * 4 * 512], BF16))
            stile = EC(nc.sbuf_tensor("stile", [128, 512], BF16))
            gpre = EC(nc.sbuf_tensor("gpre", [128, 512], F32))
            gact = EC(nc.sbuf_tensor("gact", [128, 512], F32))
            sgt = EC(nc.sbuf_tensor("sgt", [128, 512], F32))
            outb = EC(nc.sbuf_tensor("outb", [128, 2 * 4 * 512], F32))
            psS0 = EC(nc.psum_tensor("psS0", [128, 512], F32))
            psS1 = EC(nc.psum_tensor("psS1", [128, 512], F32))
            psG0 = EC(nc.psum_tensor("psG0", [128, 512], F32))
            psG1 = EC(nc.psum_tensor("psG1", [128, 512], F32))

            with nc.Block("Cc") as blk:

                @blk.sync
                def _(sync):
                    for half in range(2):
                        sync.dma_start(
                            out=wos[:, :].rearrange("p (c m) -> p c m", c=NC)[:, half * 8:(half + 1) * 8, :],
                            in_=wo_q.rearrange("(c p) m -> p c m", p=128)[:, half * 8:(half + 1) * 8, :],
                        ).then_inc(S["s_cw"], 16)
                        sync.dma_start(
                            out=wraw[:, :].rearrange("p (c m) -> p c m", c=NC)[:, half * 8:(half + 1) * 8, :],
                            in_=wf_q.rearrange("(c p) m -> p c m", p=128)[:, half * 8:(half + 1) * 8, :],
                        ).then_inc(S["s_cw"], 16)
                    sync.wait_ge(S["s_cc"], 1)
                    for tb in range(NB):
                        par = tb % 2
                        if tb >= 2:
                            sync.wait_ge(S["s_cmm"], 8 * (tb - 1))
                            sync.wait_ge(S["s_cdone"], 4 * (tb - 1))
                        ogd = ogs[:, par * 8192:(par + 1) * 8192].rearrange("p (c t) -> p c t", c=NC)
                        ogsrc = OG.ap()[:, tb * 512:(tb + 1) * 512].rearrange("(c p) t -> p c t", p=128)
                        xnd = xns[:, par * 8192:(par + 1) * 8192].rearrange("p (c t) -> p c t", c=NC)
                        xnsrc = xn_scr[:, tb * 512:(tb + 1) * 512].rearrange("(c p) t -> p c t", p=128)
                        for half in range(2):
                            sync.dma_start(out=ogd[:, half * 8:(half + 1) * 8, :],
                                           in_=ogsrc[:, half * 8:(half + 1) * 8, :]).then_inc(S["s_cg"], 16)
                            sync.dma_start(out=xnd[:, half * 8:(half + 1) * 8, :],
                                           in_=xnsrc[:, half * 8:(half + 1) * 8, :]).then_inc(S["s_cg"], 16)
                        sync.dma_start(
                            out=xqs[:, par * 2048:(par + 1) * 2048].rearrange("p (c t) -> p c t", c=4),
                            in_=xq[:, tb * 512:(tb + 1) * 512].rearrange("(c p) t -> p c t", p=128),
                        ).then_inc(S["s_cg"], 16)

                @blk.vector
                def _(vector):
                    vector.wait_ge(S["s_cw"], 64)
                    for c in range(NC):
                        vector.tensor_copy(whi[:, c * 512:(c + 1) * 512], wraw[:, c * 512:(c + 1) * 512])
                        vector.tensor_tensor(wlo[:, c * 512:(c + 1) * 512],
                                             wraw[:, c * 512:(c + 1) * 512],
                                             whi[:, c * 512:(c + 1) * 512],
                                             ALU.subtract).then_inc(S["s_wsp"], 1)
                    for tb in range(NB):
                        par = tb % 2
                        if tb >= 2:
                            vector.wait_ge(S["s_out"], 16 * (tb - 1))
                        for m in range(4):
                            gm = 4 * tb + m
                            psS = psS0 if gm % 2 == 0 else psS1
                            psG = psG0 if gm % 2 == 0 else psG1
                            vector.wait_ge(S["s_cmm"], 8 * tb + 2 * m + 1)
                            vector.tensor_copy(stile[:, :], psS[:, :])
                            vector.wait_ge(S["s_cmm"], 8 * tb + 2 * m + 2)
                            if gm >= 1:
                                vector.wait_ge(S["s_sil"], gm)
                            vector.tensor_scalar(gpre[:, :], psG[:, :], fsc[:, m:m + 1],
                                                 fbi[:, m:m + 1], ALU.mult,
                                                 ALU.add).then_inc(S["s_cev"], 1)
                            vector.wait_ge(S["s_sil"], gm + 1)
                            vector.tensor_tensor(sgt[:, :], stile[:, :], gact[:, :], ALU.mult)
                            vector.tensor_tensor(outb[:, par * 2048 + m * 512: par * 2048 + (m + 1) * 512],
                                                 sgt[:, :],
                                                 xqs[:, par * 2048 + m * 512: par * 2048 + (m + 1) * 512],
                                                 ALU.add).then_inc(S["s_cdone"], 1)

                @blk.scalar
                def _(scalar):
                    for gm in range(4 * NB):
                        scalar.wait_ge(S["s_cev"], gm + 1)
                        if gm >= 1:
                            scalar.wait_ge(S["s_cdone"], gm)
                        scalar.activation(gact[:, :], gpre[:, :], AF.Silu).then_inc(S["s_sil"], 1)

                @blk.tensor
                def _(tensor):
                    tensor.wait_ge(S["s_wsp"], NC)
                    for tb in range(NB):
                        par = tb % 2
                        tensor.wait_ge(S["s_cg"], 80 * (tb + 1))
                        ogb = ogs[:, par * 8192:(par + 1) * 8192]
                        xnb = xns[:, par * 8192:(par + 1) * 8192]
                        for m in range(4):
                            gm = 4 * tb + m
                            psS = psS0 if gm % 2 == 0 else psS1
                            psG = psG0 if gm % 2 == 0 else psG1
                            if gm >= 2:
                                tensor.wait_ge(S["s_cev"], gm - 1)
                            mm = None
                            for c in range(NC):
                                mm = tensor.matmul(psS[:, :],
                                                   wos[:, c * 512 + m * 128: c * 512 + (m + 1) * 128],
                                                   ogb[:, c * 512:(c + 1) * 512],
                                                   start=(c == 0), stop=(c == NC - 1))
                            mm.then_inc(S["s_cmm"], 1)
                            for pi, wsb in enumerate([whi, wlo]):
                                for c in range(NC):
                                    mm = tensor.matmul(psG[:, :],
                                                       wsb[:, c * 512 + m * 128: c * 512 + (m + 1) * 128],
                                                       xnb[:, c * 512:(c + 1) * 512],
                                                       start=(pi == 0 and c == 0),
                                                       stop=(pi == 1 and c == NC - 1))
                            mm.then_inc(S["s_cmm"], 1)

                @blk.gpsimd
                def _(gpsimd):
                    for tb in range(NB):
                        par = tb % 2
                        gpsimd.wait_ge(S["s_cdone"], 4 * (tb + 1))
                        gpsimd.dma_start(
                            out=out_T[:, tb * 512:(tb + 1) * 512].rearrange("(c p) t -> p c t", p=128),
                            in_=outb[:, par * 2048:(par + 1) * 2048].rearrange("p (c t) -> p c t", c=4),
                        ).then_inc(S["s_out"], 16)
                    gpsimd.wait_ge(S["s_out"], 16 * NB)

    return nc


def kernel(**inputs):
    x = inputs["L0_attn_x"]
    Wq = inputs["L0_attn_Wq"]
    bq = inputs["L0_attn_bq"]
    Wk = inputs["L0_attn_Wk"]
    bk = inputs["L0_attn_bk"]
    Wv = inputs["L0_attn_Wv"]
    bv = inputs["L0_attn_bv"]
    Wo = inputs["L0_attn_Wo"]
    wf = inputs["L0_ffn_weight"]
    fs = inputs["L0_ffn_scale_weight"]
    fb = inputs["L0_ffn_bias"]

    tt = np.arange(64, dtype=np.float32) / np.float32(64)
    inv = np.float32(1.0) / (np.float32(1.0e6) ** tt)
    ang = np.outer(np.arange(T, dtype=np.float32), inv)
    cosT = np.cos(ang).astype(np.float32).astype(bf16)
    sinT = np.sin(ang).astype(np.float32).astype(bf16)
    cosF = np.ascontiguousarray(np.concatenate([cosT.T, cosT.T], axis=0))
    sinF = np.ascontiguousarray(np.concatenate([sinT.T, sinT.T], axis=0))
    jp = np.arange(128, dtype=np.int64)[:, None]
    uu = np.arange(1024, dtype=np.int64)[None, :]
    maska = np.where(jp <= uu - 512, np.float32(0.0), np.float32(-1.0e30)).astype(bf16)
    iden = np.eye(128, dtype=np.float32).astype(bf16)
    rotM = np.zeros((128, 128), dtype=np.float32)
    for d in range(64):
        rotM[d + 64, d] = -1.0
        rotM[d, d + 64] = 1.0
    rotM = rotM.astype(bf16)
    onec = np.ones((128, 2), dtype=np.float32).astype(bf16)
    oner = np.ones((1, 128), dtype=np.float32).astype(bf16)
    onerF = np.ones((1, 128), dtype=np.float32)

    wfT = np.ascontiguousarray(wf.T)
    in_maps = []
    for c in range(8):
        b, k = c // 4, c % 4
        xTb = np.ascontiguousarray(np.asarray(x[b]).T)
        m = {
            "xT": xTb,
            "xq": np.ascontiguousarray(xTb[512 * k:512 * (k + 1), :]),
            "wq": np.ascontiguousarray(Wq[:, 512 * k:512 * (k + 1)]),
            "wk": np.ascontiguousarray(Wk[:, 128 * k:128 * (k + 1)]),
            "wv": np.ascontiguousarray(Wv[:, 128 * k:128 * (k + 1)]),
            "wo_q": np.ascontiguousarray(Wo[:, 512 * k:512 * (k + 1)]),
            "wf_q": np.ascontiguousarray(wfT[:, 512 * k:512 * (k + 1)]),
            "bq4": np.ascontiguousarray(bq[512 * k:512 * (k + 1)].reshape(4, 128).T),
            "bk1": np.ascontiguousarray(bk[128 * k:128 * (k + 1)].reshape(1, 128).T),
            "bv1": np.ascontiguousarray(bv[128 * k:128 * (k + 1)].reshape(1, 128).T),
            "fscI": np.ascontiguousarray(fs[512 * k:512 * (k + 1)].reshape(4, 128).T.astype(np.float32)),
            "fbiI": np.ascontiguousarray(fb[512 * k:512 * (k + 1)].reshape(4, 128).T.astype(np.float32)),
            "cosI": cosF, "sinI": sinF, "maskI": maska, "idenI": iden,
            "rotI": rotM, "onecI": onec, "onerI": oner, "onerFI": onerF,
        }
        in_maps.append(m)

    nc = build_nc()
    res = run_bass_kernel_spmd(nc, in_maps, core_ids=list(range(8)))

    out = np.empty((B, T, DM), dtype=np.float32)
    for c in range(8):
        b, k = c // 4, c % 4
        out[b, :, 512 * k:512 * (k + 1)] = res.results[c]["out_T"].T
    return out


